# version 03
# baseline (speedup 1.0000x reference)
import numpy as np
import ml_dtypes
import concourse.bass as bass
import concourse.mybir as mybir
from concourse.bass_utils import run_bass_kernel_spmd

F32 = mybir.dt.float32
BF16 = mybir.dt.bfloat16
I32 = mybir.dt.int32
AF = mybir.ActivationFunctionType
ALU = mybir.AluOpType
AX = mybir.AxisListType
BF = ml_dtypes.bfloat16

B, T, D = 4, 8192, 1024
TC = T // 2
H, DK = 16, 64
W = 512
NT = TC // W
DCH = D // 128
REPL = [[0, 1], [2, 3], [4, 5], [6, 7]]

UPTO = "p4"
NOFLUX = False

SEM_ENG = {"s_mm": "tensor", "s_vec": "vector", "s_act": "scalar",
           "s_gpc": "gpsimd"}
SEM_NAMES = ["s_mm", "s_vec", "s_act", "s_gpc", "s_ld", "s_wl", "s_st", "s_cc"]

SELF = 0
SELS = 1024
HSUM16 = 2048
HSEL2 = 2176
ONES1B = 2304
ONES128 = 2432
SELC_W = 2560


class Res:
    __slots__ = ("w", "rs", "name")

    def __init__(self, name=""):
        self.name = name
        self.w = None
        self.rs = []


class Prog:
    ENGINES = ("tensor", "vector", "scalar", "gpsimd", "sync")

    def __init__(self):
        self.streams = {e: [] for e in self.ENGINES}
        self.cnt = {s: 0 for s in SEM_NAMES}

    DEBUG_TAGS = set()

    JUNK = [None]

    def step(self, engine, build, reads=(), writes=(), sem=None, k=1,
             drain=False, settle=False, tag=None):
        waits = {}
        for r in reads:
            if r.w:
                s, v = r.w
                waits[s] = max(waits.get(s, 0), v)
        for wr in writes:
            if wr.w:
                s, v = wr.w
                waits[s] = max(waits.get(s, 0), v)
            for s, v in wr.rs:
                waits[s] = max(waits.get(s, 0), v)
        wl = [(s, v) for s, v in waits.items()
              if SEM_ENG.get(s) != engine and v > 0]
        if tag is not None and tag in Prog.DEBUG_TAGS:
            print(f"[STEP {tag}] eng={engine} waits={wl} sem={sem}->"
                  f"{self.cnt.get(sem, 0) + k if sem else None}")
        tgt = None
        if sem is not None:
            self.cnt[sem] += k
            tgt = self.cnt[sem]

        def fn(eng, sems, wl=wl, build=build, sem=sem, k=k, drain=drain,
               settle=settle):
            for s, v in wl:
                eng.wait_ge(sems[s], v)
            instr = build(eng)
            if drain:
                eng.drain()
            if settle:
                instr = eng.memset(Prog.JUNK[0][0:1, 0:2], 0.0)
            if sem is not None:
                instr.then_inc(sems[sem], k)
        self.streams[engine].append(fn)
        if sem is not None:
            for wr in writes:
                wr.w = (sem, tgt)
                wr.rs = []
            for r in reads:
                r.rs.append((sem, tgt))
        return tgt

    def step2(self, engine, build, reads=(), writes=(), sem=None, k=1):
        waits = {}
        for r in reads:
            if r.w:
                sv, v = r.w
                waits[sv] = max(waits.get(sv, 0), v)
        for wr in writes:
            if wr.w:
                sv, v = wr.w
                waits[sv] = max(waits.get(sv, 0), v)
            for sv, v in wr.rs:
                waits[sv] = max(waits.get(sv, 0), v)
        wl = [(sv, v) for sv, v in waits.items()
              if SEM_ENG.get(sv) != engine and v > 0]
        tgt = None
        if sem is not None:
            self.cnt[sem] += k
            tgt = self.cnt[sem]

        def fn(eng, sems, wl=wl, build=build, tgt=tgt):
            for sv, v in wl:
                eng.wait_ge(sems[sv], v)
            build(eng, sems, tgt)
        self.streams[engine].append(fn)
        if sem is not None:
            for wr in writes:
                wr.w = (sem, tgt)
                wr.rs = []
            for r in reads:
                r.rs.append((sem, tgt))
        return tgt

    def barrier(self):
        snap = [(s, v) for s, v in self.cnt.items() if v > 0]
        for e in self.ENGINES:
            def fn(eng, sems, snap=snap, e=e):
                for s, v in snap:
                    if SEM_ENG.get(s) != e:
                        eng.wait_ge(sems[s], v)
            self.streams[e].append(fn)

    def raw(self, engine, fn):
        self.streams[engine].append(fn)


def ts(i, n):
    return slice(i * n, (i + 1) * n)


def build_program(upto=None):
    upto = upto or UPTO
    nc = bass.Bass("TRN2", target_bir_lowering=False)
    p = Prog()

    def din(name, shape, dt=F32):
        return nc.dram_tensor(name, shape, dt, kind="ExternalInput").ap()

    x_in = din("x", [TC, D])
    out = nc.dram_tensor("out", [TC, D], F32, kind="ExternalOutput").ap()
    cst_f = din("cst_f", [128, 256])
    cst_b = din("cst_b", [128, 128 + SELC_W])
    wq_h = din("wq", [D, D]); wk_h = din("wk", [D, D])
    wv_h = din("wv", [D, D]); wo_h = din("wo", [D, D])
    wsm_h = din("wsm", [D, 64])
    wtf1_h = din("wtf1", [DK, 32]); wtf2_h = din("wtf2", [32, 1])
    bsm_h = din("bsm", [64, 1]); btf1_h = din("btf1", [64, 1])
    btf2_h = din("btf2", [1, 1])
    wg_h = din("wg", [D, 8])
    w1m_h = din("w1m", [6, 16, 128, 1024])
    w2m_h = din("w2m", [6, 8, 128, 2048])
    b1p_h = din("b1p", [128, 96])
    b2p_h = din("b2p", [128, 48])
    bg_h = din("bg", [8, 1])
    sel6_h = din("sel6", [128, 768])
    I32 = mybir.dt.int32
    nsw_h = din("nsw", [128, 8])
    inw_h = din("inw", [32, 128, 1024])
    inb_h = din("inb", [128, 32])
    cwp_h = din("cwp", [128, 64])
    cbp_h = din("cbp", [128, 16])
    xpw_h = din("xpw", [16, 128, 96])
    dtw_h = din("dtw", [64, 1024])
    dtb_h = din("dtb", [128, 8])
    alg_h = din("alg", [128, 128])
    dvc_h = din("dvc", [128, 16])
    otw_h = din("otw", [16, 128, 1024])
    otb_h = din("otb", [128, 8])
    sel32_h = din("sel32", [32, 4096])
    wgt_h = din("wgt", [24, 128, 1024], I32)
    wut_h = din("wut", [24, 128, 1024], I32)
    wdt_h = din("wdt", [8, 128, 3072], I32)

    dbg = nc.dram_tensor("dbg", [128, 8704], F32, kind="ExternalOutput").ap()
    alog_d = nc.dram_tensor("alog_d", [NT, 2, TC], BF16)
    cc_in = nc.dram_tensor("cc_in", [128, 2056], F32)
    cc_out = nc.dram_tensor("cc_out", [256, 2056], F32)
    w1c = nc.dram_tensor("w1c", [6, 128, 16384], BF16)
    w2c = nc.dram_tensor("w2c", [6, 8, 128, 2048], BF16)
    inwc = nc.dram_tensor("inwc", [32, 128, 1024], BF16)
    otwc = nc.dram_tensor("otwc", [16, 128, 1024], BF16)
    wgtc = nc.dram_tensor("wgtc", [24, 128, 1024], BF16)
    wutc = nc.dram_tensor("wutc", [24, 128, 1024], BF16)
    wdtc = nc.dram_tensor("wdtc", [8, 128, 3072], BF16)
    hid_d = nc.dram_tensor("hid_d", [128, 65536], BF16)
    g_d = nc.dram_tensor("g_d", [128, 65536], BF16)
    h_d = nc.dram_tensor("h_d", [128, 65536], BF16)
    dlt_d = nc.dram_tensor("dlt_d", [128, 65536], BF16)
    y_d = nc.dram_tensor("y_d", [128, 32768], BF16)
    xres_d = nc.dram_tensor("xres_d", [128, 32768], F32)
    halo_i = nc.dram_tensor("halo_i", [128, 48], BF16)
    halo_o = nc.dram_tensor("halo_o", [256, 48], BF16)
    ccx_i = nc.dram_tensor("ccx_i", [128, 4096], BF16)
    ccx_o = nc.dram_tensor("ccx_o", [256, 4096], BF16)
    cch_i = nc.dram_tensor("cch_i", [128, 32768], BF16)
    cch_o = nc.dram_tensor("cch_o", [256, 32768], BF16)
    ccy_i = nc.dram_tensor("ccy_i", [128, 32768], BF16)
    ccy_o = nc.dram_tensor("ccy_o", [256, 32768], BF16)

    import contextlib
    with contextlib.ExitStack() as stack:
        E = stack.enter_context
        x_res = E(nc.sbuf_tensor("x_res", [128, DCH * TC], F32))
        warena = E(nc.sbuf_tensor("warena", [128, 16384], BF16))
        id_f = E(nc.sbuf_tensor("id_f", [128, 128], F32))
        id_b = E(nc.sbuf_tensor("id_b", [128, 128], BF16))
        selc = E(nc.sbuf_tensor("selc", [128, SELC_W], BF16))
        ones1f = E(nc.sbuf_tensor("ones1f", [1, 128], F32))
        cval = E(nc.sbuf_tensor("cval", [128, 8], F32))
        junk = E(nc.sbuf_tensor("junk", [1, 8], F32))
        ps = E(nc.psum_tensor("ps", [128, 4096], F32))
        sems = {s: E(nc.semaphore(s)) for s in SEM_NAMES}
        block = E(nc.Block())

        Prog.JUNK[0] = junk
        R_x = Res("x_res")
        R_bank = [Res(f"bank{i}") for i in range(8)]
        R_wa = [Res("wa0"), Res("wa1")]
        R_c = Res("const")
        R_mhat = Res("mhat")
        R_ccin = Res(); R_ccout = Res(); R_alog = Res()

        def bank(i, cols=512, off=0):
            return ps[:, i * 512 + off: i * 512 + off + cols]

        psb = ps[:, :].bitcast(BF16)

        def bankb(i, cols=512, off=0):
            return psb[:, i * 1024 + off: i * 1024 + off + cols]

        def sel16(mc):
            return selc[0:128, SELF + mc * 128: SELF + mc * 128 + 128]

        def selfw(mc):
            return selc[0:128, SELF + mc * 128: SELF + (mc + 1) * 128]

        def selsw(mc):
            return selc[0:128, SELS + mc * 128: SELS + (mc + 1) * 128]

        def hsum16(mc):
            return selc[0:128, HSUM16 + mc * 16: HSUM16 + (mc + 1) * 16]

        hsel2 = selc[0:128, HSEL2: HSEL2 + 128]
        ones128b = selc[0:128, ONES128: ONES128 + 1]

        for i, v in [(0, 1e-5), (1, 1e-6), (2, -0.5), (3, -1.0), (4, 1.0),
                     (5, 1.0 / 127.0)]:
            p.step("vector", lambda e, i=i, v=v: e.memset(cval[:, i:i + 1], v),
                   writes=[R_c], sem="s_vec")

        p.step("sync", lambda e: e.dma_start(out=id_f[:, :], in_=cst_f[:, 0:128]),
               writes=[R_c], sem="s_ld", k=16)
        p.step("sync", lambda e: e.dma_start(out=ones1f[:, :],
                                             in_=cst_f[0:1, 128:256]),
               writes=[R_c], sem="s_ld", k=16)
        p.step("gpsimd", lambda e: e.dma_start(out=id_b[:, :], in_=cst_b[:, 0:128]),
               writes=[R_c], sem="s_wl", k=16)
        p.step("gpsimd", lambda e: e.dma_start(out=selc[:, :],
                                               in_=cst_b[:, 128:128 + SELC_W]),
               writes=[R_c], sem="s_wl", k=16)

        def wload(slot, w_h, ncols=1024, nbands=DCH):
            for dc in range(nbands):
                p.step("gpsimd",
                       lambda e, dc=dc, slot=slot, w_h=w_h, ncols=ncols:
                       e.dma_start(out=warena[:, slot * 8192 + dc * ncols:
                                              slot * 8192 + (dc + 1) * ncols],
                                   in_=w_h[ts(dc, 128), :]),
                       writes=[R_wa[slot]], sem="s_wl", k=16)

        def wband(slot, dc, ncols=1024):
            return warena[:, slot * 8192 + dc * ncols:
                          slot * 8192 + (dc + 1) * ncols]

        with nc.sbuf_tensor("xin", [128, 2048], F32) as xin:
            R_xin = [Res(), Res()]
            for r in range(TC // 128):
                buf = r % 2
                p.step("sync",
                       lambda e, r=r, buf=buf:
                       e.dma_start(out=xin[:, ts(buf, 1024)],
                                   in_=x_in[ts(r, 128), :]),
                       writes=[R_xin[buf]], sem="s_ld", k=16)
                for dc in range(DCH):
                    slot = dc % 2
                    p.step("tensor",
                           lambda e, r=r, dc=dc, buf=buf, slot=slot:
                           e.transpose(bank(6 + slot, 128),
                                       xin[:, buf * 1024 + dc * 128:
                                           buf * 1024 + dc * 128 + 128],
                                       id_f[:, :]),
                           reads=[R_xin[buf], R_c], writes=[R_bank[6 + slot]],
                           sem="s_mm")
                    p.step("vector",
                           lambda e, r=r, dc=dc, slot=slot:
                           e.tensor_copy(
                               x_res[:, dc * TC + r * 128: dc * TC + r * 128 + 128],
                               bank(6 + slot, 128)),
                           reads=[R_bank[6 + slot]], writes=[R_x], sem="s_vec")

        wload(0, wk_h)
        wload(1, wv_h)
        R_wc = Res("wcast")
        for ex in range(6):
            for hm in range(16):
                p.step("gpsimd",
                       lambda e, ex=ex, hm=hm:
                       e.dma_start(out=w1c.ap()[ex, :,
                                                 hm * 1024:(hm + 1) * 1024],
                                   in_=w1m_h[ex, hm, :, :]),
                       writes=[R_wc], sem="s_wl", k=16)
            for m2 in range(8):
                p.step("gpsimd",
                       lambda e, ex=ex, m2=m2:
                       e.dma_start(out=w2c.ap()[ex, m2, :, :],
                                   in_=w2m_h[ex, m2, :, :]),
                       writes=[R_wc], sem="s_wl", k=16)
        for oc in range(32):
            p.step("gpsimd",
                   lambda e, oc=oc: e.dma_start(out=inwc.ap()[oc, :, :],
                                                in_=inw_h[oc, :, :]),
                   writes=[R_wc], sem="s_wl", k=16)
        for ic in range(16):
            p.step("gpsimd",
                   lambda e, ic=ic: e.dma_start(out=otwc.ap()[ic, :, :],
                                                in_=otw_h[ic, :, :]),
                   writes=[R_wc], sem="s_wl", k=16)
        for hc in range(24):
            p.step("gpsimd",
                   lambda e, hc=hc: e.dma_start(out=wgtc.ap()[hc, :, :],
                                                in_=wgt_h[hc, :, :]),
                   writes=[R_wc], sem="s_wl", k=16)
            p.step("gpsimd",
                   lambda e, hc=hc: e.dma_start(out=wutc.ap()[hc, :, :],
                                                in_=wut_h[hc, :, :]),
                   writes=[R_wc], sem="s_wl", k=16)
        for mc in range(8):
            p.step("gpsimd",
                   lambda e, mc=mc: e.dma_start(out=wdtc.ap()[mc, :, :],
                                                in_=wdt_h[mc, :, :]),
                   writes=[R_wc], sem="s_wl", k=16)

        p.barrier()

        with contextlib.ExitStack() as p1stack:
            E1 = p1stack.enter_context
            wsm_s = E1(nc.sbuf_tensor("wsm_s", [128, 512], BF16))
            wtf1_s = E1(nc.sbuf_tensor("wtf1_s", [128, 32], BF16))
            wtf2b_s = E1(nc.sbuf_tensor("wtf2b_s", [64, 2], BF16))
            bsm_s = E1(nc.sbuf_tensor("bsm_s", [64, 1], F32))
            btf1_s = E1(nc.sbuf_tensor("btf1_s", [64, 1], F32))
            btf2_s = E1(nc.sbuf_tensor("btf2_s", [2, 1], F32))
            mhat = E1(nc.sbuf_tensor("mhat", [128, 2048], BF16))
            xn = E1(nc.sbuf_tensor("xn", [128, 4096], BF16))
            kq_bf = E1(nc.sbuf_tensor("kq_bf", [128, 4096], BF16))
            mr = E1(nc.sbuf_tensor("mr", [1, 1024], F32))
            sm = E1(nc.sbuf_tensor("sm", [64, 512], F32))
            cums = E1(nc.sbuf_tensor("cums", [32, 512], F32))
            fwb = E1(nc.sbuf_tensor("fwb", [128, 512], BF16))
            sca = E1(nc.sbuf_tensor("sca", [32, 8], F32))
            rb16 = E1(nc.sbuf_tensor("rb16", [128, 1024], BF16))
            scrb = E1(nc.sbuf_tensor("scrb", [128, 2048], BF16))
            lrow = E1(nc.sbuf_tensor("lrow", [128, 1024], BF16))
            rk_s = E1(nc.sbuf_tensor("rk", [16, 512], F32))
            redT = E1(nc.sbuf_tensor("redT", [32, 1], F32))
            redS = E1(nc.sbuf_tensor("redS", [32, 1], F32))
            mstage = scrb[:, 0:1024].bitcast(F32)
            erow = cums[:, :].bitcast(BF16)[0:1, 0:576]
            R_xn = Res(); R_kq = Res(); R_mr = Res(); R_sm = Res()
            R_cums = Res(); R_fwb = Res(); R_sca = Res(); R_rk = Res()
            R_rb16 = Res(); R_scr = [Res(f"scr{i}") for i in range(8)]
            R_lrow = Res(); R_mst = Res(); R_erow = Res()

            def scr(i, cols=512, off=0):
                return scrb[:, i * 512 + off: i * 512 + off + cols]

            p.step("vector", lambda e: e.memset(sca[:, :], 0.0),
                   writes=[R_sca], sem="s_vec")
            p.step("vector", lambda e: e.memset(fwb[:, :], 0.0),
                   writes=[R_fwb], sem="s_vec")
            p.step("vector", lambda e: e.memset(rb16[:, :], 0.0),
                   writes=[R_rb16], sem="s_vec")
            p.step("vector", lambda e: e.memset(lrow[:, :], 0.0),
                   writes=[R_lrow], sem="s_vec")
            p.step("sync", lambda e: e.dma_start(out=bsm_s[:, :], in_=bsm_h[:, :]),
                   writes=[R_c], sem="s_ld", k=16)
            p.step("sync", lambda e: e.dma_start(out=btf1_s[:, :], in_=btf1_h[:, :]),
                   writes=[R_c], sem="s_ld", k=16)
            p.step("sync", lambda e: e.dma_start(out=btf2_s[0:1, :], in_=btf2_h[:, :]),
                   writes=[R_c], sem="s_ld", k=16)
            p.step("gpsimd", lambda e: e.dma_start(out=wtf1_s[0:64, :],
                                                   in_=wtf1_h[:, :]),
                   writes=[R_c], sem="s_wl", k=16)
            p.step("gpsimd", lambda e: e.dma_start(out=wtf1_s[64:128, :],
                                                   in_=wtf1_h[:, :]),
                   writes=[R_c], sem="s_wl", k=16)
            p.step("vector", lambda e: e.memset(wtf2b_s[:, :], 0.0),
                   writes=[R_c], sem="s_vec")
            for bnum in range(2):
                p.step("gpsimd",
                       lambda e, b=bnum: e.dma_start(
                           out=wtf2b_s[ts(b, 32), b:b + 1], in_=wtf2_h[:, :]),
                       writes=[R_c], sem="s_wl", k=16)
            for dc in range(DCH):
                p.step("gpsimd",
                       lambda e, dc=dc: e.dma_start(
                           out=wsm_s[:, ts(dc, 64)], in_=wsm_h[ts(dc, 128), :]),
                       writes=[R_c], sem="s_wl", k=16)
            p.step("sync", lambda e: e.dma_start(out=btf2_s[1:2, :],
                                                 in_=btf2_h[:, :]),
                   writes=[R_c], sem="s_ld", k=16)


            def emit_ln(tile):
                for dc in range(DCH):
                    p.step("vector",
                           lambda e, dc=dc, tile=tile:
                           e.tensor_copy(scr(0), x_res[:, dc * TC + tile * W:
                                                       dc * TC + tile * W + W]),
                           reads=[R_x], writes=[R_scr[0]], sem="s_vec",
                           settle=True)
                    p.step("vector",
                           lambda e: e.tensor_mul(scr(1), scr(0), scr(0)),
                           reads=[R_scr[0]], writes=[R_scr[1]], sem="s_vec",
                           settle=True)
                    p.step("tensor",
                           lambda e, dc=dc:
                           e.matmul(bank(6)[0:1, :], ones128b, scr(0),
                                    start=(dc == 0), stop=(dc == DCH - 1)),
                           reads=[R_scr[0], R_c], writes=[R_bank[6]], sem="s_mm")
                    p.step("tensor",
                           lambda e, dc=dc:
                           e.matmul(bank(7)[0:1, :], ones128b, scr(1),
                                    start=(dc == 0), stop=(dc == DCH - 1)),
                           reads=[R_scr[1], R_c], writes=[R_bank[7]], sem="s_mm")
                p.step("vector",
                       lambda e: e.tensor_scalar(mr[0:1, 0:512], bank(6)[0:1, :],
                                                 1.0 / D, None, ALU.mult),
                       reads=[R_bank[6]], writes=[R_mr], sem="s_vec")
                p.step("vector",
                       lambda e: e.tensor_scalar(mr[0:1, 512:1024],
                                                 bank(7)[0:1, :], 1.0 / D, None,
                                                 ALU.mult),
                       reads=[R_bank[7]], writes=[R_mr], sem="s_vec")
                p.step("scalar",
                       lambda e: e.activation(bank(6)[0:1, :], mr[0:1, 0:512],
                                              AF.Square),
                       reads=[R_mr], writes=[R_bank[6]], sem="s_act")
                p.step("vector",
                       lambda e: e.tensor_sub(mr[0:1, 512:1024],
                                              mr[0:1, 512:1024],
                                              bank(6)[0:1, :]),
                       reads=[R_mr, R_bank[6]], writes=[R_mr], sem="s_vec")
                p.step("scalar",
                       lambda e: e.activation(mr[0:1, 512:1024],
                                              mr[0:1, 512:1024], AF.Ln,
                                              bias=cval[0:1, 0:1]),
                       reads=[R_mr, R_c], writes=[R_mr], sem="s_act")
                p.step("scalar",
                       lambda e: e.activation(mr[0:1, 512:1024],
                                              mr[0:1, 512:1024], AF.Exp,
                                              scale=cval[0:1, 2:3]),
                       reads=[R_mr, R_c], writes=[R_mr], sem="s_act")
                p.step("tensor",
                       lambda e: e.matmul(bank(6), ones1f[:, :], mr[0:1, 0:512],
                                          start=True, stop=True),
                       reads=[R_mr, R_c], writes=[R_bank[6]], sem="s_mm")
                p.step("tensor",
                       lambda e: e.matmul(bank(7), ones1f[:, :],
                                          mr[0:1, 512:1024], start=True,
                                          stop=True),
                       reads=[R_mr, R_c], writes=[R_bank[7]], sem="s_mm")
                for dc in range(DCH):
                    p.step("vector",
                           lambda e, dc=dc, tile=tile:
                           e.tensor_sub(scr(2), x_res[:, dc * TC + tile * W:
                                                      dc * TC + tile * W + W],
                                        bank(6)),
                           reads=[R_x, R_bank[6]], writes=[R_scr[2]],
                           sem="s_vec")
                    p.step("vector",
                           lambda e, dc=dc:
                           e.tensor_mul(xn[:, ts(dc, W)], scr(2), bank(7)),
                           reads=[R_scr[2], R_bank[7]], writes=[R_xn],
                           sem="s_vec", settle=True)

            def emit_proj(wslot, R_w, dest, R_dest, with_sums):
                for mc in range(DCH):
                    pb = 4 if (mc % 2 == 0) else 5
                    for dc in range(DCH):
                        p.step("tensor",
                               lambda e, dc=dc, mc=mc, pb=pb, wslot=wslot:
                               e.matmul(bank(pb), wband(wslot, dc)[:, ts(mc, 128)],
                                        xn[:, ts(dc, W)], start=(dc == 0),
                                        stop=(dc == DCH - 1)),
                               reads=[R_xn, R_w], writes=[R_bank[pb]],
                               sem="s_mm")
                    p.step("scalar",
                           lambda e, mc=mc, pb=pb:
                           e.activation(dest[:, ts(mc, W)], bank(pb), AF.Copy),
                           reads=[R_bank[pb]], writes=[R_dest], sem="s_act")
                    if with_sums:
                        p.step("scalar",
                               lambda e, pb=pb:
                               e.activation(scr(0), bank(pb), AF.Square),
                               reads=[R_bank[pb]], writes=[R_scr[0]],
                               sem="s_act")
                        p.step("tensor",
                               lambda e, mc=mc:
                               e.matmul(bank(6)[0:16, :], hsum16(mc), scr(0),
                                        start=(mc == 0), stop=(mc == DCH - 1)),
                               reads=[R_scr[0], R_c], writes=[R_bank[6]],
                               sem="s_mm")

            def emit_rnorm():
                p.step("vector",
                       lambda e: e.tensor_scalar(rk_s[:, :], bank(6)[0:16, :],
                                                 1e-30, None, ALU.max),
                       reads=[R_bank[6]], writes=[R_rk], sem="s_vec")
                p.step("scalar",
                       lambda e: e.activation(rk_s[:, :], rk_s[:, :], AF.Ln),
                       reads=[R_rk], writes=[R_rk], sem="s_act")
                p.step("scalar",
                       lambda e: e.activation(rk_s[:, :], rk_s[:, :], AF.Exp,
                                              scale=cval[0:16, 2:3]),
                       reads=[R_rk, R_c], writes=[R_rk], sem="s_act")
                p.step("vector",
                       lambda e: e.tensor_scalar(rk_s[:, :], rk_s[:, :], 1e12,
                                                 None, ALU.min),
                       reads=[R_rk], writes=[R_rk], sem="s_vec")

            for qb0 in range(4):
                p.step("vector",
                       lambda e, qb0=qb0: e.memset(bank(qb0), 0.0),
                       writes=[R_bank[qb0]], sem="s_vec")

            for ti, tile in enumerate(range(NT - 1, -1, -1)):
                first_t = (ti == 0)
                emit_ln(tile)
                for dc in range(DCH):
                    p.step("tensor",
                           lambda e, dc=dc:
                           e.matmul(bank(4)[0:64, :], wsm_s[:, ts(dc, 64)],
                                    xn[:, ts(dc, W)], start=(dc == 0),
                                    stop=(dc == DCH - 1)),
                           reads=[R_xn, R_c], writes=[R_bank[4]], sem="s_mm")
                p.step("scalar",
                       lambda e: e.activation(sm[:, :], bank(4)[0:64, :],
                                              AF.Sigmoid, bias=bsm_s[:, 0:1]),
                       reads=[R_bank[4], R_c], writes=[R_sm], sem="s_act")
                p.step("scalar",
                       lambda e: e.activation(cums[:, :], sm[32:64, :], AF.Ln,
                                              bias=cval[0:32, 1:2]),
                       reads=[R_sm, R_c], writes=[R_cums], sem="s_act")
                if tile == NT - 1:
                    pass
                p.step("vector",
                       lambda e: e.tensor_reduce(redT[:, :], cums[:, :], AX.X,
                                                 ALU.add),
                       reads=[R_cums], writes=[R_sca], sem="s_vec", drain=True)
                p.step("vector",
                       lambda e: e.tensor_tensor(sca[:, 1:2], sca[:, 0:1],
                                                 redT[:, :], ALU.add),
                       reads=[R_sca], writes=[R_sca], sem="s_vec")
                p.step("vector",
                       lambda e: e.tensor_tensor_scan(
                           cums[:, :], cval[0:32, 4:5].broadcast_to([32, 512]),
                           cums[:, :], 0.0, ALU.mult, ALU.add),
                       reads=[R_cums, R_c], writes=[R_cums], sem="s_vec",
                       drain=True)
                if tile == NT - 1:
                    pass

                p.step("scalar",
                       lambda e: e.activation(fwb[0:32, :], cums[:, :], AF.Exp,
                                              scale=cval[0:32, 3:4],
                                              bias=sca[:, 1:2]),
                       reads=[R_cums, R_sca, R_c], writes=[R_fwb], sem="s_act")
                p.step("vector",
                       lambda e: e.tensor_copy(sca[:, 0:1], sca[:, 1:2]),
                       reads=[R_sca], writes=[R_sca], sem="s_vec")
                pass
                pass
                pass
                p.step("vector",
                       lambda e: e.tensor_reduce(redS[:, :], fwb[0:32, :], AX.X,
                                                 ALU.add),
                       reads=[R_fwb], writes=[R_sca], sem="s_vec", drain=True)
                p.step("vector",
                       lambda e: e.tensor_tensor(sca[:, 3:4], sca[:, 3:4],
                                                 redS[:, :], ALU.add),
                       reads=[R_sca], writes=[R_sca], sem="s_vec")

                emit_proj(0, R_wa[0], kq_bf, R_kq, with_sums=True)
                emit_rnorm()
                p.step("vector",
                       lambda e: e.tensor_tensor(rb16[0:16, 0:512], rk_s[:, :],
                                                 sm[0:16, :], ALU.mult),
                       reads=[R_rk, R_sm], writes=[R_rb16], sem="s_vec",
                       settle=True)
                p.step("vector",
                       lambda e: e.tensor_copy(rb16[0:16, 512:1024],
                                               sm[0:16, :]),
                       reads=[R_sm], writes=[R_rb16], sem="s_vec",
                       settle=True)

                for mc in range(DCH):
                    pb = 4 if (mc % 2 == 0) else 5
                    for dc in range(DCH):
                        p.step("tensor",
                               lambda e, dc=dc, mc=mc, pb=pb:
                               e.matmul(bank(pb), wband(1, dc)[:, ts(mc, 128)],
                                        xn[:, ts(dc, W)], start=(dc == 0),
                                        stop=(dc == DCH - 1)),
                               reads=[R_xn, R_wa[1]], writes=[R_bank[pb]],
                               sem="s_mm")
                    p.step("tensor",
                           lambda e, mc=mc:
                           e.matmul(bank(7), sel16(mc), rb16[0:128, 512:1024],
                                    start=True, stop=True),
                           reads=[R_rb16, R_c], writes=[R_bank[7]], sem="s_mm")
                    p.step("scalar",
                           lambda e, pb=pb: e.activation(scr(0), bank(pb),
                                                         AF.Copy),
                           reads=[R_bank[pb]], writes=[R_scr[0]], sem="s_act")
                    p.step("vector",
                           lambda e: e.tensor_tensor(scr(1), scr(0),
                                                     bank(7), ALU.mult),
                           reads=[R_scr[0], R_bank[7]], writes=[R_scr[1]],
                           sem="s_vec", settle=True)
                    p.step("tensor",
                           lambda e, mc=mc:
                           e.matmul(bank(7), sel16(mc), rb16[0:128, 0:512],
                                    start=True, stop=True),
                           reads=[R_rb16, R_c], writes=[R_bank[7]], sem="s_mm")
                    p.step("vector",
                           lambda e, mc=mc: e.tensor_tensor(
                               scr(2), kq_bf[:, ts(mc, W)], bank(7), ALU.mult),
                           reads=[R_kq, R_bank[7]], writes=[R_scr[2]],
                           sem="s_vec", settle=True)
                    if not NOFLUX:
                        p.step("tensor",
                               lambda e: e.matmul(bank(6)[0:32, :], wtf1_s[0:64, :],
                                                  scrb[0:64, 1024:1536],
                                                  start=True, stop=True),
                               reads=[R_scr[2], R_c], writes=[R_bank[6]],
                               sem="s_mm")
                        p.step("tensor",
                               lambda e: e.matmul(bank(6)[32:64, :],
                                                  wtf1_s[64:128, :],
                                                  scrb[64:128, 1024:1536],
                                                  start=True, stop=True),
                               reads=[R_scr[2], R_c], writes=[R_bank[6]],
                               sem="s_mm")
                        p.step("scalar",
                               lambda e: e.activation(scrb[0:64, 1536:2048],
                                                      bank(6)[0:64, :], AF.Silu,
                                                      bias=btf1_s[:, 0:1]),
                               reads=[R_bank[6], R_c], writes=[R_scr[3]],
                               sem="s_act")
                        p.step("tensor",
                               lambda e: e.matmul(bank(7)[0:2, :], wtf2b_s[:, :],
                                                  scrb[0:64, 1536:2048],
                                                  start=True, stop=True),
                               reads=[R_scr[3], R_c], writes=[R_bank[7]],
                               sem="s_mm")
                        p.step("vector",
                               lambda e, mc=mc:
                               e.tensor_scalar(lrow[0:2, ts(mc % 2, W)],
                                               bank(7)[0:2, :],
                                               btf2_s[0:2, 0:1], None, ALU.add),
                               reads=[R_bank[7], R_c], writes=[R_lrow], sem="s_vec")
                        if mc % 2 == 1:
                            p.step("sync",
                                   lambda e, tile=tile, mh=mc // 2:
                                   e.dma_start(out=alog_d.ap()[tile, :,
                                                               mh * 1024:
                                                               (mh + 1) * 1024],
                                               in_=lrow[0:2, :]),
                                   reads=[R_lrow], writes=[R_alog], sem="s_st",
                                   k=16)
                    p.step("tensor",
                           lambda e, mc=mc: e.matmul(bank(7), selsw(mc),
                                                     fwb[0:128, :], start=True,
                                                     stop=True),
                           reads=[R_fwb, R_c], writes=[R_bank[7]], sem="s_mm")
                    p.step("vector",
                           lambda e: e.tensor_tensor(scr(0), scr(2), bank(7),
                                                     ALU.mult),
                           reads=[R_scr[2], R_bank[7]], writes=[R_scr[0]],
                           sem="s_vec", settle=True)
                    p.step("tensor",
                           lambda e, mc=mc: e.matmul(bank(7), selfw(mc),
                                                     fwb[0:128, :], start=True,
                                                     stop=True),
                           reads=[R_fwb, R_c], writes=[R_bank[7]], sem="s_mm")
                    p.step("vector",
                           lambda e: e.tensor_tensor(scr(2), scr(2), bank(7),
                                                     ALU.mult),
                           reads=[R_scr[2], R_bank[7]], writes=[R_scr[2]],
                           sem="s_vec", settle=True)
                    for lb in range(4):
                        for j, (src_i, rsrc) in enumerate(
                                [(2, R_scr[2]), (0, R_scr[0]), (1, R_scr[1])]):
                            p.step("tensor",
                                   lambda e, lb=lb, j=j, src_i=src_i:
                                   e.transpose(
                                       bankb(6, 128, off=j * 128),
                                       scrb[:, src_i * 512 + lb * 128:
                                            src_i * 512 + lb * 128 + 128],
                                       id_b[:, :]),
                                   reads=[rsrc, R_c], writes=[R_bank[6]],
                                   sem="s_mm")
                        p.step("vector",
                               lambda e: e.tensor_copy(scr(3, 384),
                                                       bankb(6, 384)),
                               reads=[R_bank[6]], writes=[R_scr[3]],
                               sem="s_vec", settle=True)
                        if tile == NT - 1 and lb == 3 and mc in (4, 6):
                            pass
                        bf_ = 0 if mc < 4 else 1
                        bs_ = 3 if mc < 4 else 2
                        p.step("tensor",
                               lambda e, mc=mc, lb=lb, bf_=bf_:
                               e.matmul(bank(bf_, 128, off=(mc % 4) * 128),
                                        scr(3, 128), scr(3, 128, off=256),
                                        start=False, stop=(lb == 3)),
                               reads=[R_scr[3]], writes=[R_bank[bf_]],
                               sem="s_mm")
                        p.step("tensor",
                               lambda e, mc=mc, lb=lb, bs_=bs_:
                               e.matmul(bank(bs_, 128, off=(mc % 4) * 128),
                                        scr(3, 128, off=128),
                                        scr(3, 128, off=256),
                                        start=False, stop=(lb == 3)),
                               reads=[R_scr[3]], writes=[R_bank[bs_]],
                               sem="s_mm")

                if tile == NT - 1:
                    fz = mhat[:, :].bitcast(F32)
                    p.step("vector",
                           lambda e, fz=fz: e.tensor_copy(fz[:, 0:512],
                                                          bank(2)),
                           reads=[R_bank[2]], writes=[R_mhat], sem="s_vec")
                    pass
                for qb, bsrc in [(0, 0), (1, 1), (2, 3), (3, 2)]:
                    p.step("vector",
                           lambda e, qb=qb, bsrc=bsrc: e.tensor_copy(
                               mstage[:, :], bank(bsrc)),
                           reads=[R_bank[bsrc]], writes=[R_mst, R_scr[0],
                                                         R_scr[1]], sem="s_vec")
                    if first_t:
                        pass
                    p.step("gpsimd",
                           lambda e, qb=qb, first_t=first_t:
                           e.dma_start(out=cc_in.ap()[:, qb * 512:
                                                      (qb + 1) * 512],
                                       in_=mstage[:, :],
                                       accum_op=(ALU.bypass if first_t
                                                 else ALU.add)),
                           reads=[R_mst, R_scr[0], R_scr[1]],
                           writes=[R_ccin], sem="s_wl", k=16)
                    p.step("vector",
                           lambda e, bsrc=bsrc: e.memset(bank(bsrc), 0.0),
                           reads=[], writes=[R_bank[bsrc]], sem="s_vec")

            p.step("sync",
                   lambda e: e.dma_start(out=cc_in.ap()[0:32, 2048:2049],
                                         in_=sca[:, 3:4],
                                         allow_slow_non_contiguous=True),
                   reads=[R_sca], writes=[R_ccin], sem="s_st", k=16)
            p.step("sync",
                   lambda e: e.dma_start(out=cc_in.ap()[0:32, 2049:2050],
                                         in_=sca[:, 0:1],
                                         allow_slow_non_contiguous=True),
                   reads=[R_sca], writes=[R_ccin], sem="s_st", k=16)

            wload(0, wq_h)
            wload(1, wo_h)

            p.step("gpsimd",
                   lambda e: e.collective_compute(
                       "AllGather", ALU.bypass, replica_groups=REPL,
                       ins=[cc_in.ap().opt()], outs=[cc_out.ap().opt()]),
                   reads=[R_ccin], writes=[R_ccout], sem="s_cc", k=1)

            g0 = kq_bf[:, :].bitcast(F32)
            g1 = xn[:, :].bitcast(F32)
            sc01 = mr[0:1, :]
            p.step("sync",
                   lambda e: e.dma_start(out=g0, in_=cc_out.ap()[0:128, 0:2048]),
                   reads=[R_ccout], writes=[R_kq], sem="s_ld", k=16)
            p.step("sync",
                   lambda e: e.dma_start(out=g1,
                                         in_=cc_out.ap()[128:256, 0:2048]),
                   reads=[R_ccout], writes=[R_xn], sem="s_ld", k=16)
            p.step("sync",
                   lambda e: e.dma_start(out=sca[:, 4:6],
                                         in_=cc_out.ap()[0:32, 2048:2050],
                                         allow_slow_non_contiguous=True),
                   reads=[R_ccout], writes=[R_sca], sem="s_ld", k=16)
            p.step("sync",
                   lambda e: e.dma_start(out=sca[:, 6:8],
                                         in_=cc_out.ap()[128:160, 2048:2050],
                                         allow_slow_non_contiguous=True),
                   reads=[R_ccout], writes=[R_sca], sem="s_ld", k=16)
            p.step("scalar",
                   lambda e: e.activation(sca[:, 7:8], sca[:, 7:8], AF.Exp),
                   reads=[R_sca], writes=[R_sca], sem="s_act")
            p.step("vector",
                   lambda e: e.scalar_tensor_tensor(sca[:, 5:6], sca[:, 4:5],
                                                    sca[:, 7:8], sca[:, 6:7],
                                                    ALU.mult, ALU.add),
                   reads=[R_sca], writes=[R_sca], sem="s_vec")
            p.step("vector",
                   lambda e: e.tensor_scalar(sca[:, 5:6], sca[:, 5:6], 1e-6,
                                             None, ALU.add),
                   reads=[R_sca], writes=[R_sca], sem="s_vec")
            p.step("vector",
                   lambda e: e.reciprocal(sca[:, 5:6], sca[:, 5:6]),
                   reads=[R_sca], writes=[R_sca], sem="s_vec")
            p.step("vector",
                   lambda e: e.tensor_tensor(sca[:, 7:8], sca[:, 7:8],
                                             sca[:, 5:6], ALU.mult),
                   reads=[R_sca], writes=[R_sca], sem="s_vec")
            p.step("tensor",
                   lambda e: e.matmul(bank(6)[0:1, 0:32], sca[:, 7:8],
                                      id_f[0:32, 0:32], start=True, stop=True),
                   reads=[R_sca, R_c], writes=[R_bank[6]], sem="s_mm")
            p.step("tensor",
                   lambda e: e.matmul(bank(6)[0:1, 32:64], sca[:, 5:6],
                                      id_f[0:32, 0:32], start=True, stop=True),
                   reads=[R_sca, R_c], writes=[R_bank[6]], sem="s_mm")
            p.step("vector",
                   lambda e: e.tensor_copy(erow[0:1, 512:576],
                                           bank(6)[0:1, 0:64]),
                   reads=[R_bank[6]], writes=[R_erow, R_cums], sem="s_vec")
            for cb4 in range(4):
                p.step("vector",
                       lambda e, cb4=cb4: e.tensor_copy(
                           mstage[:, :], g0[:, cb4 * 512:(cb4 + 1) * 512]),
                       reads=[R_kq], writes=[R_mst, R_scr[0], R_scr[1]],
                       sem="s_vec")
                pass
            p.step("vector",
                   lambda e: e.tensor_copy(mstage[0:1, 0:64],
                                           erow[0:1, 512:576]),
                   reads=[R_erow], writes=[R_mst, R_scr[0], R_scr[1]],
                   sem="s_vec")
            pass
            for cb in range(4):
                half, q2 = cb // 2, cb % 2
                for sci, bk in ((0, 6), (1, 7)):
                    base = 512 + sci * 32 + half * 16 + q2 * 8
                    p.step("vector",
                           lambda e, base=base:
                           e.tensor_copy(
                               erow[0:1, 0:512],
                               erow[0:1, base: base + 8]
                               .rearrange("p (m h) -> p m h", h=2)
                               .broadcast_to([1, 4, 2, 64])),
                           reads=[R_erow], writes=[R_erow], sem="s_vec")
                    p.step("tensor",
                           lambda e, bk=bk: e.matmul(
                               bank(bk), selc[0:1, ONES1B:ONES1B + 128],
                               erow[0:1, 0:512], start=True, stop=True),
                           reads=[R_erow, R_c], writes=[R_bank[bk]],
                           sem="s_mm")
                t2 = scrb[:, 1024:2048].bitcast(F32)
                p.step("vector",
                       lambda e, cb=cb: e.tensor_tensor(
                           mstage[:, 0:512], g0[:, cb * 512:(cb + 1) * 512],
                           bank(6), ALU.mult),
                       reads=[R_kq, R_bank[6]], writes=[R_mst, R_scr[0],
                                                        R_scr[1]],
                       sem="s_vec")
                p.step("vector",
                       lambda e, cb=cb, t2=t2: e.tensor_tensor(
                           t2, g1[:, cb * 512:(cb + 1) * 512],
                           bank(7), ALU.mult),
                       reads=[R_xn, R_bank[7]], writes=[R_scr[2], R_scr[3]],
                       sem="s_vec")
                p.step("vector",
                       lambda e, cb=cb, t2=t2: e.tensor_tensor(
                           mhat[:, cb * 512:(cb + 1) * 512], mstage[:, 0:512],
                           t2, ALU.add),
                       reads=[R_mst, R_scr[2], R_scr[3]], writes=[R_mhat],
                       sem="s_vec", settle=True)

            p.step("vector",
                   lambda e: e.tensor_copy(mstage[0:32, 0:128],
                                           selc[0:32, SELF + 5 * 128:
                                                SELF + 6 * 128]),
                   reads=[R_c], writes=[R_mst, R_scr[0], R_scr[1]],
                   sem="s_vec")
            p.step("vector",
                   lambda e: e.tensor_copy(mstage[0:32, 128:256],
                                           selc[0:32, SELS + 5 * 128:
                                                SELS + 6 * 128]),
                   reads=[R_c], writes=[R_mst, R_scr[0], R_scr[1]],
                   sem="s_vec")
            pass
            for cb in range(4):
                p.step("vector",
                       lambda e, cb=cb: e.tensor_copy(
                           mstage[:, :], mhat[:, cb * 512:(cb + 1) * 512]),
                       reads=[R_mhat], writes=[R_mst, R_scr[0], R_scr[1]],
                       sem="s_vec")
                pass
            pass
            p.step("vector",
                   lambda e: e.tensor_copy(mstage[0:32, :], fwb[0:32, :]),
                   reads=[R_fwb], writes=[R_mst, R_scr[0], R_scr[1]],
                   sem="s_vec")
            pass

            for tile in (range(NT) if upto != "p1a" else []):
                emit_ln(tile)
                emit_proj(0, R_wa[0], kq_bf, R_kq, with_sums=True)
                emit_rnorm()
                p.step("vector",
                       lambda e: e.tensor_copy(rb16[0:16, 0:512], rk_s[:, :]),
                       reads=[R_rk], writes=[R_rb16], sem="s_vec",
                       settle=True)

                for mc in range(DCH):
                    if mc % 2 == 0:
                        p.step("sync",
                               lambda e, tile=tile, mh=mc // 2:
                               e.dma_start(out=lrow[0:2, :],
                                           in_=alog_d.ap()[tile, :,
                                                           mh * 1024:
                                                           (mh + 1) * 1024]),
                               reads=[R_alog], writes=[R_lrow], sem="s_ld",
                               k=16)
                    p.step("tensor",
                           lambda e, mc=mc: e.matmul(bank(7), sel16(mc),
                                                     rb16[0:128, 0:512],
                                                     start=True, stop=True),
                           reads=[R_rb16, R_c], writes=[R_bank[7]], sem="s_mm")
                    p.step("vector",
                           lambda e, mc=mc: e.tensor_tensor(
                               scr(0), kq_bf[:, ts(mc, W)], bank(7), ALU.mult),
                           reads=[R_kq, R_bank[7]], writes=[R_scr[0]],
                           sem="s_vec", settle=True)
                    for hh in range(2):
                        p.step("tensor",
                               lambda e, mc=mc, hh=hh: e.matmul(
                                   bank(4)[hh * 64:(hh + 1) * 64, :],
                                   mhat[hh * 64:(hh + 1) * 64,
                                        mc * 128 + hh * 64: mc * 128 + hh * 64
                                        + 64],
                                   scrb[hh * 64:(hh + 1) * 64, 0:512],
                                   start=True, stop=True),
                               reads=[R_mhat, R_scr[0]], writes=[R_bank[4]],
                               sem="s_mm")
                        p.step("tensor",
                               lambda e, mc=mc, hh=hh: e.matmul(
                                   bank(5)[hh * 64:(hh + 1) * 64, :],
                                   mhat[hh * 64:(hh + 1) * 64,
                                        1024 + mc * 128 + hh * 64:
                                        1024 + mc * 128 + hh * 64 + 64],
                                   scrb[hh * 64:(hh + 1) * 64, 0:512],
                                   start=True, stop=True),
                               reads=[R_mhat, R_scr[0]], writes=[R_bank[5]],
                               sem="s_mm")
                    p.step("tensor",
                           lambda e, mc=mc: e.matmul(bank(6), hsel2,
                                                     lrow[0:128, ts(mc % 2, W)],
                                                     start=True, stop=True),
                           reads=[R_lrow, R_c], writes=[R_bank[6]], sem="s_mm")
                    p.step("scalar",
                           lambda e: e.activation(scr(1), bank(6), AF.Sigmoid),
                           reads=[R_bank[6]], writes=[R_scr[1]], sem="s_act")
                    p.step("vector",
                           lambda e: e.tensor_scalar(scr(1), scr(1), 0.3, 0.5,
                                                     ALU.mult, ALU.add),
                           reads=[R_scr[1]], writes=[R_scr[1]], sem="s_vec")
                    p.step("vector",
                           lambda e: e.tensor_scalar(scr(1), scr(1), 0.503,
                                                     0.797, ALU.max, ALU.min),
                           reads=[R_scr[1]], writes=[R_scr[1]], sem="s_vec")
                    p.step("scalar",
                           lambda e: e.activation(scr(2), bank(4), AF.Copy),
                           reads=[R_bank[4]], writes=[R_scr[2]], sem="s_act")
                    p.step("vector",
                           lambda e: e.tensor_sub(scr(2), scr(2), bank(5)),
                           reads=[R_scr[2], R_bank[5]], writes=[R_scr[2]],
                           sem="s_vec")
                    p.step("vector",
                           lambda e: e.tensor_mul(scr(3), scr(1), scr(2)),
                           reads=[R_scr[1], R_scr[2]], writes=[R_scr[3]],
                           sem="s_vec")
                    p.step("vector",
                           lambda e, mc=mc: e.tensor_tensor(
                               kq_bf[:, ts(mc, W)], bank(5), scr(3), ALU.add),
                           reads=[R_bank[5], R_scr[3]], writes=[R_kq],
                           sem="s_vec", settle=True)
                if tile == NT - 1:
                    for mc8 in range(DCH):
                        p.step("vector",
                               lambda e, mc8=mc8: e.tensor_copy(
                                   mstage[:, :], kq_bf[:, ts(mc8, W)]),
                               reads=[R_kq], writes=[R_mst, R_scr[0],
                                                     R_scr[1]], sem="s_vec")
                        pass
                    pass
                for m2 in range(DCH):
                    pb = 4 if (m2 % 2 == 0) else 5
                    for mc in range(DCH):
                        p.step("tensor",
                               lambda e, m2=m2, mc=mc, pb=pb:
                               e.matmul(bank(pb), wband(1, mc)[:, ts(m2, 128)],
                                        kq_bf[:, ts(mc, W)], start=(mc == 0),
                                        stop=(mc == DCH - 1)),
                               reads=[R_kq, R_wa[1]], writes=[R_bank[pb]],
                               sem="s_mm")
                    p.step("vector",
                           lambda e, m2=m2, tile=tile, pb=pb:
                           e.tensor_tensor(
                               x_res[:, m2 * TC + tile * W:
                                     m2 * TC + tile * W + W],
                               x_res[:, m2 * TC + tile * W:
                                     m2 * TC + tile * W + W],
                               bank(pb), ALU.add),
                           reads=[R_bank[pb], R_x], writes=[R_x], sem="s_vec")

        if upto not in ("p1a", "p1"):
            p.barrier()
            with contextlib.ExitStack() as p2stack:
                E2 = p2stack.enter_context
                xn2 = warena[:, 0:4096]
                hbuf = warena[:, 4096:12288]
                w1b = warena[:, 12288:16384]
                w2b = E2(nc.sbuf_tensor("w2b", [128, 4096], BF16))
                mr2 = E2(nc.sbuf_tensor("mr2", [1, 1024], F32))
                s2 = E2(nc.sbuf_tensor("s2", [128, 3072], BF16))
                comb = E2(nc.sbuf_tensor("comb", [128, 512], BF16))
                sel6 = E2(nc.sbuf_tensor("sel6_s", [128, 768], BF16))
                wg_s = E2(nc.sbuf_tensor("wg_sb", [128, 64], BF16))
                b1p = E2(nc.sbuf_tensor("b1p_s", [128, 96], F32))
                b2p = E2(nc.sbuf_tensor("b2p_s", [128, 48], F32))
                bgp = E2(nc.sbuf_tensor("bgp", [8, 1], F32))
                lgr = E2(nc.sbuf_tensor("lgr", [8, 512], F32))
                rt = E2(nc.sbuf_tensor("rt", [128, 40], F32))
                rmax1 = E2(nc.sbuf_tensor("rmax1", [128, 1], F32))
                rmax2 = E2(nc.sbuf_tensor("rmax2", [128, 1], F32))

                R_xn2 = Res(); R_h = Res(); R_w1 = [Res(), Res()]
                R_w2 = [Res(), Res()]
                R_mr2 = Res(); R_s2 = [Res() for _ in range(6)]
                R_comb = Res(); R_lgr = Res(); R_rt = Res()

                def s2s(i, cols=512, off=0):
                    return s2[:, i * 512 + off: i * 512 + off + cols]

                p.step("gpsimd", lambda e: e.dma_start(out=sel6[:, :],
                                                       in_=sel6_h[:, :]),
                       writes=[R_c], sem="s_wl", k=16)
                for dc in range(DCH):
                    p.step("gpsimd",
                           lambda e, dc=dc: e.dma_start(
                               out=wg_s[:, ts(dc, 8)],
                               in_=wg_h[ts(dc, 128), :]),
                           writes=[R_c], sem="s_wl", k=16)
                p.step("sync", lambda e: e.dma_start(out=b1p[:, :],
                                                     in_=b1p_h[:, :]),
                       writes=[R_c], sem="s_ld", k=16)
                p.step("sync", lambda e: e.dma_start(out=b2p[:, :],
                                                     in_=b2p_h[:, :]),
                       writes=[R_c], sem="s_ld", k=16)
                p.step("sync", lambda e: e.dma_start(out=bgp[:, :],
                                                     in_=bg_h[:, :]),
                       writes=[R_c], sem="s_ld", k=16)
                p.step("vector", lambda e: e.memset(comb[:, :], 0.0),
                       writes=[R_comb], sem="s_vec")
                p.step("vector", lambda e: e.memset(rt[:, :], 0.0),
                       writes=[R_rt], sem="s_vec")

                for pt in range(NT):
                    base = pt * W
                    for dc in range(DCH):
                        p.step("vector",
                               lambda e, dc=dc, base=base:
                               e.tensor_copy(s2s(0),
                                             x_res[:, dc * TC + base:
                                                   dc * TC + base + W]),
                               reads=[R_x], writes=[R_s2[0]], sem="s_vec",
                               settle=True)
                        p.step("vector",
                               lambda e: e.tensor_mul(s2s(1), s2s(0), s2s(0)),
                               reads=[R_s2[0]], writes=[R_s2[1]], sem="s_vec",
                               settle=True)
                        p.step("tensor",
                               lambda e, dc=dc:
                               e.matmul(bank(6)[0:1, :], ones128b, s2s(0),
                                        start=(dc == 0), stop=(dc == DCH - 1)),
                               reads=[R_s2[0], R_c], writes=[R_bank[6]],
                               sem="s_mm")
                        p.step("tensor",
                               lambda e, dc=dc:
                               e.matmul(bank(7)[0:1, :], ones128b, s2s(1),
                                        start=(dc == 0), stop=(dc == DCH - 1)),
                               reads=[R_s2[1], R_c], writes=[R_bank[7]],
                               sem="s_mm")
                    p.step("vector",
                           lambda e: e.tensor_scalar(mr2[0:1, 0:512],
                                                     bank(6)[0:1, :], 1.0 / D,
                                                     None, ALU.mult),
                           reads=[R_bank[6]], writes=[R_mr2], sem="s_vec")
                    p.step("vector",
                           lambda e: e.tensor_scalar(mr2[0:1, 512:1024],
                                                     bank(7)[0:1, :], 1.0 / D,
                                                     None, ALU.mult),
                           reads=[R_bank[7]], writes=[R_mr2], sem="s_vec")
                    p.step("scalar",
                           lambda e: e.activation(bank(6)[0:1, :],
                                                  mr2[0:1, 0:512], AF.Square),
                           reads=[R_mr2], writes=[R_bank[6]], sem="s_act")
                    p.step("vector",
                           lambda e: e.tensor_sub(mr2[0:1, 512:1024],
                                                  mr2[0:1, 512:1024],
                                                  bank(6)[0:1, :]),
                           reads=[R_mr2, R_bank[6]], writes=[R_mr2],
                           sem="s_vec")
                    p.step("scalar",
                           lambda e: e.activation(mr2[0:1, 512:1024],
                                                  mr2[0:1, 512:1024], AF.Ln,
                                                  bias=cval[0:1, 0:1]),
                           reads=[R_mr2, R_c], writes=[R_mr2], sem="s_act")
                    p.step("scalar",
                           lambda e: e.activation(mr2[0:1, 512:1024],
                                                  mr2[0:1, 512:1024], AF.Exp,
                                                  scale=cval[0:1, 2:3]),
                           reads=[R_mr2, R_c], writes=[R_mr2], sem="s_act")
                    p.step("tensor",
                           lambda e: e.matmul(bank(6), ones1f[:, :],
                                              mr2[0:1, 0:512], start=True,
                                              stop=True),
                           reads=[R_mr2, R_c], writes=[R_bank[6]], sem="s_mm")
                    p.step("tensor",
                           lambda e: e.matmul(bank(7), ones1f[:, :],
                                              mr2[0:1, 512:1024], start=True,
                                              stop=True),
                           reads=[R_mr2, R_c], writes=[R_bank[7]], sem="s_mm")
                    for dc in range(DCH):
                        p.step("vector",
                               lambda e, dc=dc, base=base:
                               e.tensor_sub(s2s(2),
                                            x_res[:, dc * TC + base:
                                                  dc * TC + base + W],
                                            bank(6)),
                               reads=[R_x, R_bank[6]], writes=[R_s2[2]],
                               sem="s_vec")
                        p.step("vector",
                               lambda e, dc=dc:
                               e.tensor_mul(xn2[:, ts(dc, W)], s2s(2),
                                            bank(7)),
                               reads=[R_s2[2], R_bank[7]], writes=[R_xn2],
                               sem="s_vec", settle=True)
                    if pt == 0:
                        pass

                    for dc in range(DCH):
                        p.step("tensor",
                               lambda e, dc=dc:
                               e.matmul(bank(6)[0:8, :], wg_s[:, ts(dc, 8)],
                                        xn2[:, ts(dc, W)], start=(dc == 0),
                                        stop=(dc == DCH - 1)),
                               reads=[R_xn2, R_c], writes=[R_bank[6]],
                               sem="s_mm")
                    p.step("scalar",
                           lambda e: e.activation(lgr[:, :], bank(6)[0:8, :],
                                                  AF.Identity,
                                                  bias=bgp[:, 0:1]),
                           reads=[R_bank[6], R_c], writes=[R_lgr],
                           sem="s_act", settle=False)
                    if pt == 0:
                        pass
                    for lb in range(4):
                        p.step("tensor",
                               lambda e, lb=lb:
                               e.transpose(bank(7, 8), lgr[:, ts(lb, 128)],
                                           id_f[0:8, 0:8]),
                               reads=[R_lgr, R_c], writes=[R_bank[7]],
                               sem="s_mm")
                        p.step("vector",
                               lambda e: e.tensor_copy(rt[:, 0:8],
                                                       bank(7, 8)),
                               reads=[R_bank[7]], writes=[R_rt], sem="s_vec", drain=True)
                        if pt == 0 and lb == 0:
                            pass
                        p.step("vector",
                               lambda e: e.tensor_reduce(rmax1[:, :],
                                                         rt[:, 0:6], AX.X,
                                                         ALU.max),
                               reads=[R_rt], writes=[R_rt], sem="s_vec",
                               drain=True)
                        p.step("vector",
                               lambda e: e.tensor_scalar(rt[:, 16:22],
                                                         rt[:, 0:6],
                                                         rmax1[:, :], None,
                                                         ALU.is_equal),
                               reads=[R_rt], writes=[R_rt], sem="s_vec", drain=True)
                        p.step("vector",
                               lambda e: e.scalar_tensor_tensor(
                                   rt[:, 24:30], rt[:, 16:22], -1e30,
                                   rt[:, 0:6], ALU.mult, ALU.add),
                               reads=[R_rt], writes=[R_rt], sem="s_vec", drain=True)
                        p.step("vector",
                               lambda e: e.tensor_reduce(rmax2[:, :],
                                                         rt[:, 24:30], AX.X,
                                                         ALU.max),
                               reads=[R_rt], writes=[R_rt], sem="s_vec",
                               drain=True)
                        p.step("vector",
                               lambda e: e.tensor_sub(rt[:, 10:11],
                                                      rmax1[:, :],
                                                      rmax2[:, :]),
                               reads=[R_rt], writes=[R_rt], sem="s_vec", drain=True)
                        p.step("scalar",
                               lambda e: e.activation(rt[:, 11:12],
                                                      rt[:, 10:11],
                                                      AF.Sigmoid),
                               reads=[R_rt], writes=[R_rt], sem="s_act")
                        p.step("vector",
                               lambda e: e.tensor_scalar(rt[:, 12:13],
                                                         rt[:, 11:12], -1.0,
                                                         1.0, ALU.mult,
                                                         ALU.add),
                               reads=[R_rt], writes=[R_rt], sem="s_vec", drain=True)
                        p.step("vector",
                               lambda e: e.tensor_scalar(rt[:, 32:38],
                                                         rt[:, 24:30],
                                                         rmax2[:, :], None,
                                                         ALU.is_equal),
                               reads=[R_rt], writes=[R_rt], sem="s_vec", drain=True)
                        p.step("vector",
                               lambda e: e.tensor_scalar(rt[:, 16:22],
                                                         rt[:, 16:22],
                                                         rt[:, 11:12], None,
                                                         ALU.mult),
                               reads=[R_rt], writes=[R_rt], sem="s_vec", drain=True)
                        p.step("vector",
                               lambda e: e.tensor_scalar(rt[:, 32:38],
                                                         rt[:, 32:38],
                                                         rt[:, 12:13], None,
                                                         ALU.mult),
                               reads=[R_rt], writes=[R_rt], sem="s_vec", drain=True)
                        p.step("vector",
                               lambda e: e.tensor_tensor(rt[:, 16:22],
                                                         rt[:, 16:22],
                                                         rt[:, 32:38],
                                                         ALU.add),
                               reads=[R_rt], writes=[R_rt], sem="s_vec",
                               drain=True, settle=True)
                        if pt == 0 and lb == 0:
                            pass
                        p.step("tensor",
                               lambda e, lb=lb:
                               e.transpose(bank(6, 128)[0:8, :], rt[:, 16:24],
                                           id_f[:, :]),
                               reads=[R_rt, R_c], writes=[R_bank[6]],
                               sem="s_mm")
                        p.step("vector",
                               lambda e, lb=lb: e.tensor_copy(
                                   comb[0:8, ts(lb, 128)],
                                   bank(6, 128)[0:8, :]),
                               reads=[R_bank[6]], writes=[R_comb],
                               sem="s_vec", settle=True)

                    if pt == 0:
                        pass

                    for ex in range(6):
                        for hm in range(16):
                            sl_ = (hm // 2) % 2
                            pb_ = hm % 2
                            if hm % 2 == 0:
                                p.step("sync",
                                       lambda e, ex=ex, hm=hm, sl_=sl_:
                                       e.dma_start(
                                           out=w1b[:, ts(sl_, 2048)],
                                           in_=w1c.ap()[ex, :,
                                                        hm * 1024:
                                                        hm * 1024 + 2048]),
                                       reads=[R_wc],
                                       writes=[R_w1[sl_]], sem="s_ld",
                                       k=16)
                            off1 = sl_ * 2048 + (hm % 2) * 1024
                            for dc in range(DCH):
                                p.step("tensor",
                                       lambda e, dc=dc, sl_=sl_, off1=off1,
                                       pb_=pb_:
                                       e.matmul(bank(4 + pb_),
                                                w1b[:, off1 + dc * 128:
                                                    off1 + dc * 128 + 128],
                                                xn2[:, ts(dc, W)],
                                                start=(dc == 0),
                                                stop=(dc == DCH - 1)),
                                       reads=[R_xn2, R_w1[sl_]],
                                       writes=[R_bank[4 + pb_]], sem="s_mm")
                            p.step("scalar",
                                   lambda e, ex=ex, hm=hm, pb_=pb_:
                                   e.activation(hbuf[:, ts(hm, W)],
                                                bank(4 + pb_), AF.Gelu,
                                                bias=b1p[:, ex * 16 + hm:
                                                         ex * 16 + hm + 1]),
                                   reads=[R_bank[4 + pb_], R_c],
                                   writes=[R_h],
                                   sem="s_act")
                            if pt == 0 and ex == 0 and hm == 0:
                                pass
                        p.step("tensor",
                               lambda e, ex=ex:
                               e.matmul(bank(6), sel6[0:128,
                                                      ts(ex, 128)],
                                        comb[0:128, :], start=True,
                                        stop=True),
                               reads=[R_comb, R_c], writes=[R_bank[6]],
                               sem="s_mm")
                        cbf = s2[:, 0:1024].bitcast(F32)
                        p.step("scalar",
                               lambda e, cbf=cbf: e.activation(cbf, bank(6),
                                                               AF.Copy),
                               reads=[R_bank[6]],
                               writes=[R_s2[0], R_s2[1]],
                               sem="s_act")
                        if pt == 0 and ex == 0:
                            pass
                        for m2 in range(DCH):
                            sl_ = m2 % 2
                            p.step("sync",
                                   lambda e, ex=ex, m2=m2, sl_=sl_:
                                   e.dma_start(out=w2b[:, ts(sl_, 2048)],
                                               in_=w2c.ap()[ex, m2, :, :]),
                                   reads=[R_wc],
                                   writes=[R_w2[sl_]], sem="s_ld", k=16)
                            for hc in range(16):
                                p.step("tensor",
                                       lambda e, m2=m2, hc=hc, sl_=sl_:
                                       e.matmul(bank(4 + sl_),
                                                w2b[:, sl_ * 2048 + hc * 128:
                                                    sl_ * 2048 + hc * 128
                                                    + 128],
                                                hbuf[:, ts(hc, W)],
                                                start=(hc == 0),
                                                stop=(hc == 15)),
                                       reads=[R_h, R_w2[sl_]],
                                       writes=[R_bank[4 + sl_]], sem="s_mm")
                            t4 = s2[:, 2048:3072].bitcast(F32)
                            p.step("vector",
                                   lambda e, ex=ex, m2=m2, sl_=sl_, t4=t4,
                                   cbf=cbf:
                                   e.scalar_tensor_tensor(
                                       t4, bank(4 + sl_),
                                       b2p[:, ex * 8 + m2: ex * 8 + m2 + 1],
                                       cbf[:, 0:512], ALU.add, ALU.mult),
                                   reads=[R_bank[4 + sl_], R_s2[0], R_s2[1],
                                          R_c],
                                   writes=[R_s2[4], R_s2[5]], sem="s_vec")
                            if pt == 0 and ex == 0 and m2 == 0:
                                pass
                            p.step("vector",
                                   lambda e, m2=m2, base=base, t4=t4:
                                   e.tensor_tensor(
                                       x_res[:, m2 * TC + base:
                                             m2 * TC + base + W],
                                       x_res[:, m2 * TC + base:
                                             m2 * TC + base + W],
                                       t4, ALU.add),
                                   reads=[R_s2[4], R_s2[5], R_x],
                                   writes=[R_x], sem="s_vec")

                p.barrier()

        if upto not in ("p1a", "p1", "p2"):
            p.barrier()
            RG = {}

            def _regs(eng, sems):
                pid = eng.partition_id()
                par = eng.alloc_register()
                eng.reg_mod(par, pid, 2)
                RG["par"] = par
            p.raw("gpsimd", _regs)

            with contextlib.ExitStack() as p3s:
                E3 = p3s.enter_context
                bc_f = E3(nc.sbuf_tensor("bc_f", [32, 8192], BF16))
                nsw_s = E3(nc.sbuf_tensor("nsw_s", [128, 8], F32))
                inb_s = E3(nc.sbuf_tensor("inb_s", [128, 32], F32))
                cw_s = E3(nc.sbuf_tensor("cw_s", [128, 64], F32))
                cb_s = E3(nc.sbuf_tensor("cb_s", [128, 16], F32))
                dvc_s = E3(nc.sbuf_tensor("dvc_s", [128, 16], F32))
                otb_s = E3(nc.sbuf_tensor("otb_s", [128, 8], F32))
                dtb_s = E3(nc.sbuf_tensor("dtb_s", [128, 8], F32))
                a_sc = E3(nc.sbuf_tensor("a_sc", [128, 128], F32))
                dtw_s = E3(nc.sbuf_tensor("dtw_s", [64, 1024], BF16))
                R3c = Res("p3consts"); R_bc = Res("bc_f"); R_asc = Res()
                R_dtw = Res()
                R_hd = Res(); R_gd = Res(); R_hidd = Res(); R_dltd = Res()
                R_halo = [Res(), Res()]
                R_ccx = [Res(), Res()]; R_cch = [Res(), Res()]
                R_ccy = [Res(), Res()]
                R_xspill = Res()

                for tgt, src in [(nsw_s, nsw_h), (inb_s, inb_h),
                                 (cw_s, cwp_h), (cb_s, cbp_h),
                                 (dvc_s, dvc_h), (otb_s, otb_h),
                                 (dtb_s, dtb_h)]:
                    p.step("sync", lambda e, t=tgt, sr=src:
                           e.dma_start(out=t[:, :], in_=sr[:, :]),
                           writes=[R3c], sem="s_ld", k=16)
                p.step("sync", lambda e: e.dma_start(out=a_sc[:, :],
                                                     in_=alg_h[:, :]),
                       writes=[R_asc], sem="s_ld", k=16)
                p.step("scalar", lambda e: e.activation(a_sc[:, :], a_sc[:, :],
                                                        AF.Exp),
                       reads=[R_asc], writes=[R_asc], sem="s_act")
                p.step("vector", lambda e: e.tensor_scalar(
                    a_sc[:, :], a_sc[:, :], -1.0, None, ALU.mult),
                       reads=[R_asc], writes=[R_asc], sem="s_vec")
                p.step("gpsimd", lambda e: e.dma_start(out=dtw_s[:, :],
                                                       in_=dtw_h[:, :]),
                       writes=[R_dtw], sem="s_wl", k=16)

                with contextlib.ExitStack() as pa:
                    EA = pa.enter_context
                    mr3 = EA(nc.sbuf_tensor("mr3", [1, 1024], F32))
                    s3 = EA(nc.sbuf_tensor("s3", [128, 1024], BF16))
                    sxf = EA(nc.sbuf_tensor("sxf", [128, 1024], F32))
                    estg = EA(nc.sbuf_tensor("estg", [128, 1024], BF16))
                    cxs = EA(nc.sbuf_tensor("cxs", [128, 4096], BF16))
                    tsstg = EA(nc.sbuf_tensor("tsstg", [64, 1024], BF16))

                    R_mr3 = Res(); R_s3 = [Res(), Res()]
                    R_sxf = [Res(), Res()]; R_estg = [Res(), Res()]
                    R_tsf = Res()
                    R_xn3 = Res(); R_w3 = [Res(), Res()]
                    R_xpws = [Res(), Res()]
                    R_hbuf = [Res(), Res()]; R_cnv = [Res(), Res()]
                    R_cacc = Res()
                    xn3 = warena[:, 0:4096]
                    w3s = [warena[:, 4096:5120], warena[:, 5120:6144]]
                    xpws = [warena[:, 6144:6272], warena[:, 6272:6400]]
                    hbufs = [warena[:, 6400:8451], warena[:, 8451:10502]]
                    cnvs = [warena[:, 10502:12550], warena[:, 12550:14598]]
                    cacc = warena[:, 14598:16384].bitcast(F32)

                    def ln3(pt):
                        for dc in range(DCH):
                            sl = dc % 2
                            p.step("vector",
                                   lambda e, dc=dc, pt=pt, sl=sl:
                                   e.tensor_copy(s3[:, ts(sl, 512)],
                                                 x_res[:, dc * TC + pt * W:
                                                       dc * TC + pt * W + W]),
                                   reads=[R_x], writes=[R_s3[sl]],
                                   sem="s_vec", settle=True)
                            p.step("vector",
                                   lambda e, sl=sl:
                                   e.tensor_mul(estg[:, 0:512],
                                                s3[:, ts(sl, 512)],
                                                s3[:, ts(sl, 512)]),
                                   reads=[R_s3[sl]], writes=[R_estg[0]],
                                   sem="s_vec", settle=True)
                            p.step("tensor",
                                   lambda e, dc=dc, sl=sl:
                                   e.matmul(bank(6)[0:1, :], ones128b,
                                            s3[:, ts(sl, 512)],
                                            start=(dc == 0),
                                            stop=(dc == DCH - 1)),
                                   reads=[R_s3[sl], R_c], writes=[R_bank[6]],
                                   sem="s_mm")
                            p.step("tensor",
                                   lambda e, dc=dc:
                                   e.matmul(bank(7)[0:1, :], ones128b,
                                            estg[:, 0:512],
                                            start=(dc == 0),
                                            stop=(dc == DCH - 1)),
                                   reads=[R_estg[0], R_c],
                                   writes=[R_bank[7]], sem="s_mm")
                        p.step("vector",
                               lambda e: e.tensor_scalar(mr3[0:1, 0:512],
                                                         bank(6)[0:1, :],
                                                         1.0 / D, None,
                                                         ALU.mult),
                               reads=[R_bank[6]], writes=[R_mr3], sem="s_vec")
                        p.step("vector",
                               lambda e: e.tensor_scalar(mr3[0:1, 512:1024],
                                                         bank(7)[0:1, :],
                                                         1.0 / D, None,
                                                         ALU.mult),
                               reads=[R_bank[7]], writes=[R_mr3], sem="s_vec")
                        p.step("scalar",
                               lambda e: e.activation(bank(6)[0:1, :],
                                                      mr3[0:1, 0:512],
                                                      AF.Square),
                               reads=[R_mr3], writes=[R_bank[6]], sem="s_act")
                        p.step("vector",
                               lambda e: e.tensor_sub(mr3[0:1, 512:1024],
                                                      mr3[0:1, 512:1024],
                                                      bank(6)[0:1, :]),
                               reads=[R_mr3, R_bank[6]], writes=[R_mr3],
                               sem="s_vec")
                        p.step("scalar",
                               lambda e: e.activation(bank(6)[0:1, :],
                                                      mr3[0:1, 512:1024],
                                                      AF.Identity,
                                                      bias=cval[0:1, 0:1]),
                               reads=[R_mr3, R_c], writes=[R_bank[6]],
                               sem="s_act")
                        p.step("vector",
                               lambda e: e.scalar_tensor_tensor(
                                   mr3[0:1, 512:1024], bank(6)[0:1, :], 1e-6,
                                   mr3[0:1, 512:1024], ALU.mult, ALU.add),
                               reads=[R_mr3, R_bank[6]], writes=[R_mr3],
                               sem="s_vec", drain=True)
                        p.step("scalar",
                               lambda e: e.activation(mr3[0:1, 512:1024],
                                                      mr3[0:1, 512:1024],
                                                      AF.Ln),
                               reads=[R_mr3], writes=[R_mr3], sem="s_act")
                        p.step("scalar",
                               lambda e: e.activation(mr3[0:1, 512:1024],
                                                      mr3[0:1, 512:1024],
                                                      AF.Exp,
                                                      scale=cval[0:1, 2:3]),
                               reads=[R_mr3, R_c], writes=[R_mr3],
                               sem="s_act")
                        p.step("tensor",
                               lambda e: e.matmul(bank(6), ones1f[:, :],
                                                  mr3[0:1, 0:512], start=True,
                                                  stop=True),
                               reads=[R_mr3, R_c], writes=[R_bank[6]],
                               sem="s_mm")
                        p.step("tensor",
                               lambda e: e.matmul(bank(7), ones1f[:, :],
                                                  mr3[0:1, 512:1024],
                                                  start=True, stop=True),
                               reads=[R_mr3, R_c], writes=[R_bank[7]],
                               sem="s_mm")
                        for dc in range(DCH):
                            sl = dc % 2
                            p.step("vector",
                                   lambda e, dc=dc, pt=pt, sl=sl:
                                   e.tensor_sub(sxf[:, ts(sl, 512)],
                                                x_res[:, dc * TC + pt * W:
                                                      dc * TC + pt * W + W],
                                                bank(6)),
                                   reads=[R_x, R_bank[6]],
                                   writes=[R_sxf[sl]], sem="s_vec")
                            p.step("vector",
                                   lambda e, dc=dc, sl=sl:
                                   e.scalar_tensor_tensor(
                                       xn3[:, ts(dc, 512)],
                                       sxf[:, ts(sl, 512)],
                                       nsw_s[:, dc:dc + 1], bank(7),
                                       ALU.mult, ALU.mult),
                                   reads=[R_sxf[sl], R_bank[7], R3c],
                                   writes=[R_xn3], sem="s_vec", settle=True)

                    for pt in range(NT):
                        ln3(pt)
                        for oc in range(32):
                            sl = oc % 2
                            pb = oc % 4
                            p.step("sync",
                                   lambda e, oc=oc, sl=sl:
                                   e.dma_start(out=w3s[sl],
                                               in_=inwc.ap()[oc, :, :]),
                                   reads=[R_wc],
                                   writes=[R_w3[sl]], sem="s_ld", k=16)
                            for kc in range(DCH):
                                p.step("tensor",
                                       lambda e, kc=kc, pb=pb, sl=sl:
                                       e.matmul(bank(pb),
                                                w3s[sl][:, ts(kc, 128)],
                                                xn3[:, ts(kc, 512)],
                                                start=(kc == 0),
                                                stop=(kc == DCH - 1)),
                                       reads=[R_xn3, R_w3[sl]],
                                       writes=[R_bank[pb]], sem="s_mm")
                            if oc < 16:
                                p.step("scalar",
                                       lambda e, oc=oc, pb=pb, sl=sl:
                                       e.activation(estg[:, ts(sl, 512)],
                                                    bank(pb), AF.Identity,
                                                    bias=inb_s[:, oc:oc + 1]),
                                       reads=[R_bank[pb], R3c],
                                       writes=[R_estg[sl]], sem="s_act")
                                p.step("sync",
                                       lambda e, oc=oc, pt=pt, sl=sl:
                                       e.dma_start(
                                           out=hid_d.ap()[:, oc * TC + pt * W:
                                                          oc * TC + pt * W
                                                          + W],
                                           in_=estg[:, ts(sl, 512)]),
                                       reads=[R_estg[sl]], writes=[R_hidd],
                                       sem="s_st", k=16)
                                if pt == NT - 1:
                                    p.step("sync",
                                           lambda e, oc=oc, sl=sl:
                                           e.dma_start(
                                               out=halo_i.ap()[:, oc * 3:
                                                               oc * 3 + 3],
                                               in_=estg[:, sl * 512 + 509:
                                                        sl * 512 + 512]),
                                           reads=[R_estg[sl]],
                                           writes=[R_halo[0]],
                                           sem="s_st", k=16)
                            else:
                                p.step("scalar",
                                       lambda e, oc=oc, pb=pb, sl=sl:
                                       e.activation(estg[:, ts(sl, 512)],
                                                    bank(pb), AF.Silu,
                                                    bias=inb_s[:, oc:oc + 1]),
                                       reads=[R_bank[pb], R3c],
                                       writes=[R_estg[sl]], sem="s_act")
                                p.step("sync",
                                       lambda e, oc=oc, pt=pt, sl=sl:
                                       e.dma_start(
                                           out=g_d.ap()[:,
                                                        (oc - 16) * TC
                                                        + pt * W:
                                                        (oc - 16) * TC
                                                        + pt * W + W],
                                           in_=estg[:, ts(sl, 512)]),
                                       reads=[R_estg[sl]], writes=[R_gd],
                                       sem="s_st", k=16)

                    p.step("gpsimd",
                           lambda e: e.collective_compute(
                               "AllGather", ALU.bypass, replica_groups=REPL,
                               ins=[halo_i.ap().opt()],
                               outs=[halo_o.ap().opt()]),
                           reads=[R_halo[0]], writes=[R_halo[1]],
                           sem="s_cc", k=1)

                    for th in range(2):
                        for ic in range(16):
                            sl = ic % 2
                            if th == 0:
                                def _pref(e, sems, tgt, ic=ic, sl=sl):
                                    with e.If_eq(RG["par"], 0):
                                        e.memset(hbufs[sl][:, 0:3],
                                                 0.0).then_inc(
                                            sems["s_wl"], 16)
                                    with e.Else():
                                        e.dma_start(
                                            out=hbufs[sl][:, 0:3],
                                            in_=halo_o.ap()[0:128, ic * 3:
                                                            ic * 3 + 3]
                                        ).then_inc(sems["s_wl"], 16)
                                    e.end_ifs()
                                p.step2("gpsimd", _pref,
                                        reads=[R_halo[1]],
                                        writes=[R_hbuf[sl]],
                                        sem="s_wl", k=16)
                            else:
                                p.step("sync",
                                       lambda e, ic=ic, sl=sl:
                                       e.dma_start(
                                           out=hbufs[sl][:, 0:3],
                                           in_=hid_d.ap()[:, ic * TC + 2045:
                                                          ic * TC + 2048]),
                                       reads=[R_hidd], writes=[R_hbuf[sl]],
                                       sem="s_ld", k=16)
                            p.step("sync",
                                   lambda e, ic=ic, th=th, sl=sl:
                                   e.dma_start(
                                       out=hbufs[sl][:, 3:2051],
                                       in_=hid_d.ap()[:, ic * TC + th * 2048:
                                                      ic * TC + th * 2048
                                                      + 2048]),
                                   reads=[R_hidd], writes=[R_hbuf[sl]],
                                   sem="s_ld", k=16)
                            p.step("vector",
                                   lambda e, ic=ic, sl=sl:
                                   e.tensor_scalar(cacc[:, 0:512],
                                                   hbufs[sl][:, 0:512],
                                                   cw_s[:, ic:ic + 1], None,
                                                   ALU.mult),
                                   reads=[R_hbuf[sl], R3c], writes=[R_cacc],
                                   sem="s_vec")
                            for q in range(4):
                                if q > 0:
                                    p.step("vector",
                                           lambda e, ic=ic, sl=sl, q=q:
                                           e.tensor_scalar(
                                               cacc[:, 0:512],
                                               hbufs[sl][:, q * 512:
                                                         q * 512 + 512],
                                               cw_s[:, ic:ic + 1], None,
                                               ALU.mult),
                                           reads=[R_hbuf[sl], R3c],
                                           writes=[R_cacc], sem="s_vec")
                                for k in range(1, 4):
                                    p.step("vector",
                                           lambda e, ic=ic, sl=sl, q=q, k=k:
                                           e.scalar_tensor_tensor(
                                               cacc[:, 0:512],
                                               hbufs[sl][:, q * 512 + k:
                                                         q * 512 + k + 512],
                                               cw_s[:, k * 16 + ic:
                                                    k * 16 + ic + 1],
                                               cacc[:, 0:512], ALU.mult,
                                               ALU.add),
                                           reads=[R_hbuf[sl], R3c],
                                           writes=[R_cacc], sem="s_vec",
                                           drain=True)
                                p.step("scalar",
                                       lambda e, ic=ic, sl=sl, q=q:
                                       e.activation(
                                           cnvs[sl][:, q * 512:
                                                    q * 512 + 512],
                                           cacc[:, 0:512], AF.Silu,
                                           bias=cb_s[:, ic:ic + 1]),
                                       reads=[R_cacc, R3c],
                                       writes=[R_cnv[sl]], sem="s_act")
                            p.step("sync",
                                   lambda e, ic=ic, th=th, sl=sl:
                                   e.dma_start(
                                       out=h_d.ap()[:, ic * TC + th * 2048:
                                                    ic * TC + th * 2048
                                                    + 2048],
                                       in_=cnvs[sl][:, 0:2048]),
                                   reads=[R_cnv[sl]], writes=[R_hd],
                                   sem="s_st", k=16)
                            if ic >= 8:
                                p.step("sync",
                                       lambda e, ic=ic, th=th, sl=sl:
                                       e.dma_start(
                                           out=cch_i.ap()[:,
                                                          (ic - 8) * TC
                                                          + th * 2048:
                                                          (ic - 8) * TC
                                                          + th * 2048 + 2048],
                                           in_=cnvs[sl][:, 0:2048]),
                                       reads=[R_cnv[sl]], writes=[R_cch[0]],
                                       sem="s_st", k=16)
                            p.step("gpsimd",
                                   lambda e, ic=ic, sl=sl:
                                   e.dma_start(out=xpws[sl][:, 0:96],
                                               in_=xpw_h[ic, :, :]),
                                   writes=[R_xpws[sl]], sem="s_wl", k=16)
                            for q in range(4):
                                p.step("tensor",
                                       lambda e, ic=ic, sl=sl, q=q:
                                       e.matmul(bank(4 + q)[0:96, :],
                                                xpws[sl][:, 0:96],
                                                cnvs[sl][:, q * 512:
                                                         q * 512 + 512],
                                                start=(ic == 0),
                                                stop=(ic == 15)),
                                       reads=[R_cnv[sl], R_xpws[sl]],
                                       writes=[R_bank[4 + q]], sem="s_mm")
                        for q in range(4):
                            p.step("scalar",
                                   lambda e, th=th, q=q:
                                   e.activation(
                                       cxs[0:64, th * 2048 + q * 512:
                                           th * 2048 + q * 512 + 512],
                                       bank(4 + q)[0:64, :], AF.Copy),
                                   reads=[R_bank[4 + q]], writes=[R_tsf],
                                   sem="s_act")
                            p.step("scalar",
                                   lambda e, th=th, q=q:
                                   e.activation(
                                       cxs[64:96, th * 2048 + q * 512:
                                           th * 2048 + q * 512 + 512],
                                       bank(4 + q)[64:96, :], AF.Copy),
                                   reads=[R_bank[4 + q]], writes=[R_tsf],
                                   sem="s_act")
                    p.step("sync",
                           lambda e: e.dma_start(out=ccx_i.ap()[:, :],
                                                 in_=cxs[:, :]),
                           reads=[R_tsf], writes=[R_ccx[0]], sem="s_st",
                           k=16)
                    p.step("gpsimd",
                           lambda e: e.collective_compute(
                               "AllGather", ALU.bypass, replica_groups=REPL,
                               ins=[ccx_i.ap().opt()],
                               outs=[ccx_o.ap().opt()]),
                           reads=[R_ccx[0]], writes=[R_ccx[1]], sem="s_cc",
                           k=1)
                    p.step("gpsimd",
                           lambda e: e.collective_compute(
                               "AllGather", ALU.bypass, replica_groups=REPL,
                               ins=[cch_i.ap().opt()],
                               outs=[cch_o.ap().opt()]),
                           reads=[R_cch[0]], writes=[R_cch[1]], sem="s_cc",
                           k=1)
                    p.step("sync",
                           lambda e: e.dma_start(out=bc_f[0:32, 0:4096],
                                                 in_=ccx_o.ap()[64:96, :]),
                           reads=[R_ccx[1]], writes=[R_bc], sem="s_ld",
                           k=16)
                    p.step("sync",
                           lambda e: e.dma_start(out=bc_f[0:32, 4096:8192],
                                                 in_=ccx_o.ap()[192:224, :]),
                           reads=[R_ccx[1]], writes=[R_bc], sem="s_ld",
                           k=16)
                    R_tss = [Res(), Res()]
                    for t16 in range(16):
                        tsl = t16 % 2
                        src = (ccx_o.ap()[0:64, ts(t16, 512)]
                               if t16 < 8 else
                               ccx_o.ap()[128:192, ts(t16 - 8, 512)])
                        p.step("sync",
                               lambda e, tsl=tsl, src=src:
                               e.dma_start(out=tsstg[0:64, ts(tsl, 512)],
                                           in_=src),
                               reads=[R_ccx[1]], writes=[R_tss[tsl]],
                               sem="s_ld", k=16)
                        for m in range(DCH):
                            pb = m % 2
                            sl = m % 2
                            p.step("tensor",
                                   lambda e, m=m, pb=pb, tsl=tsl:
                                   e.matmul(bank(pb),
                                            dtw_s[0:64, ts(m, 128)],
                                            tsstg[0:64, ts(tsl, 512)],
                                            start=True, stop=True),
                                   reads=[R_tss[tsl], R_dtw],
                                   writes=[R_bank[pb]], sem="s_mm")
                            p.step("scalar",
                                   lambda e, m=m, pb=pb:
                                   e.activation(sxf[:, 512:1024],
                                                bank(pb), AF.Exp,
                                                bias=dtb_s[:, m:m + 1]),
                                   reads=[R_bank[pb], R3c], writes=[R_sxf[1]],
                                   sem="s_act")
                            p.step("scalar",
                                   lambda e, sl=sl:
                                   e.activation(estg[:, ts(sl, 512)],
                                                sxf[:, 512:1024], AF.Ln,
                                                bias=cval[:, 4:5]),
                                   reads=[R_sxf[1], R_c],
                                   writes=[R_estg[sl]],
                                   sem="s_act")
                            p.step("sync",
                                   lambda e, m=m, t16=t16, sl=sl:
                                   e.dma_start(
                                       out=dlt_d.ap()[:, m * 8192 + t16 * W:
                                                      m * 8192 + t16 * W
                                                      + W],
                                       in_=estg[:, ts(sl, 512)]),
                                   reads=[R_estg[sl]], writes=[R_dltd],
                                   sem="s_st", k=16)
                    for i in range(8):
                        p.step("sync",
                               lambda e, i=i:
                               e.dma_start(out=xres_d.ap()[:, ts(i, 4096)],
                                           in_=x_res[:, ts(i, 4096)]),
                               reads=[R_x], writes=[R_xspill], sem="s_st",
                               k=16)

                p.barrier()

                with contextlib.ExitStack() as pb3:
                    EB = pb3.enter_context
                    hst = EB(nc.sbuf_tensor("hst", [128, 4096], BF16))
                    sel32_s = EB(nc.sbuf_tensor("sel32_s", [32, 4096],
                                                BF16))
                    R_s32 = Res()
                    p.step("gpsimd",
                           lambda e: e.dma_start(out=sel32_s[:, :],
                                                 in_=sel32_h[:, :]),
                           writes=[R_s32], sem="s_wl", k=16)
                    car = [EB(nc.sbuf_tensor(f"car{i}", [128, 1], F32))
                           for i in range(128)]
                    R_car = [Res() for _ in range(128)]
                    R_hst = [Res(), Res()]
                    R_ya = [Res() for _ in range(8)]
                    R_zs = [Res() for _ in range(8)]
                    R_ds = [Res() for _ in range(8)]
                    R_a3 = [Res(), Res()]; R_u3 = [Res(), Res()]
                    R_Bb = [Res(), Res()]; R_Cb = [Res(), Res()]
                    yav = x_res[:, 0:8192].bitcast(BF16)
                    ya = [yav[:, m * 2048:(m + 1) * 2048] for m in range(8)]
                    zv = x_res[:, 16384:24576].bitcast(BF16)
                    zs = [zv[:, m * 2048:(m + 1) * 2048] for m in range(8)]
                    dvv = x_res[:, 24576:32768].bitcast(BF16)
                    ds3 = [dvv[:, m * 2048:(m + 1) * 2048] for m in range(8)]
                    asl = [warena[:, 0:2048], warena[:, 2048:4096]]
                    usl = [warena[:, 4096:6144], warena[:, 6144:8192]]
                    Bbs = [warena[:, 8192:10240], warena[:, 10240:12288]]
                    Cbs = [warena[:, 12288:14336], warena[:, 14336:16384]]
                    for i in range(128):
                        p.step("vector",
                               lambda e, i=i: e.memset(car[i][:, :], 0.0),
                               writes=[R_car[i]], sem="s_vec")
                    for tt in range(4):
                        for m in range(8):
                            msl = m % 2

                            def _hload(e, sems, tgt, tt=tt, m=m, msl=msl):
                                lo = m * TC + (tt % 2) * 2048
                                loc = h_d.ap()[:, lo:lo + 2048]
                                if tt < 2:
                                    rem = cch_o.ap()[0:128, lo:lo + 2048]
                                    s0, s1 = loc, rem
                                else:
                                    rem = cch_o.ap()[128:256, lo:lo + 2048]
                                    s0, s1 = rem, loc
                                dst = hst[:, msl * 2048:msl * 2048 + 2048]
                                with e.If_eq(RG["par"], 0):
                                    e.dma_start(out=dst, in_=s0).then_inc(
                                        sems["s_wl"], 16)
                                with e.Else():
                                    e.dma_start(out=dst, in_=s1).then_inc(
                                        sems["s_wl"], 16)
                                e.end_ifs()
                            p.step2("gpsimd", _hload,
                                    reads=[R_hd, R_cch[1]],
                                    writes=[R_hst[msl]], sem="s_wl", k=16)
                            p.step("sync",
                                   lambda e, tt=tt, m=m:
                                   e.dma_start(
                                       out=ds3[m],
                                       in_=dlt_d.ap()[:,
                                                      m * 8192 + tt * 2048:
                                                      m * 8192 + tt * 2048
                                                      + 2048]),
                                   reads=[R_dltd], writes=[R_ds[m]],
                                   sem="s_ld", k=16)
                            p.step("vector",
                                   lambda e, m=m, msl=msl:
                                   e.tensor_mul(zs[m], ds3[m],
                                                hst[:, msl * 2048:
                                                    msl * 2048 + 2048]),
                                   reads=[R_ds[m], R_hst[msl]],
                                   writes=[R_zs[m]], sem="s_vec")
                        for n in range(16):
                            nb = n % 2
                            for q in range(4):
                                p.step("tensor",
                                       lambda e, tt=tt, n=n, q=q:
                                       e.matmul(bank(q),
                                                sel32_s[0:32, ts(n, 128)],
                                                bc_f[0:32,
                                                     tt * 2048 + q * 512:
                                                     tt * 2048 + q * 512
                                                     + 512],
                                                start=True, stop=True),
                                       reads=[R_bc, R_s32],
                                       writes=[R_bank[q]], sem="s_mm")
                                p.step("tensor",
                                       lambda e, tt=tt, n=n, q=q:
                                       e.matmul(bank(4 + q),
                                                sel32_s[0:32,
                                                        ts(16 + n, 128)],
                                                bc_f[0:32,
                                                     tt * 2048 + q * 512:
                                                     tt * 2048 + q * 512
                                                     + 512],
                                                start=True, stop=True),
                                       reads=[R_bc, R_s32],
                                       writes=[R_bank[4 + q]], sem="s_mm")
                            for q in range(4):
                                p.step("scalar",
                                       lambda e, nb=nb, q=q:
                                       e.activation(
                                           Bbs[nb][:, q * 512:q * 512 + 512],
                                           bank(q), AF.Copy),
                                       reads=[R_bank[q]], writes=[R_Bb[nb]],
                                       sem="s_act")
                                p.step("scalar",
                                       lambda e, nb=nb, q=q:
                                       e.activation(
                                           Cbs[nb][:, q * 512:q * 512 + 512],
                                           bank(4 + q), AF.Copy),
                                       reads=[R_bank[4 + q]],
                                       writes=[R_Cb[nb]], sem="s_act")
                            for m in range(8):
                                sl = (n * 8 + m) % 2
                                ci = n * 8 + m
                                p.step("scalar",
                                       lambda e, m=m, n=n, sl=sl:
                                       e.activation(asl[sl], ds3[m], AF.Exp,
                                                    scale=a_sc[:,
                                                               m * 16 + n:
                                                               m * 16 + n
                                                               + 1]),
                                       reads=[R_ds[m], R_asc],
                                       writes=[R_a3[sl]], sem="s_act")
                                p.step("vector",
                                       lambda e, m=m, sl=sl, nb=nb:
                                       e.tensor_mul(usl[sl], zs[m],
                                                    Bbs[nb]),
                                       reads=[R_zs[m], R_Bb[nb]],
                                       writes=[R_u3[sl]], sem="s_vec")
                                p.step("vector",
                                       lambda e, sl=sl, ci=ci:
                                       e.tensor_tensor_scan(
                                           usl[sl], asl[sl], usl[sl],
                                           car[ci][:, 0:1], ALU.mult,
                                           ALU.add),
                                       reads=[R_a3[sl], R_car[ci]],
                                       writes=[R_u3[sl]], sem="s_vec",
                                       drain=True)
                                p.step("scalar",
                                       lambda e, sl=sl, ci=ci:
                                       e.activation(car[ci][:, 0:1],
                                                    usl[sl][:, 2047:2048],
                                                    AF.Copy),
                                       reads=[R_u3[sl]], writes=[R_car[ci]],
                                       sem="s_act")
                                p.step("vector",
                                       lambda e, sl=sl, nb=nb:
                                       e.tensor_mul(
                                           hst[:, sl * 2048:sl * 2048
                                               + 2048],
                                           usl[sl], Cbs[nb]),
                                       reads=[R_u3[sl], R_Cb[nb]],
                                       writes=[R_hst[sl]], sem="s_vec")
                                if n == 0:
                                    p.step("vector",
                                           lambda e, m=m, sl=sl:
                                           e.tensor_copy(
                                               ya[m],
                                               hst[:, sl * 2048:sl * 2048
                                                   + 2048]),
                                           reads=[R_hst[sl]],
                                           writes=[R_ya[m]], sem="s_vec")
                                else:
                                    p.step("vector",
                                           lambda e, m=m, sl=sl:
                                           e.tensor_tensor(
                                               ya[m], ya[m],
                                               hst[:, sl * 2048:sl * 2048
                                                   + 2048], ALU.add),
                                           reads=[R_hst[sl], R_ya[m]],
                                           writes=[R_ya[m]], sem="s_vec")
                        for m in range(8):
                            def _ystore(e, sems, tgt, tt=tt, m=m):
                                loc = y_d.ap()[:, m * TC + (tt % 2) * 2048:
                                               m * TC + (tt % 2) * 2048
                                               + 2048]
                                rem = ccy_i.ap()[:,
                                                 m * TC + (tt % 2) * 2048:
                                                 m * TC + (tt % 2) * 2048
                                                 + 2048]
                                with e.If_eq(RG["par"], 0):
                                    e.dma_start(
                                        out=(loc if tt < 2 else rem),
                                        in_=ya[m]).then_inc(sems["s_st"], 16)
                                with e.Else():
                                    e.dma_start(
                                        out=(rem if tt < 2 else loc),
                                        in_=ya[m]).then_inc(sems["s_st"], 16)
                                e.end_ifs()
                            p.step2("gpsimd", _ystore, reads=[R_ya[m]],
                                    writes=[R_ccy[0]], sem="s_st", k=16)
                    p.step("gpsimd",
                           lambda e: e.collective_compute(
                               "AllGather", ALU.bypass, replica_groups=REPL,
                               ins=[ccy_i.ap().opt()],
                               outs=[ccy_o.ap().opt()]),
                           reads=[R_ccy[0]], writes=[R_ccy[1]], sem="s_cc",
                           k=1)

                p.barrier()

                with contextlib.ExitStack() as pc3:
                    EC = pc3.enter_context
                    ssm_s = EC(nc.sbuf_tensor("ssm_s", [128, 8192], BF16))
                    ystg = EC(nc.sbuf_tensor("ystg", [128, 1024], BF16))
                    hstg2 = EC(nc.sbuf_tensor("hstg2", [128, 1024], BF16))
                    gstg2 = EC(nc.sbuf_tensor("gstg2", [128, 1024], BF16))
                    R_ssm = Res(); R_ystg = [Res(), Res()]
                    R_hstg2 = [Res(), Res()]; R_gstg2 = [Res(), Res()]
                    R_otw = [Res(), Res()]
                    otws = [warena[:, 0:1024], warena[:, 1024:2048]]
                    for i in range(8):
                        p.step("sync",
                               lambda e, i=i:
                               e.dma_start(out=x_res[:, ts(i, 4096)],
                                           in_=xres_d.ap()[:, ts(i, 4096)]),
                               reads=[R_xspill], writes=[R_x], sem="s_ld",
                               k=16)
                    for pt in range(NT):
                        for ic in range(16):
                            sl = ic % 2
                            if ic < 8:
                                p.step("sync",
                                       lambda e, ic=ic, pt=pt, sl=sl:
                                       e.dma_start(
                                           out=ystg[:, ts(sl, 512)],
                                           in_=y_d.ap()[:, ic * TC + pt * W:
                                                        ic * TC + pt * W
                                                        + W]),
                                       reads=[R_ccy[0]],
                                       writes=[R_ystg[sl]], sem="s_ld",
                                       k=16)
                            else:
                                def _yld(e, sems, tgt, ic=ic, pt=pt, sl=sl):
                                    r0 = ccy_o.ap()[0:128,
                                                    (ic - 8) * TC + pt * W:
                                                    (ic - 8) * TC + pt * W
                                                    + W]
                                    r1 = ccy_o.ap()[128:256,
                                                    (ic - 8) * TC + pt * W:
                                                    (ic - 8) * TC + pt * W
                                                    + W]
                                    dst = ystg[:, sl * 512:sl * 512 + 512]
                                    with e.If_eq(RG["par"], 0):
                                        e.dma_start(out=dst, in_=r1
                                                    ).then_inc(
                                            sems["s_wl"], 16)
                                    with e.Else():
                                        e.dma_start(out=dst, in_=r0
                                                    ).then_inc(
                                            sems["s_wl"], 16)
                                    e.end_ifs()
                                p.step2("gpsimd", _yld, reads=[R_ccy[1]],
                                        writes=[R_ystg[sl]], sem="s_wl",
                                        k=16)
                            p.step("sync",
                                   lambda e, ic=ic, pt=pt, sl=sl:
                                   e.dma_start(
                                       out=hstg2[:, ts(sl, 512)],
                                       in_=h_d.ap()[:, ic * TC + pt * W:
                                                    ic * TC + pt * W + W]),
                                   reads=[R_hd], writes=[R_hstg2[sl]],
                                   sem="s_ld", k=16)
                            p.step("sync",
                                   lambda e, ic=ic, pt=pt, sl=sl:
                                   e.dma_start(
                                       out=gstg2[:, ts(sl, 512)],
                                       in_=g_d.ap()[:, ic * TC + pt * W:
                                                    ic * TC + pt * W + W]),
                                   reads=[R_gd], writes=[R_gstg2[sl]],
                                   sem="s_ld", k=16)
                            p.step("vector",
                                   lambda e, ic=ic, sl=sl:
                                   e.scalar_tensor_tensor(
                                       ystg[:, ts(sl, 512)],
                                       hstg2[:, ts(sl, 512)],
                                       dvc_s[:, ic:ic + 1],
                                       ystg[:, ts(sl, 512)], ALU.mult,
                                       ALU.add),
                                   reads=[R_hstg2[sl], R_ystg[sl], R3c],
                                   writes=[R_ystg[sl]], sem="s_vec",
                                   drain=True)
                            p.step("vector",
                                   lambda e, ic=ic, sl=sl:
                                   e.tensor_mul(ssm_s[:, ts(ic, 512)],
                                                ystg[:, ts(sl, 512)],
                                                gstg2[:, ts(sl, 512)]),
                                   reads=[R_ystg[sl], R_gstg2[sl]],
                                   writes=[R_ssm], sem="s_vec",
                                   settle=True)
                        for ic in range(16):
                            sl = ic % 2
                            p.step("sync",
                                   lambda e, ic=ic, sl=sl:
                                   e.dma_start(out=otws[sl],
                                               in_=otwc.ap()[ic, :, :]),
                                   reads=[R_wc],
                                   writes=[R_otw[sl]], sem="s_ld", k=16)
                            for mc in range(8):
                                p.step("tensor",
                                       lambda e, ic=ic, mc=mc, sl=sl:
                                       e.matmul(bank(mc),
                                                otws[sl][:, ts(mc, 128)],
                                                ssm_s[:, ts(ic, 512)],
                                                start=(ic == 0),
                                                stop=(ic == 15)),
                                       reads=[R_ssm, R_otw[sl]],
                                       writes=[R_bank[mc]], sem="s_mm")
                        for mc in range(8):
                            p.step("vector",
                                   lambda e, mc=mc, pt=pt:
                                   e.scalar_tensor_tensor(
                                       x_res[:, mc * TC + pt * W:
                                             mc * TC + pt * W + W],
                                       bank(mc), otb_s[:, mc:mc + 1],
                                       x_res[:, mc * TC + pt * W:
                                             mc * TC + pt * W + W],
                                       ALU.add, ALU.add),
                                   reads=[R_bank[mc], R3c, R_x],
                                   writes=[R_x], sem="s_vec")

                p.barrier()
        
        if upto not in ("p1a", "p1", "p2", "p3"):
            p.barrier()
            with contextlib.ExitStack() as p4s:
                E4 = p4s.enter_context
                h4 = E4(nc.sbuf_tensor("h4", [128, 12288], BF16))
                sg4 = E4(nc.sbuf_tensor("sg4", [128, 2048], BF16))
                mr4 = E4(nc.sbuf_tensor("mr4", [1, 1024], F32))
                s4 = E4(nc.sbuf_tensor("s4", [128, 1024], BF16))
                sxf4 = E4(nc.sbuf_tensor("sxf4", [128, 1024], F32))
                R_h4 = Res(); R_sg4 = [Res(), Res()]
                R_mr4 = Res(); R_s4 = [Res(), Res()]
                R_sxf4 = [Res(), Res()]
                R_wg4 = [Res(), Res()]; R_wu4 = [Res(), Res()]
                R_wd4 = [Res(), Res()]; R_xn4 = Res()
                xn4 = warena[:, 0:4096]
                wg4 = [warena[:, 4096:5120], warena[:, 5120:6144]]
                wu4 = [warena[:, 6144:7168], warena[:, 7168:8192]]
                wd4 = [warena[:, 8192:11264], warena[:, 11264:14336]]

                def ln4(pt):
                    for dc in range(DCH):
                        sl = dc % 2
                        p.step("vector",
                               lambda e, dc=dc, pt=pt, sl=sl:
                               e.tensor_copy(s4[:, ts(sl, 512)],
                                             x_res[:, dc * TC + pt * W:
                                                   dc * TC + pt * W + W]),
                               reads=[R_x], writes=[R_s4[sl]], sem="s_vec",
                               settle=True)
                        p.step("vector",
                               lambda e, sl=sl:
                               e.tensor_mul(sg4[:, 0:512],
                                            s4[:, ts(sl, 512)],
                                            s4[:, ts(sl, 512)]),
                               reads=[R_s4[sl]], writes=[R_sg4[0]],
                               sem="s_vec", settle=True)
                        p.step("tensor",
                               lambda e, dc=dc, sl=sl:
                               e.matmul(bank(6)[0:1, :], ones128b,
                                        s4[:, ts(sl, 512)],
                                        start=(dc == 0),
                                        stop=(dc == DCH - 1)),
                               reads=[R_s4[sl], R_c], writes=[R_bank[6]],
                               sem="s_mm")
                        p.step("tensor",
                               lambda e, dc=dc:
                               e.matmul(bank(7)[0:1, :], ones128b,
                                        sg4[:, 0:512],
                                        start=(dc == 0),
                                        stop=(dc == DCH - 1)),
                               reads=[R_sg4[0], R_c], writes=[R_bank[7]],
                               sem="s_mm")
                    p.step("vector",
                           lambda e: e.tensor_scalar(mr4[0:1, 0:512],
                                                     bank(6)[0:1, :],
                                                     1.0 / D, None,
                                                     ALU.mult),
                           reads=[R_bank[6]], writes=[R_mr4], sem="s_vec")
                    p.step("vector",
                           lambda e: e.tensor_scalar(mr4[0:1, 512:1024],
                                                     bank(7)[0:1, :],
                                                     1.0 / D, None,
                                                     ALU.mult),
                           reads=[R_bank[7]], writes=[R_mr4], sem="s_vec")
                    p.step("scalar",
                           lambda e: e.activation(bank(6)[0:1, :],
                                                  mr4[0:1, 0:512],
                                                  AF.Square),
                           reads=[R_mr4], writes=[R_bank[6]], sem="s_act")
                    p.step("vector",
                           lambda e: e.tensor_sub(mr4[0:1, 512:1024],
                                                  mr4[0:1, 512:1024],
                                                  bank(6)[0:1, :]),
                           reads=[R_mr4, R_bank[6]], writes=[R_mr4],
                           sem="s_vec")
                    p.step("scalar",
                           lambda e: e.activation(mr4[0:1, 512:1024],
                                                  mr4[0:1, 512:1024], AF.Ln,
                                                  bias=cval[0:1, 0:1]),
                           reads=[R_mr4, R_c], writes=[R_mr4], sem="s_act")
                    p.step("scalar",
                           lambda e: e.activation(mr4[0:1, 512:1024],
                                                  mr4[0:1, 512:1024], AF.Exp,
                                                  scale=cval[0:1, 2:3]),
                           reads=[R_mr4, R_c], writes=[R_mr4], sem="s_act")
                    p.step("tensor",
                           lambda e: e.matmul(bank(6), ones1f[:, :],
                                              mr4[0:1, 0:512], start=True,
                                              stop=True),
                           reads=[R_mr4, R_c], writes=[R_bank[6]],
                           sem="s_mm")
                    p.step("tensor",
                           lambda e: e.matmul(bank(7), ones1f[:, :],
                                              mr4[0:1, 512:1024], start=True,
                                              stop=True),
                           reads=[R_mr4, R_c], writes=[R_bank[7]],
                           sem="s_mm")
                    for dc in range(DCH):
                        sl = dc % 2
                        p.step("vector",
                               lambda e, dc=dc, pt=pt, sl=sl:
                               e.tensor_sub(sxf4[:, ts(sl, 512)],
                                            x_res[:, dc * TC + pt * W:
                                                  dc * TC + pt * W + W],
                                            bank(6)),
                               reads=[R_x, R_bank[6]], writes=[R_sxf4[sl]],
                               sem="s_vec")
                        p.step("vector",
                               lambda e, dc=dc, sl=sl:
                               e.tensor_mul(xn4[:, ts(dc, 512)],
                                            sxf4[:, ts(sl, 512)], bank(7)),
                               reads=[R_sxf4[sl], R_bank[7]],
                               writes=[R_xn4], sem="s_vec", settle=True)

                for pt in range(NT):
                    ln4(pt)
                    for hc in range(24):
                        sl = hc % 2
                        p.step("sync",
                               lambda e, hc=hc, sl=sl:
                               e.dma_start(out=wg4[sl],
                                           in_=wgtc.ap()[hc, :, :]),
                               reads=[R_wc],
                               writes=[R_wg4[sl]], sem="s_ld", k=16)
                        for kc in range(DCH):
                            p.step("tensor",
                                   lambda e, kc=kc, sl=sl:
                                   e.matmul(bank(sl),
                                            wg4[sl][:, ts(kc, 128)],
                                            xn4[:, ts(kc, 512)],
                                            start=(kc == 0),
                                            stop=(kc == DCH - 1)),
                                   reads=[R_xn4, R_wg4[sl]],
                                   writes=[R_bank[sl]], sem="s_mm")
                        p.step("scalar",
                               lambda e, sl=sl:
                               e.activation(sg4[:, sl * 512:sl * 512 + 512],
                                            bank(sl), AF.Silu,
                                            scale=cval[:, 5:6]),
                               reads=[R_bank[sl], R_c], writes=[R_sg4[sl]],
                               sem="s_act")
                        p.step("sync",
                               lambda e, hc=hc, sl=sl:
                               e.dma_start(out=wu4[sl],
                                           in_=wutc.ap()[hc, :, :]),
                               reads=[R_wc],
                               writes=[R_wu4[sl]], sem="s_ld", k=16)
                        for kc in range(DCH):
                            p.step("tensor",
                                   lambda e, kc=kc, sl=sl:
                                   e.matmul(bank(2 + sl),
                                            wu4[sl][:, ts(kc, 128)],
                                            xn4[:, ts(kc, 512)],
                                            start=(kc == 0),
                                            stop=(kc == DCH - 1)),
                                   reads=[R_xn4, R_wu4[sl]],
                                   writes=[R_bank[2 + sl]], sem="s_mm")
                        p.step("scalar",
                               lambda e, sl=sl:
                               e.activation(sg4[:, 1024 + sl * 512:
                                                1024 + sl * 512 + 512],
                                            bank(2 + sl), AF.Identity,
                                            scale=cval[:, 5:6]),
                               reads=[R_bank[2 + sl], R_c],
                               writes=[R_sg4[sl]], sem="s_act")
                        p.step("vector",
                               lambda e, hc=hc, sl=sl:
                               e.tensor_mul(h4[:, ts(hc, 512)],
                                            sg4[:, sl * 512:sl * 512 + 512],
                                            sg4[:, 1024 + sl * 512:
                                                1024 + sl * 512 + 512]),
                               reads=[R_sg4[sl]], writes=[R_h4],
                               sem="s_vec", settle=True)
                    for mc in range(DCH):
                        sl = mc % 2
                        p.step("sync",
                               lambda e, mc=mc, sl=sl:
                               e.dma_start(out=wd4[sl],
                                           in_=wdtc.ap()[mc, :, :]),
                               reads=[R_wc],
                               writes=[R_wd4[sl]], sem="s_ld", k=16)
                        for kc in range(24):
                            p.step("tensor",
                                   lambda e, kc=kc, sl=sl:
                                   e.matmul(bank(4 + sl),
                                            wd4[sl][:, ts(kc, 128)],
                                            h4[:, ts(kc, 512)],
                                            start=(kc == 0),
                                            stop=(kc == 23)),
                                   reads=[R_h4, R_wd4[sl]],
                                   writes=[R_bank[4 + sl]], sem="s_mm")
                        p.step("vector",
                               lambda e, mc=mc, pt=pt, sl=sl:
                               e.scalar_tensor_tensor(
                                   x_res[:, mc * TC + pt * W:
                                         mc * TC + pt * W + W],
                                   bank(4 + sl), 1.0 / 127.0,
                                   x_res[:, mc * TC + pt * W:
                                         mc * TC + pt * W + W],
                                   ALU.mult, ALU.add),
                               reads=[R_bank[4 + sl], R_x], writes=[R_x],
                               sem="s_vec")

        p.step("sync", lambda e: e.dma_start(out=dbg[0:1, 0:8],
                                             in_=cval[0:1, 0:8]),
               reads=[R_c], sem="s_st", k=16)

        with nc.sbuf_tensor("xout", [128, 2048], F32) as xout:
            R_xout = [Res(), Res()]
            for r in range(TC // 128):
                buf = r % 2
                for dc in range(DCH):
                    slot = dc % 2
                    p.step("tensor",
                           lambda e, r=r, dc=dc, slot=slot:
                           e.transpose(bank(6 + slot, 128),
                                       x_res[:, dc * TC + r * 128:
                                             dc * TC + r * 128 + 128],
                                       id_f[:, :]),
                           reads=[R_x, R_c], writes=[R_bank[6 + slot]],
                           sem="s_mm")
                    p.step("vector",
                           lambda e, dc=dc, buf=buf, slot=slot:
                           e.tensor_copy(xout[:, buf * 1024 + dc * 128:
                                              buf * 1024 + dc * 128 + 128],
                                         bank(6 + slot, 128)),
                           reads=[R_bank[6 + slot]], writes=[R_xout[buf]],
                           sem="s_vec")
                p.step("sync",
                       lambda e, r=r, buf=buf:
                       e.dma_start(out=out[ts(r, 128), :],
                                   in_=xout[:, ts(buf, 1024)]),
                       reads=[R_xout[buf]], sem="s_st", k=16)
            p.raw("sync", lambda e, s: e.wait_ge(s["s_st"], p.cnt["s_st"]))

        for ename in Prog.ENGINES:
            eng_obj = {"tensor": block.tensor, "vector": block.vector,
                       "scalar": block.scalar, "gpsimd": block.gpsimd,
                       "sync": block.sync}[ename]

            def make(ename=ename):
                def _(eng):
                    for fn in p.streams[ename]:
                        fn(eng, sems)
                return _
            eng_obj(make())

    return nc


def make_consts():
    cst_f = np.zeros((128, 256), dtype=np.float32)
    cst_f[:, 0:128] = np.eye(128, dtype=np.float32)
    cst_f[0, 128:256] = 1.0
    cst_b = np.zeros((128, 128 + SELC_W), dtype=np.float32)
    cst_b[:, 0:128] = np.eye(128, dtype=np.float32)
    sel = np.zeros((128, SELC_W), dtype=np.float32)
    pidx = np.arange(128)
    for mc in range(8):
        for r in range(32):
            sel[r, SELF + mc * 128: SELF + (mc + 1) * 128] = \
                (r == 2 * mc + pidx // 64)
            sel[r, SELS + mc * 128: SELS + (mc + 1) * 128] = \
                (r == 16 + 2 * mc + pidx // 64)
        for m in range(16):
            sel[:, HSUM16 + mc * 16 + m] = (m == 2 * mc + pidx // 64)
    for r in range(2):
        sel[r, HSEL2: HSEL2 + 128] = (r == pidx // 64)
    sel[0, ONES1B: ONES1B + 128] = 1.0
    sel[:, ONES128] = 1.0
    cst_b[:, 128:] = sel
    return cst_f, cst_b


def kernel(**inputs):
    f32 = lambda k: np.ascontiguousarray(np.asarray(inputs[k], dtype=np.float32))
    x = f32("L0_attn_x")
    wsm = np.ascontiguousarray(np.concatenate(
        [f32("L0_attn_Wb"), np.zeros((D, 16), np.float32),
         f32("L0_attn_Wfd"), f32("L0_attn_Wsd")], axis=1))
    bsm = np.concatenate([np.zeros(32, np.float32),
                          f32("L0_attn_bfd").ravel(),
                          f32("L0_attn_bsd").ravel()]).reshape(64, 1)
    btf1 = np.tile(f32("L0_attn_btf1").ravel(), 2).reshape(64, 1)
    btf2 = f32("L0_attn_btf2").reshape(1, 1)
    cst_f, cst_b = make_consts()

    wg = np.zeros((D, 8), np.float32)
    wg[:, 0:6] = f32("L0_ffn_Wg")
    bg = np.zeros((8, 1), np.float32)
    bg[0:6, 0] = f32("L0_ffn_bg").ravel()
    w1m = np.ascontiguousarray(
        f32("L0_ffn_W1").reshape(6, 8, 128, 16, 128)
        .transpose(0, 3, 2, 1, 4).reshape(6, 16, 128, 1024))
    w2m = np.ascontiguousarray(
        f32("L0_ffn_W2").reshape(6, 16, 128, 8, 128)
        .transpose(0, 3, 2, 1, 4).reshape(6, 8, 128, 2048))
    b1p = np.ascontiguousarray(
        f32("L0_ffn_b1").reshape(6, 16, 128).transpose(2, 0, 1)
        .reshape(128, 96))
    b2p = np.ascontiguousarray(
        f32("L0_ffn_b2").reshape(6, 8, 128).transpose(2, 0, 1)
        .reshape(128, 48))
    sel6 = np.zeros((128, 768), np.float32)
    for ex in range(6):
        sel6[ex, ex * 128:(ex + 1) * 128] = 1.0

    in_w = f32("L1_attn_in_w"); in_b = f32("L1_attn_in_b")
    conv_w = f32("L1_attn_conv_w"); conv_b = f32("L1_attn_conv_b")
    xproj_w = f32("L1_attn_xproj_w")
    dt_w = f32("L1_attn_dt_w"); dt_b = f32("L1_attn_dt_b")
    A_log = f32("L1_attn_A_log"); Dv = f32("L1_attn_D")
    out_w = f32("L1_attn_out_w"); out_b = f32("L1_attn_out_b")
    nsc = f32("L1_attn_norm_scale")
    p3h = []
    for h in (0, 1):
        Pm = np.r_[h * 1024:(h + 1) * 1024, (1 - h) * 1024:(2 - h) * 1024]
        wperm = np.concatenate([in_w[:, :2048][:, Pm], in_w[:, 2048:][:, Pm]],
                               axis=1)
        inw_p = np.ascontiguousarray(
            wperm.reshape(1024 // 128, 128, 32, 128)
            .transpose(2, 1, 0, 3).reshape(32, 128, 1024))
        inb_c = np.concatenate([in_b[:2048][Pm], in_b[2048:][Pm]])
        inb_p = np.ascontiguousarray(inb_c.reshape(32, 128).T)
        cwP = conv_w[:, Pm]
        cwp_p = np.ascontiguousarray(
            cwP.reshape(4, 16, 128).transpose(2, 0, 1).reshape(128, 64))
        cbp_p = np.ascontiguousarray(conv_b[Pm].reshape(16, 128).T)
        xpw_p = np.ascontiguousarray(xproj_w[Pm, :].reshape(16, 128, 96))
        dtw_p = np.ascontiguousarray(dt_w[:, Pm[:1024]])
        dtb_p = np.ascontiguousarray(dt_b[Pm[:1024]].reshape(8, 128).T)
        alg_p = np.ascontiguousarray(
            A_log[Pm[:1024], :].reshape(8, 128, 16)
            .transpose(1, 0, 2).reshape(128, 128))
        dvc_p = np.ascontiguousarray(Dv[Pm].reshape(16, 128).T)
        otw_p = np.ascontiguousarray(out_w[Pm, :].reshape(16, 128, 1024))
        p3h.append(dict(inw=inw_p, inb=inb_p, cwp=cwp_p, cbp=cbp_p,
                        xpw=xpw_p, dtw=dtw_p, dtb=dtb_p, alg=alg_p,
                        dvc=dvc_p, otw=otw_p))
    otb_p = np.ascontiguousarray(out_b.reshape(8, 128).T)
    nsw_p = np.ascontiguousarray(nsc.reshape(8, 128).T)
    sel32 = np.zeros((32, 4096), np.float32)
    for n in range(32):
        sel32[n, n * 128:(n + 1) * 128] = 1.0

    wgT = np.ascontiguousarray(np.asarray(inputs["L1_ffn_w_gate"]).T)
    wuT = np.ascontiguousarray(np.asarray(inputs["L1_ffn_w_up"]).T)
    wdT = np.ascontiguousarray(np.asarray(inputs["L1_ffn_w_down"]).T)
    wgt_p = np.ascontiguousarray(
        wgT.reshape(8, 128, 24, 128).transpose(2, 1, 0, 3)
        .reshape(24, 128, 1024).astype(np.int32))
    wut_p = np.ascontiguousarray(
        wuT.reshape(8, 128, 24, 128).transpose(2, 1, 0, 3)
        .reshape(24, 128, 1024).astype(np.int32))
    wdt_p = np.ascontiguousarray(
        wdT.reshape(24, 128, 8, 128).transpose(2, 1, 0, 3)
        .reshape(8, 128, 3072).astype(np.int32))

    nc = build_program()
    common = {
        "cst_f": cst_f, "cst_b": cst_b,
        "wq": f32("L0_attn_Wq"), "wk": f32("L0_attn_Wk"),
        "wv": f32("L0_attn_Wv"), "wo": f32("L0_attn_Wo"),
        "wsm": wsm, "wtf1": f32("L0_attn_Wtf1"),
        "wtf2": f32("L0_attn_Wtf2").reshape(32, 1),
        "bsm": bsm, "btf1": btf1, "btf2": btf2,
        "wg": wg, "bg": bg, "w1m": w1m, "w2m": w2m,
        "b1p": b1p, "b2p": b2p, "sel6": sel6,
        "nsw": nsw_p, "otb": otb_p, "sel32": sel32,
        "wgt": wgt_p, "wut": wut_p, "wdt": wdt_p,
    }
    in_maps = []
    for c in range(8):
        b, h = c // 2, c % 2
        m = dict(common)
        m["x"] = np.ascontiguousarray(x[b, h * TC:(h + 1) * TC, :])
        m.update(p3h[h])
        in_maps.append(m)
    res = run_bass_kernel_spmd(nc, in_maps, core_ids=list(range(8)))
    y = np.empty((B, T, D), dtype=np.float32)
    kernel.dbg = [res.results[c].get("dbg") for c in range(8)]
    for c in range(8):
        b, h = c // 2, c % 2
        y[b, h * TC:(h + 1) * TC, :] = res.results[c]["out"]
    return y


# version 09
# speedup vs baseline: 1.0715x; 1.0715x over previous
import numpy as np
import ml_dtypes
import concourse.bass as bass
import concourse.mybir as mybir
from concourse.bass_utils import run_bass_kernel_spmd

F32 = mybir.dt.float32
BF16 = mybir.dt.bfloat16
I32 = mybir.dt.int32
AF = mybir.ActivationFunctionType
ALU = mybir.AluOpType
AX = mybir.AxisListType
BF = ml_dtypes.bfloat16

B, T, D = 4, 8192, 1024
TC = T // 2
H, DK = 16, 64
W = 512
NT = TC // W
DCH = D // 128
REPL = [[0, 1], [2, 3], [4, 5], [6, 7]]

UPTO = "p4"
NOFLUX = False

SEM_ENG = {"s_mm": "tensor", "s_vec": "vector", "s_act": "scalar",
           "s_gpc": "gpsimd"}
SEM_NAMES = ["s_mm", "s_vec", "s_act", "s_gpc", "s_ld", "s_wl", "s_st", "s_cc"]

SELF = 0
SELS = 1024
HSUM16 = 2048
HSEL2 = 2176
ONES1B = 2304
ONES128 = 2432
SELC_W = 2560


class Res:
    __slots__ = ("w", "rs", "name")

    def __init__(self, name=""):
        self.name = name
        self.w = None
        self.rs = []


class Prog:
    ENGINES = ("tensor", "vector", "scalar", "gpsimd", "sync")
    ELIDE = True

    def __init__(self):
        self.streams = {e: [] for e in self.ENGINES}
        self.cnt = {s: 0 for s in SEM_NAMES}
        self.waited = {e: {s: 0 for s in SEM_NAMES} for e in self.ENGINES}

    DEBUG_TAGS = set()

    JUNK = [None]

    def step(self, engine, build, reads=(), writes=(), sem=None, k=1,
             drain=False, settle=False, tag=None):
        waits = {}
        for r in reads:
            if r.w:
                s, v = r.w
                waits[s] = max(waits.get(s, 0), v)
        for wr in writes:
            if wr.w:
                s, v = wr.w
                waits[s] = max(waits.get(s, 0), v)
            for s, v in wr.rs:
                waits[s] = max(waits.get(s, 0), v)
        wd = (self.waited[engine] if Prog.ELIDE
              else {s: 0 for s in SEM_NAMES})
        wl = [(s, v) for s, v in waits.items()
              if SEM_ENG.get(s) != engine and v > wd[s]]
        for s, v in wl:
            wd[s] = v
        if tag is not None and tag in Prog.DEBUG_TAGS:
            print(f"[STEP {tag}] eng={engine} waits={wl} sem={sem}->"
                  f"{self.cnt.get(sem, 0) + k if sem else None}")
        tgt = None
        if sem is not None:
            self.cnt[sem] += k
            tgt = self.cnt[sem]

        def fn(eng, sems, wl=wl, build=build, sem=sem, k=k, drain=drain,
               settle=settle):
            for s, v in wl:
                eng.wait_ge(sems[s], v)
            instr = build(eng)
            if drain:
                eng.drain()
            if settle:
                instr = eng.memset(Prog.JUNK[0][0:1, 0:2], 0.0)
            if sem is not None:
                instr.then_inc(sems[sem], k)
        self.streams[engine].append(fn)
        if sem is not None:
            for wr in writes:
                wr.w = (sem, tgt)
                wr.rs = []
            for r in reads:
                r.rs.append((sem, tgt))
        return tgt

    def step2(self, engine, build, reads=(), writes=(), sem=None, k=1):
        waits = {}
        for r in reads:
            if r.w:
                sv, v = r.w
                waits[sv] = max(waits.get(sv, 0), v)
        for wr in writes:
            if wr.w:
                sv, v = wr.w
                waits[sv] = max(waits.get(sv, 0), v)
            for sv, v in wr.rs:
                waits[sv] = max(waits.get(sv, 0), v)
        wd = (self.waited[engine] if Prog.ELIDE
              else {sv: 0 for sv in SEM_NAMES})
        wl = [(sv, v) for sv, v in waits.items()
              if SEM_ENG.get(sv) != engine and v > wd[sv]]
        for sv, v in wl:
            wd[sv] = v
        tgt = None
        if sem is not None:
            self.cnt[sem] += k
            tgt = self.cnt[sem]

        def fn(eng, sems, wl=wl, build=build, tgt=tgt):
            for sv, v in wl:
                eng.wait_ge(sems[sv], v)
            build(eng, sems, tgt)
        self.streams[engine].append(fn)
        if sem is not None:
            for wr in writes:
                wr.w = (sem, tgt)
                wr.rs = []
            for r in reads:
                r.rs.append((sem, tgt))
        return tgt

    def barrier(self):
        snap = [(s, v) for s, v in self.cnt.items() if v > 0]
        for e in self.ENGINES:
            wd = (self.waited[e] if Prog.ELIDE
                  else {s: 0 for s in SEM_NAMES})
            mine = [(s, v) for s, v in snap
                    if SEM_ENG.get(s) != e and v > wd[s]]
            for s, v in mine:
                wd[s] = v

            def fn(eng, sems, mine=mine):
                for s, v in mine:
                    eng.wait_ge(sems[s], v)
            self.streams[e].append(fn)

    def raw(self, engine, fn):
        self.streams[engine].append(fn)


def ts(i, n):
    return slice(i * n, (i + 1) * n)


def build_program(upto=None):
    upto = upto or UPTO
    nc = bass.Bass("TRN2", target_bir_lowering=False)
    p = Prog()

    def din(name, shape, dt=F32):
        return nc.dram_tensor(name, shape, dt, kind="ExternalInput").ap()

    x_in = din("x", [TC, D])
    out = nc.dram_tensor("out", [TC, D], F32, kind="ExternalOutput").ap()
    cst_f = din("cst_f", [128, 256])
    cst_b = din("cst_b", [128, 128 + SELC_W])
    wq_h = din("wq", [D, D]); wk_h = din("wk", [D, D])
    wv_h = din("wv", [D, D]); wo_h = din("wo", [D, D])
    wsm_h = din("wsm", [D, 64])
    wtf1_h = din("wtf1", [DK, 32]); wtf2_h = din("wtf2", [32, 1])
    bsm_h = din("bsm", [64, 1]); btf1_h = din("btf1", [64, 1])
    btf2_h = din("btf2", [1, 1])
    wg_h = din("wg", [D, 8])
    w1m_h = din("w1m", [6, 16, 128, 1024])
    w2m_h = din("w2m", [6, 8, 128, 2048])
    b1p_h = din("b1p", [128, 96])
    b2p_h = din("b2p", [128, 48])
    bg_h = din("bg", [8, 1])
    sel6_h = din("sel6", [128, 768])
    I32 = mybir.dt.int32
    nsw_h = din("nsw", [128, 8])
    inw_h = din("inw", [32, 128, 1024])
    inb_h = din("inb", [128, 32])
    cwp_h = din("cwp", [128, 64])
    cbp_h = din("cbp", [128, 16])
    xpw_h = din("xpw", [16, 128, 96])
    dtw_h = din("dtw", [64, 1024])
    dtb_h = din("dtb", [128, 8])
    alg_h = din("alg", [128, 128])
    dvc_h = din("dvc", [128, 16])
    otw_h = din("otw", [16, 128, 1024])
    otb_h = din("otb", [128, 8])
    sel32_h = din("sel32", [32, 4096])
    wgt_h = din("wgt", [24, 128, 1024], I32)
    wut_h = din("wut", [24, 128, 1024], I32)
    wdt_h = din("wdt", [8, 128, 3072], I32)

    dbg = nc.dram_tensor("dbg", [128, 8704], F32, kind="ExternalOutput").ap()
    alog_d = nc.dram_tensor("alog_d", [NT, 2, TC], BF16)
    cc_in = nc.dram_tensor("cc_in", [128, 2056], F32)
    cc_out = nc.dram_tensor("cc_out", [256, 2056], F32)
    w1c = nc.dram_tensor("w1c", [6, 128, 16384], BF16)
    w2c = nc.dram_tensor("w2c", [6, 8, 128, 2048], BF16)
    inwc = nc.dram_tensor("inwc", [32, 128, 1024], BF16)
    otwc = nc.dram_tensor("otwc", [16, 128, 1024], BF16)
    wgtc = nc.dram_tensor("wgtc", [24, 128, 1024], BF16)
    wutc = nc.dram_tensor("wutc", [24, 128, 1024], BF16)
    wdtc = nc.dram_tensor("wdtc", [8, 128, 3072], BF16)
    hid_d = nc.dram_tensor("hid_d", [128, 65536], BF16)
    g_d = nc.dram_tensor("g_d", [128, 65536], BF16)
    h_d = nc.dram_tensor("h_d", [128, 65536], BF16)
    dlt_d = nc.dram_tensor("dlt_d", [128, 65536], BF16)
    y_d = nc.dram_tensor("y_d", [128, 32768], BF16)
    xres_d = nc.dram_tensor("xres_d", [128, 32768], F32)
    halo_i = nc.dram_tensor("halo_i", [128, 48], BF16)
    halo_o = nc.dram_tensor("halo_o", [256, 48], BF16)
    ccx_i = nc.dram_tensor("ccx_i", [128, 4096], BF16)
    ccx_o = nc.dram_tensor("ccx_o", [256, 4096], BF16)
    cch_im = [nc.dram_tensor(f"cch_i{m}", [128, 4096], BF16)
              for m in range(8)]
    cch_om = [nc.dram_tensor(f"cch_o{m}", [256, 4096], BF16)
              for m in range(8)]
    ccy_i = nc.dram_tensor("ccy_i", [128, 32768], BF16)
    ccy_o = nc.dram_tensor("ccy_o", [256, 32768], BF16)

    import contextlib
    with contextlib.ExitStack() as stack:
        E = stack.enter_context
        x_res = E(nc.sbuf_tensor("x_res", [128, DCH * TC], F32))
        warena = E(nc.sbuf_tensor("warena", [128, 16384], BF16))
        id_f = E(nc.sbuf_tensor("id_f", [128, 128], F32))
        id_b = E(nc.sbuf_tensor("id_b", [128, 128], BF16))
        selc = E(nc.sbuf_tensor("selc", [128, SELC_W], BF16))
        ones1f = E(nc.sbuf_tensor("ones1f", [1, 128], F32))
        cval = E(nc.sbuf_tensor("cval", [128, 8], F32))
        junk = E(nc.sbuf_tensor("junk", [1, 8], F32))
        ps = E(nc.psum_tensor("ps", [128, 4096], F32))
        sems = {s: E(nc.semaphore(s)) for s in SEM_NAMES}
        block = E(nc.Block())

        Prog.JUNK[0] = junk
        R_x = Res("x_res")
        R_bank = [Res(f"bank{i}") for i in range(8)]
        R_wa = [Res("wa0"), Res("wa1")]
        R_c = Res("const")
        R_mhat = Res("mhat")
        R_ccin = Res(); R_ccout = Res(); R_alog = Res()

        def bank(i, cols=512, off=0):
            return ps[:, i * 512 + off: i * 512 + off + cols]

        psb = ps[:, :].bitcast(BF16)

        def bankb(i, cols=512, off=0):
            return psb[:, i * 1024 + off: i * 1024 + off + cols]

        def sel16(mc):
            return selc[0:128, SELF + mc * 128: SELF + mc * 128 + 128]

        def selfw(mc):
            return selc[0:128, SELF + mc * 128: SELF + (mc + 1) * 128]

        def selsw(mc):
            return selc[0:128, SELS + mc * 128: SELS + (mc + 1) * 128]

        def hsum16(mc):
            return selc[0:128, HSUM16 + mc * 16: HSUM16 + (mc + 1) * 16]

        hsel2 = selc[0:128, HSEL2: HSEL2 + 128]
        ones128b = selc[0:128, ONES128: ONES128 + 1]

        for i, v in [(0, 1e-5), (1, 1e-6), (2, -0.5), (3, -1.0), (4, 1.0),
                     (5, 1.0 / 127.0)]:
            p.step("vector", lambda e, i=i, v=v: e.memset(cval[:, i:i + 1], v),
                   writes=[R_c], sem="s_vec")

        p.step("sync", lambda e: e.dma_start(out=id_f[:, :], in_=cst_f[:, 0:128]),
               writes=[R_c], sem="s_ld", k=16)
        p.step("sync", lambda e: e.dma_start(out=ones1f[:, :],
                                             in_=cst_f[0:1, 128:256]),
               writes=[R_c], sem="s_ld", k=16)
        p.step("gpsimd", lambda e: e.dma_start(out=id_b[:, :], in_=cst_b[:, 0:128]),
               writes=[R_c], sem="s_wl", k=16)
        p.step("gpsimd", lambda e: e.dma_start(out=selc[:, :],
                                               in_=cst_b[:, 128:128 + SELC_W]),
               writes=[R_c], sem="s_wl", k=16)

        def wload(slot, w_h, ncols=1024, nbands=DCH):
            for dc in range(nbands):
                p.step("gpsimd",
                       lambda e, dc=dc, slot=slot, w_h=w_h, ncols=ncols:
                       e.dma_start(out=warena[:, slot * 8192 + dc * ncols:
                                              slot * 8192 + (dc + 1) * ncols],
                                   in_=w_h[ts(dc, 128), :]),
                       writes=[R_wa[slot]], sem="s_wl", k=16)

        def wband(slot, dc, ncols=1024):
            return warena[:, slot * 8192 + dc * ncols:
                          slot * 8192 + (dc + 1) * ncols]

        with nc.sbuf_tensor("xin", [128, 2048], F32) as xin:
            R_xin = [Res(), Res()]
            for r in range(TC // 128):
                buf = r % 2
                p.step("sync",
                       lambda e, r=r, buf=buf:
                       e.dma_start(out=xin[:, ts(buf, 1024)],
                                   in_=x_in[ts(r, 128), :]),
                       writes=[R_xin[buf]], sem="s_ld", k=16)
                for dc in range(DCH):
                    slot = dc % 2
                    p.step("tensor",
                           lambda e, r=r, dc=dc, buf=buf, slot=slot:
                           e.transpose(bank(6 + slot, 128),
                                       xin[:, buf * 1024 + dc * 128:
                                           buf * 1024 + dc * 128 + 128],
                                       id_f[:, :]),
                           reads=[R_xin[buf], R_c], writes=[R_bank[6 + slot]],
                           sem="s_mm")
                    p.step("vector",
                           lambda e, r=r, dc=dc, slot=slot:
                           e.tensor_copy(
                               x_res[:, dc * TC + r * 128: dc * TC + r * 128 + 128],
                               bank(6 + slot, 128)),
                           reads=[R_bank[6 + slot]], writes=[R_x], sem="s_vec")

        wload(0, wk_h)
        wload(1, wv_h)
        p.barrier()
        R_wc = Res("wcast")
        for ex in range(6):
            for hm in range(16):
                p.step("gpsimd",
                       lambda e, ex=ex, hm=hm:
                       e.dma_start(out=w1c.ap()[ex, :,
                                                 hm * 1024:(hm + 1) * 1024],
                                   in_=w1m_h[ex, hm, :, :]),
                       writes=[R_wc], sem="s_wl", k=16)
            for m2 in range(8):
                p.step("gpsimd",
                       lambda e, ex=ex, m2=m2:
                       e.dma_start(out=w2c.ap()[ex, m2, :, :],
                                   in_=w2m_h[ex, m2, :, :]),
                       writes=[R_wc], sem="s_wl", k=16)
        for oc in range(32):
            p.step("gpsimd",
                   lambda e, oc=oc: e.dma_start(out=inwc.ap()[oc, :, :],
                                                in_=inw_h[oc, :, :]),
                   writes=[R_wc], sem="s_wl", k=16)
        for ic in range(16):
            p.step("gpsimd",
                   lambda e, ic=ic: e.dma_start(out=otwc.ap()[ic, :, :],
                                                in_=otw_h[ic, :, :]),
                   writes=[R_wc], sem="s_wl", k=16)
        for hc in range(24):
            p.step("gpsimd",
                   lambda e, hc=hc: e.dma_start(out=wgtc.ap()[hc, :, :],
                                                in_=wgt_h[hc, :, :]),
                   writes=[R_wc], sem="s_wl", k=16)
            p.step("gpsimd",
                   lambda e, hc=hc: e.dma_start(out=wutc.ap()[hc, :, :],
                                                in_=wut_h[hc, :, :]),
                   writes=[R_wc], sem="s_wl", k=16)
        for mc in range(8):
            p.step("gpsimd",
                   lambda e, mc=mc: e.dma_start(out=wdtc.ap()[mc, :, :],
                                                in_=wdt_h[mc, :, :]),
                   writes=[R_wc], sem="s_wl", k=16)


        with contextlib.ExitStack() as p1stack:
            E1 = p1stack.enter_context
            wsm_s = E1(nc.sbuf_tensor("wsm_s", [128, 512], BF16))
            wtf1_s = E1(nc.sbuf_tensor("wtf1_s", [128, 32], BF16))
            wtf2b_s = E1(nc.sbuf_tensor("wtf2b_s", [64, 2], BF16))
            bsm_s = E1(nc.sbuf_tensor("bsm_s", [64, 1], F32))
            btf1_s = E1(nc.sbuf_tensor("btf1_s", [64, 1], F32))
            btf2_s = E1(nc.sbuf_tensor("btf2_s", [2, 1], F32))
            mhat = E1(nc.sbuf_tensor("mhat", [128, 2048], BF16))
            xn = E1(nc.sbuf_tensor("xn", [128, 4096], BF16))
            kq_bf = E1(nc.sbuf_tensor("kq_bf", [128, 4096], BF16))
            mr = E1(nc.sbuf_tensor("mr", [1, 1024], F32))
            sm = E1(nc.sbuf_tensor("sm", [64, 512], F32))
            cums = E1(nc.sbuf_tensor("cums", [32, 512], F32))
            fwb = E1(nc.sbuf_tensor("fwb", [128, 512], BF16))
            sca = E1(nc.sbuf_tensor("sca", [32, 8], F32))
            rb16 = E1(nc.sbuf_tensor("rb16", [128, 1024], BF16))
            scrb = E1(nc.sbuf_tensor("scrb", [128, 2048], BF16))
            lrow = E1(nc.sbuf_tensor("lrow", [128, 1024], BF16))
            rk_s = E1(nc.sbuf_tensor("rk", [16, 512], F32))
            redT = E1(nc.sbuf_tensor("redT", [32, 1], F32))
            redS = E1(nc.sbuf_tensor("redS", [32, 1], F32))
            mstage = scrb[:, 0:1024].bitcast(F32)
            erow = cums[:, :].bitcast(BF16)[0:1, 0:576]
            R_xn = Res(); R_kq = Res(); R_mr = Res(); R_sm = Res()
            R_cums = Res(); R_fwb = Res(); R_sca = Res(); R_rk = Res()
            R_rb16 = Res(); R_scr = [Res(f"scr{i}") for i in range(8)]
            R_lrow = Res(); R_mst = Res(); R_erow = Res()

            def scr(i, cols=512, off=0):
                return scrb[:, i * 512 + off: i * 512 + off + cols]

            p.step("vector", lambda e: e.memset(sca[:, :], 0.0),
                   writes=[R_sca], sem="s_vec")
            p.step("vector", lambda e: e.memset(fwb[:, :], 0.0),
                   writes=[R_fwb], sem="s_vec")
            p.step("vector", lambda e: e.memset(rb16[:, :], 0.0),
                   writes=[R_rb16], sem="s_vec")
            p.step("vector", lambda e: e.memset(lrow[:, :], 0.0),
                   writes=[R_lrow], sem="s_vec")
            p.step("sync", lambda e: e.dma_start(out=bsm_s[:, :], in_=bsm_h[:, :]),
                   writes=[R_c], sem="s_ld", k=16)
            p.step("sync", lambda e: e.dma_start(out=btf1_s[:, :], in_=btf1_h[:, :]),
                   writes=[R_c], sem="s_ld", k=16)
            p.step("sync", lambda e: e.dma_start(out=btf2_s[0:1, :], in_=btf2_h[:, :]),
                   writes=[R_c], sem="s_ld", k=16)
            p.step("gpsimd", lambda e: e.dma_start(out=wtf1_s[0:64, :],
                                                   in_=wtf1_h[:, :]),
                   writes=[R_c], sem="s_wl", k=16)
            p.step("gpsimd", lambda e: e.dma_start(out=wtf1_s[64:128, :],
                                                   in_=wtf1_h[:, :]),
                   writes=[R_c], sem="s_wl", k=16)
            p.step("vector", lambda e: e.memset(wtf2b_s[:, :], 0.0),
                   writes=[R_c], sem="s_vec")
            for bnum in range(2):
                p.step("gpsimd",
                       lambda e, b=bnum: e.dma_start(
                           out=wtf2b_s[ts(b, 32), b:b + 1], in_=wtf2_h[:, :]),
                       writes=[R_c], sem="s_wl", k=16)
            for dc in range(DCH):
                p.step("gpsimd",
                       lambda e, dc=dc: e.dma_start(
                           out=wsm_s[:, ts(dc, 64)], in_=wsm_h[ts(dc, 128), :]),
                       writes=[R_c], sem="s_wl", k=16)
            p.step("sync", lambda e: e.dma_start(out=btf2_s[1:2, :],
                                                 in_=btf2_h[:, :]),
                   writes=[R_c], sem="s_ld", k=16)


            def emit_ln(tile):
                for dc in range(DCH):
                    p.step("vector",
                           lambda e, dc=dc, tile=tile:
                           e.tensor_copy(scr(0), x_res[:, dc * TC + tile * W:
                                                       dc * TC + tile * W + W]),
                           reads=[R_x], writes=[R_scr[0]], sem="s_vec",
                           settle=True)
                    p.step("vector",
                           lambda e: e.tensor_mul(scr(1), scr(0), scr(0)),
                           reads=[R_scr[0]], writes=[R_scr[1]], sem="s_vec",
                           settle=True)
                    p.step("tensor",
                           lambda e, dc=dc:
                           e.matmul(bank(6)[0:1, :], ones128b, scr(0),
                                    start=(dc == 0), stop=(dc == DCH - 1)),
                           reads=[R_scr[0], R_c], writes=[R_bank[6]], sem="s_mm")
                    p.step("tensor",
                           lambda e, dc=dc:
                           e.matmul(bank(7)[0:1, :], ones128b, scr(1),
                                    start=(dc == 0), stop=(dc == DCH - 1)),
                           reads=[R_scr[1], R_c], writes=[R_bank[7]], sem="s_mm")
                p.step("vector",
                       lambda e: e.tensor_scalar(mr[0:1, 0:512], bank(6)[0:1, :],
                                                 1.0 / D, None, ALU.mult),
                       reads=[R_bank[6]], writes=[R_mr], sem="s_vec")
                p.step("vector",
                       lambda e: e.tensor_scalar(mr[0:1, 512:1024],
                                                 bank(7)[0:1, :], 1.0 / D, None,
                                                 ALU.mult),
                       reads=[R_bank[7]], writes=[R_mr], sem="s_vec")
                p.step("scalar",
                       lambda e: e.activation(bank(6)[0:1, :], mr[0:1, 0:512],
                                              AF.Square),
                       reads=[R_mr], writes=[R_bank[6]], sem="s_act")
                p.step("vector",
                       lambda e: e.tensor_sub(mr[0:1, 512:1024],
                                              mr[0:1, 512:1024],
                                              bank(6)[0:1, :]),
                       reads=[R_mr, R_bank[6]], writes=[R_mr], sem="s_vec")
                p.step("scalar",
                       lambda e: e.activation(mr[0:1, 512:1024],
                                              mr[0:1, 512:1024], AF.Ln,
                                              bias=cval[0:1, 0:1]),
                       reads=[R_mr, R_c], writes=[R_mr], sem="s_act")
                p.step("scalar",
                       lambda e: e.activation(mr[0:1, 512:1024],
                                              mr[0:1, 512:1024], AF.Exp,
                                              scale=cval[0:1, 2:3]),
                       reads=[R_mr, R_c], writes=[R_mr], sem="s_act")
                p.step("tensor",
                       lambda e: e.matmul(bank(6), ones1f[:, :], mr[0:1, 0:512],
                                          start=True, stop=True),
                       reads=[R_mr, R_c], writes=[R_bank[6]], sem="s_mm")
                p.step("tensor",
                       lambda e: e.matmul(bank(7), ones1f[:, :],
                                          mr[0:1, 512:1024], start=True,
                                          stop=True),
                       reads=[R_mr, R_c], writes=[R_bank[7]], sem="s_mm")
                for dc in range(DCH):
                    p.step("vector",
                           lambda e, dc=dc, tile=tile:
                           e.tensor_sub(scr(2), x_res[:, dc * TC + tile * W:
                                                      dc * TC + tile * W + W],
                                        bank(6)),
                           reads=[R_x, R_bank[6]], writes=[R_scr[2]],
                           sem="s_vec")
                    p.step("vector",
                           lambda e, dc=dc:
                           e.tensor_mul(xn[:, ts(dc, W)], scr(2), bank(7)),
                           reads=[R_scr[2], R_bank[7]], writes=[R_xn],
                           sem="s_vec", settle=True)

            def emit_proj(wslot, R_w, dest, R_dest, with_sums):
                for mc in range(DCH):
                    pb = 4 if (mc % 2 == 0) else 5
                    for dc in range(DCH):
                        p.step("tensor",
                               lambda e, dc=dc, mc=mc, pb=pb, wslot=wslot:
                               e.matmul(bank(pb), wband(wslot, dc)[:, ts(mc, 128)],
                                        xn[:, ts(dc, W)], start=(dc == 0),
                                        stop=(dc == DCH - 1)),
                               reads=[R_xn, R_w], writes=[R_bank[pb]],
                               sem="s_mm")
                    p.step("scalar",
                           lambda e, mc=mc, pb=pb:
                           e.activation(dest[:, ts(mc, W)], bank(pb), AF.Copy),
                           reads=[R_bank[pb]], writes=[R_dest], sem="s_act")
                    if with_sums:
                        p.step("scalar",
                               lambda e, pb=pb:
                               e.activation(scr(0), bank(pb), AF.Square),
                               reads=[R_bank[pb]], writes=[R_scr[0]],
                               sem="s_act")
                        p.step("tensor",
                               lambda e, mc=mc:
                               e.matmul(bank(6)[0:16, :], hsum16(mc), scr(0),
                                        start=(mc == 0), stop=(mc == DCH - 1)),
                               reads=[R_scr[0], R_c], writes=[R_bank[6]],
                               sem="s_mm")

            def emit_rnorm():
                p.step("vector",
                       lambda e: e.tensor_scalar(rk_s[:, :], bank(6)[0:16, :],
                                                 1e-30, None, ALU.max),
                       reads=[R_bank[6]], writes=[R_rk], sem="s_vec")
                p.step("scalar",
                       lambda e: e.activation(rk_s[:, :], rk_s[:, :], AF.Ln),
                       reads=[R_rk], writes=[R_rk], sem="s_act")
                p.step("scalar",
                       lambda e: e.activation(rk_s[:, :], rk_s[:, :], AF.Exp,
                                              scale=cval[0:16, 2:3]),
                       reads=[R_rk, R_c], writes=[R_rk], sem="s_act")
                p.step("vector",
                       lambda e: e.tensor_scalar(rk_s[:, :], rk_s[:, :], 1e12,
                                                 None, ALU.min),
                       reads=[R_rk], writes=[R_rk], sem="s_vec")

            for qb0 in range(4):
                p.step("vector",
                       lambda e, qb0=qb0: e.memset(bank(qb0), 0.0),
                       writes=[R_bank[qb0]], sem="s_vec")

            for ti, tile in enumerate(range(NT - 1, -1, -1)):
                first_t = (ti == 0)
                emit_ln(tile)
                for dc in range(DCH):
                    p.step("tensor",
                           lambda e, dc=dc:
                           e.matmul(bank(4)[0:64, :], wsm_s[:, ts(dc, 64)],
                                    xn[:, ts(dc, W)], start=(dc == 0),
                                    stop=(dc == DCH - 1)),
                           reads=[R_xn, R_c], writes=[R_bank[4]], sem="s_mm")
                p.step("scalar",
                       lambda e: e.activation(sm[:, :], bank(4)[0:64, :],
                                              AF.Sigmoid, bias=bsm_s[:, 0:1]),
                       reads=[R_bank[4], R_c], writes=[R_sm], sem="s_act")
                p.step("scalar",
                       lambda e: e.activation(cums[:, :], sm[32:64, :], AF.Ln,
                                              bias=cval[0:32, 1:2]),
                       reads=[R_sm, R_c], writes=[R_cums], sem="s_act")
                if tile == NT - 1:
                    pass
                p.step("vector",
                       lambda e: e.tensor_reduce(redT[:, :], cums[:, :], AX.X,
                                                 ALU.add),
                       reads=[R_cums], writes=[R_sca], sem="s_vec", drain=True)
                p.step("vector",
                       lambda e: e.tensor_tensor(sca[:, 1:2], sca[:, 0:1],
                                                 redT[:, :], ALU.add),
                       reads=[R_sca], writes=[R_sca], sem="s_vec")
                p.step("vector",
                       lambda e: e.tensor_tensor_scan(
                           cums[:, :], cval[0:32, 4:5].broadcast_to([32, 512]),
                           cums[:, :], 0.0, ALU.mult, ALU.add),
                       reads=[R_cums, R_c], writes=[R_cums], sem="s_vec",
                       drain=True)
                if tile == NT - 1:
                    pass

                p.step("scalar",
                       lambda e: e.activation(fwb[0:32, :], cums[:, :], AF.Exp,
                                              scale=cval[0:32, 3:4],
                                              bias=sca[:, 1:2]),
                       reads=[R_cums, R_sca, R_c], writes=[R_fwb], sem="s_act")
                p.step("vector",
                       lambda e: e.tensor_copy(sca[:, 0:1], sca[:, 1:2]),
                       reads=[R_sca], writes=[R_sca], sem="s_vec")
                pass
                pass
                pass
                p.step("vector",
                       lambda e: e.tensor_reduce(redS[:, :], fwb[0:32, :], AX.X,
                                                 ALU.add),
                       reads=[R_fwb], writes=[R_sca], sem="s_vec", drain=True)
                p.step("vector",
                       lambda e: e.tensor_tensor(sca[:, 3:4], sca[:, 3:4],
                                                 redS[:, :], ALU.add),
                       reads=[R_sca], writes=[R_sca], sem="s_vec")

                emit_proj(0, R_wa[0], kq_bf, R_kq, with_sums=True)
                emit_rnorm()
                p.step("vector",
                       lambda e: e.tensor_tensor(rb16[0:16, 0:512], rk_s[:, :],
                                                 sm[0:16, :], ALU.mult),
                       reads=[R_rk, R_sm], writes=[R_rb16], sem="s_vec",
                       settle=True)
                p.step("vector",
                       lambda e: e.tensor_copy(rb16[0:16, 512:1024],
                                               sm[0:16, :]),
                       reads=[R_sm], writes=[R_rb16], sem="s_vec",
                       settle=True)

                for mc in range(DCH):
                    pb = 4 if (mc % 2 == 0) else 5
                    for dc in range(DCH):
                        p.step("tensor",
                               lambda e, dc=dc, mc=mc, pb=pb:
                               e.matmul(bank(pb), wband(1, dc)[:, ts(mc, 128)],
                                        xn[:, ts(dc, W)], start=(dc == 0),
                                        stop=(dc == DCH - 1)),
                               reads=[R_xn, R_wa[1]], writes=[R_bank[pb]],
                               sem="s_mm")
                    p.step("tensor",
                           lambda e, mc=mc:
                           e.matmul(bank(7), sel16(mc), rb16[0:128, 512:1024],
                                    start=True, stop=True),
                           reads=[R_rb16, R_c], writes=[R_bank[7]], sem="s_mm")
                    p.step("scalar",
                           lambda e, pb=pb: e.activation(scr(0), bank(pb),
                                                         AF.Copy),
                           reads=[R_bank[pb]], writes=[R_scr[0]], sem="s_act")
                    p.step("vector",
                           lambda e: e.tensor_tensor(scr(1), scr(0),
                                                     bank(7), ALU.mult),
                           reads=[R_scr[0], R_bank[7]], writes=[R_scr[1]],
                           sem="s_vec", settle=True)
                    p.step("tensor",
                           lambda e, mc=mc:
                           e.matmul(bank(7), sel16(mc), rb16[0:128, 0:512],
                                    start=True, stop=True),
                           reads=[R_rb16, R_c], writes=[R_bank[7]], sem="s_mm")
                    p.step("vector",
                           lambda e, mc=mc: e.tensor_tensor(
                               scr(2), kq_bf[:, ts(mc, W)], bank(7), ALU.mult),
                           reads=[R_kq, R_bank[7]], writes=[R_scr[2]],
                           sem="s_vec", settle=True)
                    if not NOFLUX:
                        p.step("tensor",
                               lambda e: e.matmul(bank(6)[0:32, :], wtf1_s[0:64, :],
                                                  scrb[0:64, 1024:1536],
                                                  start=True, stop=True),
                               reads=[R_scr[2], R_c], writes=[R_bank[6]],
                               sem="s_mm")
                        p.step("tensor",
                               lambda e: e.matmul(bank(6)[32:64, :],
                                                  wtf1_s[64:128, :],
                                                  scrb[64:128, 1024:1536],
                                                  start=True, stop=True),
                               reads=[R_scr[2], R_c], writes=[R_bank[6]],
                               sem="s_mm")
                        p.step("scalar",
                               lambda e: e.activation(scrb[0:64, 1536:2048],
                                                      bank(6)[0:64, :], AF.Silu,
                                                      bias=btf1_s[:, 0:1]),
                               reads=[R_bank[6], R_c], writes=[R_scr[3]],
                               sem="s_act")
                        p.step("tensor",
                               lambda e: e.matmul(bank(7)[0:2, :], wtf2b_s[:, :],
                                                  scrb[0:64, 1536:2048],
                                                  start=True, stop=True),
                               reads=[R_scr[3], R_c], writes=[R_bank[7]],
                               sem="s_mm")
                        p.step("vector",
                               lambda e, mc=mc:
                               e.tensor_scalar(lrow[0:2, ts(mc % 2, W)],
                                               bank(7)[0:2, :],
                                               btf2_s[0:2, 0:1], None, ALU.add),
                               reads=[R_bank[7], R_c], writes=[R_lrow], sem="s_vec")
                        if mc % 2 == 1:
                            p.step("sync",
                                   lambda e, tile=tile, mh=mc // 2:
                                   e.dma_start(out=alog_d.ap()[tile, :,
                                                               mh * 1024:
                                                               (mh + 1) * 1024],
                                               in_=lrow[0:2, :]),
                                   reads=[R_lrow], writes=[R_alog], sem="s_st",
                                   k=16)
                    p.step("tensor",
                           lambda e, mc=mc: e.matmul(bank(7), selsw(mc),
                                                     fwb[0:128, :], start=True,
                                                     stop=True),
                           reads=[R_fwb, R_c], writes=[R_bank[7]], sem="s_mm")
                    p.step("vector",
                           lambda e: e.tensor_tensor(scr(0), scr(2), bank(7),
                                                     ALU.mult),
                           reads=[R_scr[2], R_bank[7]], writes=[R_scr[0]],
                           sem="s_vec", settle=True)
                    p.step("tensor",
                           lambda e, mc=mc: e.matmul(bank(7), selfw(mc),
                                                     fwb[0:128, :], start=True,
                                                     stop=True),
                           reads=[R_fwb, R_c], writes=[R_bank[7]], sem="s_mm")
                    p.step("vector",
                           lambda e: e.tensor_tensor(scr(2), scr(2), bank(7),
                                                     ALU.mult),
                           reads=[R_scr[2], R_bank[7]], writes=[R_scr[2]],
                           sem="s_vec", settle=True)
                    for lb in range(4):
                        for j, (src_i, rsrc) in enumerate(
                                [(2, R_scr[2]), (0, R_scr[0]), (1, R_scr[1])]):
                            p.step("tensor",
                                   lambda e, lb=lb, j=j, src_i=src_i:
                                   e.transpose(
                                       bankb(6, 128, off=j * 128),
                                       scrb[:, src_i * 512 + lb * 128:
                                            src_i * 512 + lb * 128 + 128],
                                       id_b[:, :]),
                                   reads=[rsrc, R_c], writes=[R_bank[6]],
                                   sem="s_mm")
                        p.step("vector",
                               lambda e: e.tensor_copy(scr(3, 384),
                                                       bankb(6, 384)),
                               reads=[R_bank[6]], writes=[R_scr[3]],
                               sem="s_vec", settle=True)
                        if tile == NT - 1 and lb == 3 and mc in (4, 6):
                            pass
                        bf_ = 0 if mc < 4 else 1
                        bs_ = 3 if mc < 4 else 2
                        p.step("tensor",
                               lambda e, mc=mc, lb=lb, bf_=bf_:
                               e.matmul(bank(bf_, 128, off=(mc % 4) * 128),
                                        scr(3, 128), scr(3, 128, off=256),
                                        start=False, stop=(lb == 3)),
                               reads=[R_scr[3]], writes=[R_bank[bf_]],
                               sem="s_mm")
                        p.step("tensor",
                               lambda e, mc=mc, lb=lb, bs_=bs_:
                               e.matmul(bank(bs_, 128, off=(mc % 4) * 128),
                                        scr(3, 128, off=128),
                                        scr(3, 128, off=256),
                                        start=False, stop=(lb == 3)),
                               reads=[R_scr[3]], writes=[R_bank[bs_]],
                               sem="s_mm")

                if tile == NT - 1:
                    fz = mhat[:, :].bitcast(F32)
                    p.step("vector",
                           lambda e, fz=fz: e.tensor_copy(fz[:, 0:512],
                                                          bank(2)),
                           reads=[R_bank[2]], writes=[R_mhat], sem="s_vec")
                    pass
                for qb, bsrc in [(0, 0), (1, 1), (2, 3), (3, 2)]:
                    p.step("vector",
                           lambda e, qb=qb, bsrc=bsrc: e.tensor_copy(
                               mstage[:, :], bank(bsrc)),
                           reads=[R_bank[bsrc]], writes=[R_mst, R_scr[0],
                                                         R_scr[1]], sem="s_vec")
                    if first_t:
                        pass
                    p.step("gpsimd",
                           lambda e, qb=qb, first_t=first_t:
                           e.dma_start(out=cc_in.ap()[:, qb * 512:
                                                      (qb + 1) * 512],
                                       in_=mstage[:, :],
                                       accum_op=(ALU.bypass if first_t
                                                 else ALU.add)),
                           reads=[R_mst, R_scr[0], R_scr[1]],
                           writes=[R_ccin], sem="s_wl", k=16)
                    p.step("vector",
                           lambda e, bsrc=bsrc: e.memset(bank(bsrc), 0.0),
                           reads=[], writes=[R_bank[bsrc]], sem="s_vec")

            p.step("sync",
                   lambda e: e.dma_start(out=cc_in.ap()[0:32, 2048:2049],
                                         in_=sca[:, 3:4],
                                         allow_slow_non_contiguous=True),
                   reads=[R_sca], writes=[R_ccin], sem="s_st", k=16)
            p.step("sync",
                   lambda e: e.dma_start(out=cc_in.ap()[0:32, 2049:2050],
                                         in_=sca[:, 0:1],
                                         allow_slow_non_contiguous=True),
                   reads=[R_sca], writes=[R_ccin], sem="s_st", k=16)

            wload(0, wq_h)
            wload(1, wo_h)

            p.step("gpsimd",
                   lambda e: e.collective_compute(
                       "AllGather", ALU.bypass, replica_groups=REPL,
                       ins=[cc_in.ap().opt()], outs=[cc_out.ap().opt()]),
                   reads=[R_ccin], writes=[R_ccout], sem="s_cc", k=1)

            g0 = kq_bf[:, :].bitcast(F32)
            g1 = xn[:, :].bitcast(F32)
            sc01 = mr[0:1, :]
            p.step("sync",
                   lambda e: e.dma_start(out=g0, in_=cc_out.ap()[0:128, 0:2048]),
                   reads=[R_ccout], writes=[R_kq], sem="s_ld", k=16)
            p.step("sync",
                   lambda e: e.dma_start(out=g1,
                                         in_=cc_out.ap()[128:256, 0:2048]),
                   reads=[R_ccout], writes=[R_xn], sem="s_ld", k=16)
            p.step("sync",
                   lambda e: e.dma_start(out=sca[:, 4:6],
                                         in_=cc_out.ap()[0:32, 2048:2050],
                                         allow_slow_non_contiguous=True),
                   reads=[R_ccout], writes=[R_sca], sem="s_ld", k=16)
            p.step("sync",
                   lambda e: e.dma_start(out=sca[:, 6:8],
                                         in_=cc_out.ap()[128:160, 2048:2050],
                                         allow_slow_non_contiguous=True),
                   reads=[R_ccout], writes=[R_sca], sem="s_ld", k=16)
            p.step("scalar",
                   lambda e: e.activation(sca[:, 7:8], sca[:, 7:8], AF.Exp),
                   reads=[R_sca], writes=[R_sca], sem="s_act")
            p.step("vector",
                   lambda e: e.scalar_tensor_tensor(sca[:, 5:6], sca[:, 4:5],
                                                    sca[:, 7:8], sca[:, 6:7],
                                                    ALU.mult, ALU.add),
                   reads=[R_sca], writes=[R_sca], sem="s_vec")
            p.step("vector",
                   lambda e: e.tensor_scalar(sca[:, 5:6], sca[:, 5:6], 1e-6,
                                             None, ALU.add),
                   reads=[R_sca], writes=[R_sca], sem="s_vec")
            p.step("vector",
                   lambda e: e.reciprocal(sca[:, 5:6], sca[:, 5:6]),
                   reads=[R_sca], writes=[R_sca], sem="s_vec")
            p.step("vector",
                   lambda e: e.tensor_tensor(sca[:, 7:8], sca[:, 7:8],
                                             sca[:, 5:6], ALU.mult),
                   reads=[R_sca], writes=[R_sca], sem="s_vec")
            p.step("tensor",
                   lambda e: e.matmul(bank(6)[0:1, 0:32], sca[:, 7:8],
                                      id_f[0:32, 0:32], start=True, stop=True),
                   reads=[R_sca, R_c], writes=[R_bank[6]], sem="s_mm")
            p.step("tensor",
                   lambda e: e.matmul(bank(6)[0:1, 32:64], sca[:, 5:6],
                                      id_f[0:32, 0:32], start=True, stop=True),
                   reads=[R_sca, R_c], writes=[R_bank[6]], sem="s_mm")
            p.step("vector",
                   lambda e: e.tensor_copy(erow[0:1, 512:576],
                                           bank(6)[0:1, 0:64]),
                   reads=[R_bank[6]], writes=[R_erow, R_cums], sem="s_vec")
            for cb4 in range(4):
                p.step("vector",
                       lambda e, cb4=cb4: e.tensor_copy(
                           mstage[:, :], g0[:, cb4 * 512:(cb4 + 1) * 512]),
                       reads=[R_kq], writes=[R_mst, R_scr[0], R_scr[1]],
                       sem="s_vec")
                pass
            p.step("vector",
                   lambda e: e.tensor_copy(mstage[0:1, 0:64],
                                           erow[0:1, 512:576]),
                   reads=[R_erow], writes=[R_mst, R_scr[0], R_scr[1]],
                   sem="s_vec")
            pass
            for cb in range(4):
                half, q2 = cb // 2, cb % 2
                for sci, bk in ((0, 6), (1, 7)):
                    base = 512 + sci * 32 + half * 16 + q2 * 8
                    p.step("vector",
                           lambda e, base=base:
                           e.tensor_copy(
                               erow[0:1, 0:512],
                               erow[0:1, base: base + 8]
                               .rearrange("p (m h) -> p m h", h=2)
                               .broadcast_to([1, 4, 2, 64])),
                           reads=[R_erow], writes=[R_erow], sem="s_vec")
                    p.step("tensor",
                           lambda e, bk=bk: e.matmul(
                               bank(bk), selc[0:1, ONES1B:ONES1B + 128],
                               erow[0:1, 0:512], start=True, stop=True),
                           reads=[R_erow, R_c], writes=[R_bank[bk]],
                           sem="s_mm")
                t2 = scrb[:, 1024:2048].bitcast(F32)
                p.step("vector",
                       lambda e, cb=cb: e.tensor_tensor(
                           mstage[:, 0:512], g0[:, cb * 512:(cb + 1) * 512],
                           bank(6), ALU.mult),
                       reads=[R_kq, R_bank[6]], writes=[R_mst, R_scr[0],
                                                        R_scr[1]],
                       sem="s_vec")
                p.step("vector",
                       lambda e, cb=cb, t2=t2: e.tensor_tensor(
                           t2, g1[:, cb * 512:(cb + 1) * 512],
                           bank(7), ALU.mult),
                       reads=[R_xn, R_bank[7]], writes=[R_scr[2], R_scr[3]],
                       sem="s_vec")
                p.step("vector",
                       lambda e, cb=cb, t2=t2: e.tensor_tensor(
                           mhat[:, cb * 512:(cb + 1) * 512], mstage[:, 0:512],
                           t2, ALU.add),
                       reads=[R_mst, R_scr[2], R_scr[3]], writes=[R_mhat],
                       sem="s_vec", settle=True)

            p.step("vector",
                   lambda e: e.tensor_copy(mstage[0:32, 0:128],
                                           selc[0:32, SELF + 5 * 128:
                                                SELF + 6 * 128]),
                   reads=[R_c], writes=[R_mst, R_scr[0], R_scr[1]],
                   sem="s_vec")
            p.step("vector",
                   lambda e: e.tensor_copy(mstage[0:32, 128:256],
                                           selc[0:32, SELS + 5 * 128:
                                                SELS + 6 * 128]),
                   reads=[R_c], writes=[R_mst, R_scr[0], R_scr[1]],
                   sem="s_vec")
            pass
            for cb in range(4):
                p.step("vector",
                       lambda e, cb=cb: e.tensor_copy(
                           mstage[:, :], mhat[:, cb * 512:(cb + 1) * 512]),
                       reads=[R_mhat], writes=[R_mst, R_scr[0], R_scr[1]],
                       sem="s_vec")
                pass
            pass
            p.step("vector",
                   lambda e: e.tensor_copy(mstage[0:32, :], fwb[0:32, :]),
                   reads=[R_fwb], writes=[R_mst, R_scr[0], R_scr[1]],
                   sem="s_vec")
            pass

            for tile in (range(NT) if upto != "p1a" else []):
                emit_ln(tile)
                emit_proj(0, R_wa[0], kq_bf, R_kq, with_sums=True)
                emit_rnorm()
                p.step("vector",
                       lambda e: e.tensor_copy(rb16[0:16, 0:512], rk_s[:, :]),
                       reads=[R_rk], writes=[R_rb16], sem="s_vec",
                       settle=True)

                for mc in range(DCH):
                    if mc % 2 == 0:
                        p.step("sync",
                               lambda e, tile=tile, mh=mc // 2:
                               e.dma_start(out=lrow[0:2, :],
                                           in_=alog_d.ap()[tile, :,
                                                           mh * 1024:
                                                           (mh + 1) * 1024]),
                               reads=[R_alog], writes=[R_lrow], sem="s_ld",
                               k=16)
                    p.step("tensor",
                           lambda e, mc=mc: e.matmul(bank(7), sel16(mc),
                                                     rb16[0:128, 0:512],
                                                     start=True, stop=True),
                           reads=[R_rb16, R_c], writes=[R_bank[7]], sem="s_mm")
                    p.step("vector",
                           lambda e, mc=mc: e.tensor_tensor(
                               scr(0), kq_bf[:, ts(mc, W)], bank(7), ALU.mult),
                           reads=[R_kq, R_bank[7]], writes=[R_scr[0]],
                           sem="s_vec", settle=True)
                    for hh in range(2):
                        p.step("tensor",
                               lambda e, mc=mc, hh=hh: e.matmul(
                                   bank(4)[hh * 64:(hh + 1) * 64, :],
                                   mhat[hh * 64:(hh + 1) * 64,
                                        mc * 128 + hh * 64: mc * 128 + hh * 64
                                        + 64],
                                   scrb[hh * 64:(hh + 1) * 64, 0:512],
                                   start=True, stop=True),
                               reads=[R_mhat, R_scr[0]], writes=[R_bank[4]],
                               sem="s_mm")
                        p.step("tensor",
                               lambda e, mc=mc, hh=hh: e.matmul(
                                   bank(5)[hh * 64:(hh + 1) * 64, :],
                                   mhat[hh * 64:(hh + 1) * 64,
                                        1024 + mc * 128 + hh * 64:
                                        1024 + mc * 128 + hh * 64 + 64],
                                   scrb[hh * 64:(hh + 1) * 64, 0:512],
                                   start=True, stop=True),
                               reads=[R_mhat, R_scr[0]], writes=[R_bank[5]],
                               sem="s_mm")
                    p.step("tensor",
                           lambda e, mc=mc: e.matmul(bank(6), hsel2,
                                                     lrow[0:128, ts(mc % 2, W)],
                                                     start=True, stop=True),
                           reads=[R_lrow, R_c], writes=[R_bank[6]], sem="s_mm")
                    p.step("scalar",
                           lambda e: e.activation(scr(1), bank(6), AF.Sigmoid),
                           reads=[R_bank[6]], writes=[R_scr[1]], sem="s_act")
                    p.step("vector",
                           lambda e: e.tensor_scalar(scr(1), scr(1), 0.3, 0.5,
                                                     ALU.mult, ALU.add),
                           reads=[R_scr[1]], writes=[R_scr[1]], sem="s_vec")
                    p.step("vector",
                           lambda e: e.tensor_scalar(scr(1), scr(1), 0.503,
                                                     0.797, ALU.max, ALU.min),
                           reads=[R_scr[1]], writes=[R_scr[1]], sem="s_vec")
                    p.step("scalar",
                           lambda e: e.activation(scr(2), bank(4), AF.Copy),
                           reads=[R_bank[4]], writes=[R_scr[2]], sem="s_act")
                    p.step("vector",
                           lambda e: e.tensor_sub(scr(2), scr(2), bank(5)),
                           reads=[R_scr[2], R_bank[5]], writes=[R_scr[2]],
                           sem="s_vec")
                    p.step("vector",
                           lambda e: e.tensor_mul(scr(3), scr(1), scr(2)),
                           reads=[R_scr[1], R_scr[2]], writes=[R_scr[3]],
                           sem="s_vec")
                    p.step("vector",
                           lambda e, mc=mc: e.tensor_tensor(
                               kq_bf[:, ts(mc, W)], bank(5), scr(3), ALU.add),
                           reads=[R_bank[5], R_scr[3]], writes=[R_kq],
                           sem="s_vec", settle=True)
                if tile == NT - 1:
                    for mc8 in range(DCH):
                        p.step("vector",
                               lambda e, mc8=mc8: e.tensor_copy(
                                   mstage[:, :], kq_bf[:, ts(mc8, W)]),
                               reads=[R_kq], writes=[R_mst, R_scr[0],
                                                     R_scr[1]], sem="s_vec")
                        pass
                    pass
                for m2 in range(DCH):
                    pb = 4 if (m2 % 2 == 0) else 5
                    for mc in range(DCH):
                        p.step("tensor",
                               lambda e, m2=m2, mc=mc, pb=pb:
                               e.matmul(bank(pb), wband(1, mc)[:, ts(m2, 128)],
                                        kq_bf[:, ts(mc, W)], start=(mc == 0),
                                        stop=(mc == DCH - 1)),
                               reads=[R_kq, R_wa[1]], writes=[R_bank[pb]],
                               sem="s_mm")
                    p.step("vector",
                           lambda e, m2=m2, tile=tile, pb=pb:
                           e.tensor_tensor(
                               x_res[:, m2 * TC + tile * W:
                                     m2 * TC + tile * W + W],
                               x_res[:, m2 * TC + tile * W:
                                     m2 * TC + tile * W + W],
                               bank(pb), ALU.add),
                           reads=[R_bank[pb], R_x], writes=[R_x], sem="s_vec")

        if upto not in ("p1a", "p1"):
            p.barrier()
            with contextlib.ExitStack() as p2stack:
                E2 = p2stack.enter_context
                xn2 = warena[:, 0:4096]
                hbuf = warena[:, 4096:12288]
                w1b = warena[:, 12288:16384]
                w2b = E2(nc.sbuf_tensor("w2b", [128, 4096], BF16))
                mr2 = E2(nc.sbuf_tensor("mr2", [1, 1024], F32))
                s2 = E2(nc.sbuf_tensor("s2", [128, 3072], BF16))
                comb = E2(nc.sbuf_tensor("comb", [128, 512], BF16))
                sel6 = E2(nc.sbuf_tensor("sel6_s", [128, 768], BF16))
                wg_s = E2(nc.sbuf_tensor("wg_sb", [128, 64], BF16))
                b1p = E2(nc.sbuf_tensor("b1p_s", [128, 96], F32))
                b2p = E2(nc.sbuf_tensor("b2p_s", [128, 48], F32))
                bgp = E2(nc.sbuf_tensor("bgp", [8, 1], F32))
                lgr = E2(nc.sbuf_tensor("lgr", [8, 512], F32))
                rt = E2(nc.sbuf_tensor("rt", [128, 40], F32))
                rmax1 = E2(nc.sbuf_tensor("rmax1", [128, 1], F32))
                rmax2 = E2(nc.sbuf_tensor("rmax2", [128, 1], F32))

                R_xn2 = Res(); R_h = Res(); R_w1 = [Res(), Res()]
                R_w2 = [Res(), Res()]
                R_mr2 = Res(); R_s2 = [Res() for _ in range(6)]
                R_comb = Res(); R_lgr = Res(); R_rt = Res()

                def s2s(i, cols=512, off=0):
                    return s2[:, i * 512 + off: i * 512 + off + cols]

                p.step("gpsimd", lambda e: e.dma_start(out=sel6[:, :],
                                                       in_=sel6_h[:, :]),
                       writes=[R_c], sem="s_wl", k=16)
                for dc in range(DCH):
                    p.step("gpsimd",
                           lambda e, dc=dc: e.dma_start(
                               out=wg_s[:, ts(dc, 8)],
                               in_=wg_h[ts(dc, 128), :]),
                           writes=[R_c], sem="s_wl", k=16)
                p.step("sync", lambda e: e.dma_start(out=b1p[:, :],
                                                     in_=b1p_h[:, :]),
                       writes=[R_c], sem="s_ld", k=16)
                p.step("sync", lambda e: e.dma_start(out=b2p[:, :],
                                                     in_=b2p_h[:, :]),
                       writes=[R_c], sem="s_ld", k=16)
                p.step("sync", lambda e: e.dma_start(out=bgp[:, :],
                                                     in_=bg_h[:, :]),
                       writes=[R_c], sem="s_ld", k=16)
                p.step("vector", lambda e: e.memset(comb[:, :], 0.0),
                       writes=[R_comb], sem="s_vec")
                p.step("vector", lambda e: e.memset(rt[:, :], 0.0),
                       writes=[R_rt], sem="s_vec")

                for pt in range(NT):
                    base = pt * W
                    for dc in range(DCH):
                        p.step("vector",
                               lambda e, dc=dc, base=base:
                               e.tensor_copy(s2s(0),
                                             x_res[:, dc * TC + base:
                                                   dc * TC + base + W]),
                               reads=[R_x], writes=[R_s2[0]], sem="s_vec",
                               settle=True)
                        p.step("vector",
                               lambda e: e.tensor_mul(s2s(1), s2s(0), s2s(0)),
                               reads=[R_s2[0]], writes=[R_s2[1]], sem="s_vec",
                               settle=True)
                        p.step("tensor",
                               lambda e, dc=dc:
                               e.matmul(bank(6)[0:1, :], ones128b, s2s(0),
                                        start=(dc == 0), stop=(dc == DCH - 1)),
                               reads=[R_s2[0], R_c], writes=[R_bank[6]],
                               sem="s_mm")
                        p.step("tensor",
                               lambda e, dc=dc:
                               e.matmul(bank(7)[0:1, :], ones128b, s2s(1),
                                        start=(dc == 0), stop=(dc == DCH - 1)),
                               reads=[R_s2[1], R_c], writes=[R_bank[7]],
                               sem="s_mm")
                    p.step("vector",
                           lambda e: e.tensor_scalar(mr2[0:1, 0:512],
                                                     bank(6)[0:1, :], 1.0 / D,
                                                     None, ALU.mult),
                           reads=[R_bank[6]], writes=[R_mr2], sem="s_vec")
                    p.step("vector",
                           lambda e: e.tensor_scalar(mr2[0:1, 512:1024],
                                                     bank(7)[0:1, :], 1.0 / D,
                                                     None, ALU.mult),
                           reads=[R_bank[7]], writes=[R_mr2], sem="s_vec")
                    p.step("scalar",
                           lambda e: e.activation(bank(6)[0:1, :],
                                                  mr2[0:1, 0:512], AF.Square),
                           reads=[R_mr2], writes=[R_bank[6]], sem="s_act")
                    p.step("vector",
                           lambda e: e.tensor_sub(mr2[0:1, 512:1024],
                                                  mr2[0:1, 512:1024],
                                                  bank(6)[0:1, :]),
                           reads=[R_mr2, R_bank[6]], writes=[R_mr2],
                           sem="s_vec")
                    p.step("scalar",
                           lambda e: e.activation(mr2[0:1, 512:1024],
                                                  mr2[0:1, 512:1024], AF.Ln,
                                                  bias=cval[0:1, 0:1]),
                           reads=[R_mr2, R_c], writes=[R_mr2], sem="s_act")
                    p.step("scalar",
                           lambda e: e.activation(mr2[0:1, 512:1024],
                                                  mr2[0:1, 512:1024], AF.Exp,
                                                  scale=cval[0:1, 2:3]),
                           reads=[R_mr2, R_c], writes=[R_mr2], sem="s_act")
                    p.step("tensor",
                           lambda e: e.matmul(bank(6), ones1f[:, :],
                                              mr2[0:1, 0:512], start=True,
                                              stop=True),
                           reads=[R_mr2, R_c], writes=[R_bank[6]], sem="s_mm")
                    p.step("tensor",
                           lambda e: e.matmul(bank(7), ones1f[:, :],
                                              mr2[0:1, 512:1024], start=True,
                                              stop=True),
                           reads=[R_mr2, R_c], writes=[R_bank[7]], sem="s_mm")
                    for dc in range(DCH):
                        p.step("vector",
                               lambda e, dc=dc, base=base:
                               e.tensor_sub(s2s(2),
                                            x_res[:, dc * TC + base:
                                                  dc * TC + base + W],
                                            bank(6)),
                               reads=[R_x, R_bank[6]], writes=[R_s2[2]],
                               sem="s_vec")
                        p.step("vector",
                               lambda e, dc=dc:
                               e.tensor_mul(xn2[:, ts(dc, W)], s2s(2),
                                            bank(7)),
                               reads=[R_s2[2], R_bank[7]], writes=[R_xn2],
                               sem="s_vec", settle=True)
                    if pt == 0:
                        pass

                    for dc in range(DCH):
                        p.step("tensor",
                               lambda e, dc=dc:
                               e.matmul(bank(6)[0:8, :], wg_s[:, ts(dc, 8)],
                                        xn2[:, ts(dc, W)], start=(dc == 0),
                                        stop=(dc == DCH - 1)),
                               reads=[R_xn2, R_c], writes=[R_bank[6]],
                               sem="s_mm")
                    p.step("scalar",
                           lambda e: e.activation(lgr[:, :], bank(6)[0:8, :],
                                                  AF.Identity,
                                                  bias=bgp[:, 0:1]),
                           reads=[R_bank[6], R_c], writes=[R_lgr],
                           sem="s_act", settle=False)
                    if pt == 0:
                        pass
                    for lb in range(4):
                        p.step("tensor",
                               lambda e, lb=lb:
                               e.transpose(bank(7, 8), lgr[:, ts(lb, 128)],
                                           id_f[0:8, 0:8]),
                               reads=[R_lgr, R_c], writes=[R_bank[7]],
                               sem="s_mm")
                        p.step("vector",
                               lambda e: e.tensor_copy(rt[:, 0:8],
                                                       bank(7, 8)),
                               reads=[R_bank[7]], writes=[R_rt], sem="s_vec", drain=True)
                        if pt == 0 and lb == 0:
                            pass
                        p.step("vector",
                               lambda e: e.tensor_reduce(rmax1[:, :],
                                                         rt[:, 0:6], AX.X,
                                                         ALU.max),
                               reads=[R_rt], writes=[R_rt], sem="s_vec",
                               drain=True)
                        p.step("vector",
                               lambda e: e.tensor_scalar(rt[:, 16:22],
                                                         rt[:, 0:6],
                                                         rmax1[:, :], None,
                                                         ALU.is_equal),
                               reads=[R_rt], writes=[R_rt], sem="s_vec", drain=True)
                        p.step("vector",
                               lambda e: e.scalar_tensor_tensor(
                                   rt[:, 24:30], rt[:, 16:22], -1e30,
                                   rt[:, 0:6], ALU.mult, ALU.add),
                               reads=[R_rt], writes=[R_rt], sem="s_vec", drain=True)
                        p.step("vector",
                               lambda e: e.tensor_reduce(rmax2[:, :],
                                                         rt[:, 24:30], AX.X,
                                                         ALU.max),
                               reads=[R_rt], writes=[R_rt], sem="s_vec",
                               drain=True)
                        p.step("vector",
                               lambda e: e.tensor_sub(rt[:, 10:11],
                                                      rmax1[:, :],
                                                      rmax2[:, :]),
                               reads=[R_rt], writes=[R_rt], sem="s_vec", drain=True)
                        p.step("scalar",
                               lambda e: e.activation(rt[:, 11:12],
                                                      rt[:, 10:11],
                                                      AF.Sigmoid),
                               reads=[R_rt], writes=[R_rt], sem="s_act")
                        p.step("vector",
                               lambda e: e.tensor_scalar(rt[:, 12:13],
                                                         rt[:, 11:12], -1.0,
                                                         1.0, ALU.mult,
                                                         ALU.add),
                               reads=[R_rt], writes=[R_rt], sem="s_vec", drain=True)
                        p.step("vector",
                               lambda e: e.tensor_scalar(rt[:, 32:38],
                                                         rt[:, 24:30],
                                                         rmax2[:, :], None,
                                                         ALU.is_equal),
                               reads=[R_rt], writes=[R_rt], sem="s_vec", drain=True)
                        p.step("vector",
                               lambda e: e.tensor_scalar(rt[:, 16:22],
                                                         rt[:, 16:22],
                                                         rt[:, 11:12], None,
                                                         ALU.mult),
                               reads=[R_rt], writes=[R_rt], sem="s_vec", drain=True)
                        p.step("vector",
                               lambda e: e.tensor_scalar(rt[:, 32:38],
                                                         rt[:, 32:38],
                                                         rt[:, 12:13], None,
                                                         ALU.mult),
                               reads=[R_rt], writes=[R_rt], sem="s_vec", drain=True)
                        p.step("vector",
                               lambda e: e.tensor_tensor(rt[:, 16:22],
                                                         rt[:, 16:22],
                                                         rt[:, 32:38],
                                                         ALU.add),
                               reads=[R_rt], writes=[R_rt], sem="s_vec",
                               drain=True, settle=True)
                        if pt == 0 and lb == 0:
                            pass
                        p.step("tensor",
                               lambda e, lb=lb:
                               e.transpose(bank(6, 128)[0:8, :], rt[:, 16:24],
                                           id_f[:, :]),
                               reads=[R_rt, R_c], writes=[R_bank[6]],
                               sem="s_mm")
                        p.step("vector",
                               lambda e, lb=lb: e.tensor_copy(
                                   comb[0:8, ts(lb, 128)],
                                   bank(6, 128)[0:8, :]),
                               reads=[R_bank[6]], writes=[R_comb],
                               sem="s_vec", settle=True)

                    if pt == 0:
                        pass

                    for ex in range(6):
                        for hm in range(16):
                            sl_ = (hm // 2) % 2
                            pb_ = hm % 2
                            if hm % 2 == 0:
                                p.step("sync",
                                       lambda e, ex=ex, hm=hm, sl_=sl_:
                                       e.dma_start(
                                           out=w1b[:, ts(sl_, 2048)],
                                           in_=w1c.ap()[ex, :,
                                                        hm * 1024:
                                                        hm * 1024 + 2048]),
                                       reads=[R_wc],
                                       writes=[R_w1[sl_]], sem="s_ld",
                                       k=16)
                            off1 = sl_ * 2048 + (hm % 2) * 1024
                            for dc in range(DCH):
                                p.step("tensor",
                                       lambda e, dc=dc, sl_=sl_, off1=off1,
                                       pb_=pb_:
                                       e.matmul(bank(4 + pb_),
                                                w1b[:, off1 + dc * 128:
                                                    off1 + dc * 128 + 128],
                                                xn2[:, ts(dc, W)],
                                                start=(dc == 0),
                                                stop=(dc == DCH - 1)),
                                       reads=[R_xn2, R_w1[sl_]],
                                       writes=[R_bank[4 + pb_]], sem="s_mm")
                            p.step("scalar",
                                   lambda e, ex=ex, hm=hm, pb_=pb_:
                                   e.activation(hbuf[:, ts(hm, W)],
                                                bank(4 + pb_), AF.Gelu,
                                                bias=b1p[:, ex * 16 + hm:
                                                         ex * 16 + hm + 1]),
                                   reads=[R_bank[4 + pb_], R_c],
                                   writes=[R_h],
                                   sem="s_act")
                            if pt == 0 and ex == 0 and hm == 0:
                                pass
                        p.step("tensor",
                               lambda e, ex=ex:
                               e.matmul(bank(6), sel6[0:128,
                                                      ts(ex, 128)],
                                        comb[0:128, :], start=True,
                                        stop=True),
                               reads=[R_comb, R_c], writes=[R_bank[6]],
                               sem="s_mm")
                        cbf = s2[:, 0:1024].bitcast(F32)
                        p.step("scalar",
                               lambda e, cbf=cbf: e.activation(cbf, bank(6),
                                                               AF.Copy),
                               reads=[R_bank[6]],
                               writes=[R_s2[0], R_s2[1]],
                               sem="s_act")
                        if pt == 0 and ex == 0:
                            pass
                        for m2 in range(DCH):
                            sl_ = m2 % 2
                            p.step("sync",
                                   lambda e, ex=ex, m2=m2, sl_=sl_:
                                   e.dma_start(out=w2b[:, ts(sl_, 2048)],
                                               in_=w2c.ap()[ex, m2, :, :]),
                                   reads=[R_wc],
                                   writes=[R_w2[sl_]], sem="s_ld", k=16)
                            for hc in range(16):
                                p.step("tensor",
                                       lambda e, m2=m2, hc=hc, sl_=sl_:
                                       e.matmul(bank(4 + sl_),
                                                w2b[:, sl_ * 2048 + hc * 128:
                                                    sl_ * 2048 + hc * 128
                                                    + 128],
                                                hbuf[:, ts(hc, W)],
                                                start=(hc == 0),
                                                stop=(hc == 15)),
                                       reads=[R_h, R_w2[sl_]],
                                       writes=[R_bank[4 + sl_]], sem="s_mm")
                            t4 = s2[:, 2048:3072].bitcast(F32)
                            p.step("vector",
                                   lambda e, ex=ex, m2=m2, sl_=sl_, t4=t4,
                                   cbf=cbf:
                                   e.scalar_tensor_tensor(
                                       t4, bank(4 + sl_),
                                       b2p[:, ex * 8 + m2: ex * 8 + m2 + 1],
                                       cbf[:, 0:512], ALU.add, ALU.mult),
                                   reads=[R_bank[4 + sl_], R_s2[0], R_s2[1],
                                          R_c],
                                   writes=[R_s2[4], R_s2[5]], sem="s_vec")
                            if pt == 0 and ex == 0 and m2 == 0:
                                pass
                            p.step("vector",
                                   lambda e, m2=m2, base=base, t4=t4:
                                   e.tensor_tensor(
                                       x_res[:, m2 * TC + base:
                                             m2 * TC + base + W],
                                       x_res[:, m2 * TC + base:
                                             m2 * TC + base + W],
                                       t4, ALU.add),
                                   reads=[R_s2[4], R_s2[5], R_x],
                                   writes=[R_x], sem="s_vec")

                p.barrier()

        if upto not in ("p1a", "p1", "p2"):
            p.barrier()
            RG = {}

            def _regs(eng, sems):
                pid = eng.partition_id()
                par = eng.alloc_register()
                eng.reg_mod(par, pid, 2)
                RG["par"] = par
            p.raw("gpsimd", _regs)

            with contextlib.ExitStack() as p3s:
                E3 = p3s.enter_context
                bc_f = E3(nc.sbuf_tensor("bc_f", [32, 8192], BF16))
                nsw_s = E3(nc.sbuf_tensor("nsw_s", [128, 8], F32))
                inb_s = E3(nc.sbuf_tensor("inb_s", [128, 32], F32))
                cw_s = E3(nc.sbuf_tensor("cw_s", [128, 64], F32))
                cb_s = E3(nc.sbuf_tensor("cb_s", [128, 16], F32))
                dvc_s = E3(nc.sbuf_tensor("dvc_s", [128, 16], F32))
                otb_s = E3(nc.sbuf_tensor("otb_s", [128, 8], F32))
                dtb_s = E3(nc.sbuf_tensor("dtb_s", [128, 8], F32))
                a_sc = E3(nc.sbuf_tensor("a_sc", [128, 128], F32))
                dtw_s = E3(nc.sbuf_tensor("dtw_s", [64, 1024], BF16))
                R3c = Res("p3consts"); R_bc = Res("bc_f"); R_asc = Res()
                R_dtw = Res()
                R_hd = Res(); R_gd = Res(); R_hidd = Res(); R_dltd = Res()
                R_halo = [Res(), Res()]
                R_ccx = [Res(), Res()]; R_cch = [Res(), Res()]
                R_ccy = [Res(), Res()]
                R_xspill = Res()

                for tgt, src in [(nsw_s, nsw_h), (inb_s, inb_h),
                                 (cw_s, cwp_h), (cb_s, cbp_h),
                                 (dvc_s, dvc_h), (otb_s, otb_h),
                                 (dtb_s, dtb_h)]:
                    p.step("sync", lambda e, t=tgt, sr=src:
                           e.dma_start(out=t[:, :], in_=sr[:, :]),
                           writes=[R3c], sem="s_ld", k=16)
                p.step("sync", lambda e: e.dma_start(out=a_sc[:, :],
                                                     in_=alg_h[:, :]),
                       writes=[R_asc], sem="s_ld", k=16)
                p.step("scalar", lambda e: e.activation(a_sc[:, :], a_sc[:, :],
                                                        AF.Exp),
                       reads=[R_asc], writes=[R_asc], sem="s_act")
                p.step("vector", lambda e: e.tensor_scalar(
                    a_sc[:, :], a_sc[:, :], -1.0, None, ALU.mult),
                       reads=[R_asc], writes=[R_asc], sem="s_vec")
                p.step("gpsimd", lambda e: e.dma_start(out=dtw_s[:, :],
                                                       in_=dtw_h[:, :]),
                       writes=[R_dtw], sem="s_wl", k=16)

                with contextlib.ExitStack() as pa:
                    EA = pa.enter_context
                    mr3 = EA(nc.sbuf_tensor("mr3", [1, 1024], F32))
                    s3 = EA(nc.sbuf_tensor("s3", [128, 1024], BF16))
                    sxf = EA(nc.sbuf_tensor("sxf", [128, 1024], F32))
                    estg = EA(nc.sbuf_tensor("estg", [128, 1024], BF16))
                    cxs = EA(nc.sbuf_tensor("cxs", [128, 4096], BF16))
                    tsstg = EA(nc.sbuf_tensor("tsstg", [64, 1024], BF16))

                    R_mr3 = Res(); R_s3 = [Res(), Res()]
                    R_sxf = [Res(), Res()]; R_estg = [Res(), Res()]
                    R_tsf = Res()
                    R_xn3 = Res(); R_w3 = [Res(), Res()]
                    R_xpws = [Res(), Res()]
                    R_hbuf = [Res(), Res()]; R_cnv = [Res(), Res()]
                    R_cacc = Res()
                    xn3 = warena[:, 0:4096]
                    w3s = [warena[:, 4096:5120], warena[:, 5120:6144]]
                    xpws = [warena[:, 6144:6272], warena[:, 6272:6400]]
                    hbufs = [warena[:, 6400:8451], warena[:, 8451:10502]]
                    cnvs = [warena[:, 10502:12550], warena[:, 12550:14598]]
                    cacc = warena[:, 14598:16384].bitcast(F32)

                    def ln3(pt):
                        for dc in range(DCH):
                            sl = dc % 2
                            p.step("vector",
                                   lambda e, dc=dc, pt=pt, sl=sl:
                                   e.tensor_copy(s3[:, ts(sl, 512)],
                                                 x_res[:, dc * TC + pt * W:
                                                       dc * TC + pt * W + W]),
                                   reads=[R_x], writes=[R_s3[sl]],
                                   sem="s_vec", settle=True)
                            p.step("vector",
                                   lambda e, sl=sl:
                                   e.tensor_mul(estg[:, 0:512],
                                                s3[:, ts(sl, 512)],
                                                s3[:, ts(sl, 512)]),
                                   reads=[R_s3[sl]], writes=[R_estg[0]],
                                   sem="s_vec", settle=True)
                            p.step("tensor",
                                   lambda e, dc=dc, sl=sl:
                                   e.matmul(bank(6)[0:1, :], ones128b,
                                            s3[:, ts(sl, 512)],
                                            start=(dc == 0),
                                            stop=(dc == DCH - 1)),
                                   reads=[R_s3[sl], R_c], writes=[R_bank[6]],
                                   sem="s_mm")
                            p.step("tensor",
                                   lambda e, dc=dc:
                                   e.matmul(bank(7)[0:1, :], ones128b,
                                            estg[:, 0:512],
                                            start=(dc == 0),
                                            stop=(dc == DCH - 1)),
                                   reads=[R_estg[0], R_c],
                                   writes=[R_bank[7]], sem="s_mm")
                        p.step("vector",
                               lambda e: e.tensor_scalar(mr3[0:1, 0:512],
                                                         bank(6)[0:1, :],
                                                         1.0 / D, None,
                                                         ALU.mult),
                               reads=[R_bank[6]], writes=[R_mr3], sem="s_vec")
                        p.step("vector",
                               lambda e: e.tensor_scalar(mr3[0:1, 512:1024],
                                                         bank(7)[0:1, :],
                                                         1.0 / D, None,
                                                         ALU.mult),
                               reads=[R_bank[7]], writes=[R_mr3], sem="s_vec")
                        p.step("scalar",
                               lambda e: e.activation(bank(6)[0:1, :],
                                                      mr3[0:1, 0:512],
                                                      AF.Square),
                               reads=[R_mr3], writes=[R_bank[6]], sem="s_act")
                        p.step("vector",
                               lambda e: e.tensor_sub(mr3[0:1, 512:1024],
                                                      mr3[0:1, 512:1024],
                                                      bank(6)[0:1, :]),
                               reads=[R_mr3, R_bank[6]], writes=[R_mr3],
                               sem="s_vec")
                        p.step("scalar",
                               lambda e: e.activation(bank(6)[0:1, :],
                                                      mr3[0:1, 512:1024],
                                                      AF.Identity,
                                                      bias=cval[0:1, 0:1]),
                               reads=[R_mr3, R_c], writes=[R_bank[6]],
                               sem="s_act")
                        p.step("vector",
                               lambda e: e.scalar_tensor_tensor(
                                   mr3[0:1, 512:1024], bank(6)[0:1, :], 1e-6,
                                   mr3[0:1, 512:1024], ALU.mult, ALU.add),
                               reads=[R_mr3, R_bank[6]], writes=[R_mr3],
                               sem="s_vec", drain=True)
                        p.step("scalar",
                               lambda e: e.activation(mr3[0:1, 512:1024],
                                                      mr3[0:1, 512:1024],
                                                      AF.Ln),
                               reads=[R_mr3], writes=[R_mr3], sem="s_act")
                        p.step("scalar",
                               lambda e: e.activation(mr3[0:1, 512:1024],
                                                      mr3[0:1, 512:1024],
                                                      AF.Exp,
                                                      scale=cval[0:1, 2:3]),
                               reads=[R_mr3, R_c], writes=[R_mr3],
                               sem="s_act")
                        p.step("tensor",
                               lambda e: e.matmul(bank(6), ones1f[:, :],
                                                  mr3[0:1, 0:512], start=True,
                                                  stop=True),
                               reads=[R_mr3, R_c], writes=[R_bank[6]],
                               sem="s_mm")
                        p.step("tensor",
                               lambda e: e.matmul(bank(7), ones1f[:, :],
                                                  mr3[0:1, 512:1024],
                                                  start=True, stop=True),
                               reads=[R_mr3, R_c], writes=[R_bank[7]],
                               sem="s_mm")
                        for dc in range(DCH):
                            sl = dc % 2
                            p.step("vector",
                                   lambda e, dc=dc, pt=pt, sl=sl:
                                   e.tensor_sub(sxf[:, ts(sl, 512)],
                                                x_res[:, dc * TC + pt * W:
                                                      dc * TC + pt * W + W],
                                                bank(6)),
                                   reads=[R_x, R_bank[6]],
                                   writes=[R_sxf[sl]], sem="s_vec")
                            p.step("vector",
                                   lambda e, dc=dc, sl=sl:
                                   e.scalar_tensor_tensor(
                                       xn3[:, ts(dc, 512)],
                                       sxf[:, ts(sl, 512)],
                                       nsw_s[:, dc:dc + 1], bank(7),
                                       ALU.mult, ALU.mult),
                                   reads=[R_sxf[sl], R_bank[7], R3c],
                                   writes=[R_xn3], sem="s_vec", settle=True)

                    for pt in range(NT):
                        ln3(pt)
                        for oc in range(32):
                            sl = oc % 2
                            pb = oc % 6
                            p.step("sync",
                                   lambda e, oc=oc, sl=sl:
                                   e.dma_start(out=w3s[sl],
                                               in_=inwc.ap()[oc, :, :]),
                                   reads=[R_wc],
                                   writes=[R_w3[sl]], sem="s_ld", k=16)
                            for kc in range(DCH):
                                p.step("tensor",
                                       lambda e, kc=kc, pb=pb, sl=sl:
                                       e.matmul(bank(pb),
                                                w3s[sl][:, ts(kc, 128)],
                                                xn3[:, ts(kc, 512)],
                                                start=(kc == 0),
                                                stop=(kc == DCH - 1)),
                                       reads=[R_xn3, R_w3[sl]],
                                       writes=[R_bank[pb]], sem="s_mm")
                            if oc < 16:
                                p.step("scalar",
                                       lambda e, oc=oc, pb=pb, sl=sl:
                                       e.activation(estg[:, ts(sl, 512)],
                                                    bank(pb), AF.Identity,
                                                    bias=inb_s[:, oc:oc + 1]),
                                       reads=[R_bank[pb], R3c],
                                       writes=[R_estg[sl]], sem="s_act")
                                p.step("sync",
                                       lambda e, oc=oc, pt=pt, sl=sl:
                                       e.dma_start(
                                           out=hid_d.ap()[:, oc * TC + pt * W:
                                                          oc * TC + pt * W
                                                          + W],
                                           in_=estg[:, ts(sl, 512)]),
                                       reads=[R_estg[sl]], writes=[R_hidd],
                                       sem="s_st", k=16)
                                if pt == NT - 1:
                                    p.step("sync",
                                           lambda e, oc=oc, sl=sl:
                                           e.dma_start(
                                               out=halo_i.ap()[:, oc * 3:
                                                               oc * 3 + 3],
                                               in_=estg[:, sl * 512 + 509:
                                                        sl * 512 + 512]),
                                           reads=[R_estg[sl]],
                                           writes=[R_halo[0]],
                                           sem="s_st", k=16)
                            else:
                                p.step("scalar",
                                       lambda e, oc=oc, pb=pb, sl=sl:
                                       e.activation(estg[:, ts(sl, 512)],
                                                    bank(pb), AF.Silu,
                                                    bias=inb_s[:, oc:oc + 1]),
                                       reads=[R_bank[pb], R3c],
                                       writes=[R_estg[sl]], sem="s_act")
                                p.step("sync",
                                       lambda e, oc=oc, pt=pt, sl=sl:
                                       e.dma_start(
                                           out=g_d.ap()[:,
                                                        (oc - 16) * TC
                                                        + pt * W:
                                                        (oc - 16) * TC
                                                        + pt * W + W],
                                           in_=estg[:, ts(sl, 512)]),
                                       reads=[R_estg[sl]], writes=[R_gd],
                                       sem="s_st", k=16)

                    p.step("gpsimd",
                           lambda e: e.collective_compute(
                               "AllGather", ALU.bypass, replica_groups=REPL,
                               ins=[halo_i.ap().opt()],
                               outs=[halo_o.ap().opt()]),
                           reads=[R_halo[0]], writes=[R_halo[1]],
                           sem="s_cc", k=1)

                    for th in range(2):
                        for ic in range(16):
                            sl = ic % 2
                            if th == 0:
                                def _pref(e, sems, tgt, ic=ic, sl=sl):
                                    with e.If_eq(RG["par"], 0):
                                        e.memset(hbufs[sl][:, 0:3],
                                                 0.0).then_inc(
                                            sems["s_wl"], 16)
                                    with e.Else():
                                        e.dma_start(
                                            out=hbufs[sl][:, 0:3],
                                            in_=halo_o.ap()[0:128, ic * 3:
                                                            ic * 3 + 3]
                                        ).then_inc(sems["s_wl"], 16)
                                    e.end_ifs()
                                p.step2("gpsimd", _pref,
                                        reads=[R_halo[1]],
                                        writes=[R_hbuf[sl]],
                                        sem="s_wl", k=16)
                            else:
                                p.step("sync",
                                       lambda e, ic=ic, sl=sl:
                                       e.dma_start(
                                           out=hbufs[sl][:, 0:3],
                                           in_=hid_d.ap()[:, ic * TC + 2045:
                                                          ic * TC + 2048]),
                                       reads=[R_hidd], writes=[R_hbuf[sl]],
                                       sem="s_ld", k=16)
                            p.step("sync",
                                   lambda e, ic=ic, th=th, sl=sl:
                                   e.dma_start(
                                       out=hbufs[sl][:, 3:2051],
                                       in_=hid_d.ap()[:, ic * TC + th * 2048:
                                                      ic * TC + th * 2048
                                                      + 2048]),
                                   reads=[R_hidd], writes=[R_hbuf[sl]],
                                   sem="s_ld", k=16)
                            p.step("vector",
                                   lambda e, ic=ic, sl=sl:
                                   e.tensor_scalar(cacc[:, 0:512],
                                                   hbufs[sl][:, 0:512],
                                                   cw_s[:, ic:ic + 1], None,
                                                   ALU.mult),
                                   reads=[R_hbuf[sl], R3c], writes=[R_cacc],
                                   sem="s_vec")
                            for q in range(4):
                                if q > 0:
                                    p.step("vector",
                                           lambda e, ic=ic, sl=sl, q=q:
                                           e.tensor_scalar(
                                               cacc[:, 0:512],
                                               hbufs[sl][:, q * 512:
                                                         q * 512 + 512],
                                               cw_s[:, ic:ic + 1], None,
                                               ALU.mult),
                                           reads=[R_hbuf[sl], R3c],
                                           writes=[R_cacc], sem="s_vec")
                                for k in range(1, 4):
                                    p.step("vector",
                                           lambda e, ic=ic, sl=sl, q=q, k=k:
                                           e.scalar_tensor_tensor(
                                               cacc[:, 0:512],
                                               hbufs[sl][:, q * 512 + k:
                                                         q * 512 + k + 512],
                                               cw_s[:, k * 16 + ic:
                                                    k * 16 + ic + 1],
                                               cacc[:, 0:512], ALU.mult,
                                               ALU.add),
                                           reads=[R_hbuf[sl], R3c],
                                           writes=[R_cacc], sem="s_vec",
                                           drain=True)
                                p.step("scalar",
                                       lambda e, ic=ic, sl=sl, q=q:
                                       e.activation(
                                           cnvs[sl][:, q * 512:
                                                    q * 512 + 512],
                                           cacc[:, 0:512], AF.Silu,
                                           bias=cb_s[:, ic:ic + 1]),
                                       reads=[R_cacc, R3c],
                                       writes=[R_cnv[sl]], sem="s_act")
                            p.step("sync",
                                   lambda e, ic=ic, th=th, sl=sl:
                                   e.dma_start(
                                       out=h_d.ap()[:, ic * TC + th * 2048:
                                                    ic * TC + th * 2048
                                                    + 2048],
                                       in_=cnvs[sl][:, 0:2048]),
                                   reads=[R_cnv[sl]], writes=[R_hd],
                                   sem="s_st", k=16)
                            if ic >= 8:
                                p.step("sync",
                                       lambda e, ic=ic, th=th, sl=sl:
                                       e.dma_start(
                                           out=cch_im[ic - 8].ap()[
                                               :, th * 2048:
                                               th * 2048 + 2048],
                                           in_=cnvs[sl][:, 0:2048]),
                                       reads=[R_cnv[sl]], writes=[R_cch[0]],
                                       sem="s_st", k=16)
                            p.step("gpsimd",
                                   lambda e, ic=ic, sl=sl:
                                   e.dma_start(out=xpws[sl][:, 0:96],
                                               in_=xpw_h[ic, :, :]),
                                   writes=[R_xpws[sl]], sem="s_wl", k=16)
                            for q in range(4):
                                p.step("tensor",
                                       lambda e, ic=ic, sl=sl, q=q:
                                       e.matmul(bank(4 + q)[0:96, :],
                                                xpws[sl][:, 0:96],
                                                cnvs[sl][:, q * 512:
                                                         q * 512 + 512],
                                                start=(ic == 0),
                                                stop=(ic == 15)),
                                       reads=[R_cnv[sl], R_xpws[sl]],
                                       writes=[R_bank[4 + q]], sem="s_mm")
                        for q in range(4):
                            p.step("scalar",
                                   lambda e, th=th, q=q:
                                   e.activation(
                                       cxs[0:64, th * 2048 + q * 512:
                                           th * 2048 + q * 512 + 512],
                                       bank(4 + q)[0:64, :], AF.Copy),
                                   reads=[R_bank[4 + q]], writes=[R_tsf],
                                   sem="s_act")
                            p.step("scalar",
                                   lambda e, th=th, q=q:
                                   e.activation(
                                       cxs[64:96, th * 2048 + q * 512:
                                           th * 2048 + q * 512 + 512],
                                       bank(4 + q)[64:96, :], AF.Copy),
                                   reads=[R_bank[4 + q]], writes=[R_tsf],
                                   sem="s_act")
                    p.step("sync",
                           lambda e: e.dma_start(out=ccx_i.ap()[:, :],
                                                 in_=cxs[:, :]),
                           reads=[R_tsf], writes=[R_ccx[0]], sem="s_st",
                           k=16)
                    p.step("gpsimd",
                           lambda e: e.collective_compute(
                               "AllGather", ALU.bypass, replica_groups=REPL,
                               ins=[ccx_i.ap().opt()],
                               outs=[ccx_o.ap().opt()]),
                           reads=[R_ccx[0]], writes=[R_ccx[1]], sem="s_cc",
                           k=1)
                    R_cchg = [Res() for _ in range(8)]
                    for mm8 in range(8):
                        p.step("gpsimd",
                               lambda e, mm8=mm8: e.collective_compute(
                                   "AllGather", ALU.bypass,
                                   replica_groups=REPL,
                                   ins=[cch_im[mm8].ap().opt()],
                                   outs=[cch_om[mm8].ap().opt()]),
                               reads=[R_cch[0]], writes=[R_cchg[mm8]],
                               sem="s_cc", k=1)
                    p.step("sync",
                           lambda e: e.dma_start(out=bc_f[0:32, 0:4096],
                                                 in_=ccx_o.ap()[64:96, :]),
                           reads=[R_ccx[1]], writes=[R_bc], sem="s_ld",
                           k=16)
                    p.step("sync",
                           lambda e: e.dma_start(out=bc_f[0:32, 4096:8192],
                                                 in_=ccx_o.ap()[192:224, :]),
                           reads=[R_ccx[1]], writes=[R_bc], sem="s_ld",
                           k=16)
                    R_tss = [Res(), Res()]
                    for t16 in range(16):
                        tsl = t16 % 2
                        src = (ccx_o.ap()[0:64, ts(t16, 512)]
                               if t16 < 8 else
                               ccx_o.ap()[128:192, ts(t16 - 8, 512)])
                        p.step("sync",
                               lambda e, tsl=tsl, src=src:
                               e.dma_start(out=tsstg[0:64, ts(tsl, 512)],
                                           in_=src),
                               reads=[R_ccx[1]], writes=[R_tss[tsl]],
                               sem="s_ld", k=16)
                        for m in range(DCH):
                            pb = m % 2
                            sl = m % 2
                            p.step("tensor",
                                   lambda e, m=m, pb=pb, tsl=tsl:
                                   e.matmul(bank(pb),
                                            dtw_s[0:64, ts(m, 128)],
                                            tsstg[0:64, ts(tsl, 512)],
                                            start=True, stop=True),
                                   reads=[R_tss[tsl], R_dtw],
                                   writes=[R_bank[pb]], sem="s_mm")
                            p.step("scalar",
                                   lambda e, m=m, pb=pb:
                                   e.activation(sxf[:, 512:1024],
                                                bank(pb), AF.Exp,
                                                bias=dtb_s[:, m:m + 1]),
                                   reads=[R_bank[pb], R3c], writes=[R_sxf[1]],
                                   sem="s_act")
                            p.step("scalar",
                                   lambda e, sl=sl:
                                   e.activation(estg[:, ts(sl, 512)],
                                                sxf[:, 512:1024], AF.Ln,
                                                bias=cval[:, 4:5]),
                                   reads=[R_sxf[1], R_c],
                                   writes=[R_estg[sl]],
                                   sem="s_act")
                            p.step("sync",
                                   lambda e, m=m, t16=t16, sl=sl:
                                   e.dma_start(
                                       out=dlt_d.ap()[:, m * 8192 + t16 * W:
                                                      m * 8192 + t16 * W
                                                      + W],
                                       in_=estg[:, ts(sl, 512)]),
                                   reads=[R_estg[sl]], writes=[R_dltd],
                                   sem="s_st", k=16)
                    for i in range(8):
                        p.step("sync",
                               lambda e, i=i:
                               e.dma_start(out=xres_d.ap()[:, ts(i, 4096)],
                                           in_=x_res[:, ts(i, 4096)]),
                               reads=[R_x], writes=[R_xspill], sem="s_st",
                               k=16)

                p.barrier()

                with contextlib.ExitStack() as pb3:
                    EB = pb3.enter_context
                    hst = EB(nc.sbuf_tensor("hst", [128, 4096], BF16))
                    sel32_s = EB(nc.sbuf_tensor("sel32_s", [32, 4096],
                                                BF16))
                    R_s32 = Res()
                    p.step("gpsimd",
                           lambda e: e.dma_start(out=sel32_s[:, :],
                                                 in_=sel32_h[:, :]),
                           writes=[R_s32], sem="s_wl", k=16)
                    car = [EB(nc.sbuf_tensor(f"car{i}", [128, 1], F32))
                           for i in range(128)]
                    R_car = [Res() for _ in range(128)]
                    R_hst = [Res(), Res()]
                    R_ya = [Res() for _ in range(8)]
                    R_zs = [Res() for _ in range(8)]
                    R_ds = [Res() for _ in range(8)]
                    R_a3 = [Res(), Res()]; R_u3 = [Res(), Res()]
                    R_Bb = [Res(), Res()]; R_Cb = [Res(), Res()]
                    yav = x_res[:, 0:8192].bitcast(BF16)
                    ya = [yav[:, m * 2048:(m + 1) * 2048] for m in range(8)]
                    zv = x_res[:, 16384:24576].bitcast(BF16)
                    zs = [zv[:, m * 2048:(m + 1) * 2048] for m in range(8)]
                    dvv = x_res[:, 24576:32768].bitcast(BF16)
                    ds3 = [dvv[:, m * 2048:(m + 1) * 2048] for m in range(8)]
                    asl = [warena[:, 0:2048], warena[:, 2048:4096]]
                    usl = [warena[:, 4096:6144], warena[:, 6144:8192]]
                    Bbs = [warena[:, 8192:10240], warena[:, 10240:12288]]
                    Cbs = [warena[:, 12288:14336], warena[:, 14336:16384]]
                    for i in range(128):
                        p.step("vector",
                               lambda e, i=i: e.memset(car[i][:, :], 0.0),
                               writes=[R_car[i]], sem="s_vec")
                    for tt in range(4):
                        for m in range(8):
                            msl = m % 2

                            def _hload(e, sems, tgt, tt=tt, m=m, msl=msl):
                                lo = m * TC + (tt % 2) * 2048
                                loc = h_d.ap()[:, lo:lo + 2048]
                                lr = (tt % 2) * 2048
                                if tt < 2:
                                    rem = cch_om[m].ap()[0:128, lr:lr + 2048]
                                    s0, s1 = loc, rem
                                else:
                                    rem = cch_om[m].ap()[128:256,
                                                         lr:lr + 2048]
                                    s0, s1 = rem, loc
                                dst = hst[:, msl * 2048:msl * 2048 + 2048]
                                with e.If_eq(RG["par"], 0):
                                    e.dma_start(out=dst, in_=s0).then_inc(
                                        sems["s_wl"], 16)
                                with e.Else():
                                    e.dma_start(out=dst, in_=s1).then_inc(
                                        sems["s_wl"], 16)
                                e.end_ifs()
                            p.step2("gpsimd", _hload,
                                    reads=[R_hd, R_cchg[m]],
                                    writes=[R_hst[msl]], sem="s_wl", k=16)
                            p.step("sync",
                                   lambda e, tt=tt, m=m:
                                   e.dma_start(
                                       out=ds3[m],
                                       in_=dlt_d.ap()[:,
                                                      m * 8192 + tt * 2048:
                                                      m * 8192 + tt * 2048
                                                      + 2048]),
                                   reads=[R_dltd], writes=[R_ds[m]],
                                   sem="s_ld", k=16)
                            p.step("vector",
                                   lambda e, m=m, msl=msl:
                                   e.tensor_mul(zs[m], ds3[m],
                                                hst[:, msl * 2048:
                                                    msl * 2048 + 2048]),
                                   reads=[R_ds[m], R_hst[msl]],
                                   writes=[R_zs[m]], sem="s_vec")
                        for n in range(16):
                            nb = n % 2
                            for q in range(4):
                                p.step("tensor",
                                       lambda e, tt=tt, n=n, q=q:
                                       e.matmul(bank(q),
                                                sel32_s[0:32, ts(n, 128)],
                                                bc_f[0:32,
                                                     tt * 2048 + q * 512:
                                                     tt * 2048 + q * 512
                                                     + 512],
                                                start=True, stop=True),
                                       reads=[R_bc, R_s32],
                                       writes=[R_bank[q]], sem="s_mm")
                                p.step("tensor",
                                       lambda e, tt=tt, n=n, q=q:
                                       e.matmul(bank(4 + q),
                                                sel32_s[0:32,
                                                        ts(16 + n, 128)],
                                                bc_f[0:32,
                                                     tt * 2048 + q * 512:
                                                     tt * 2048 + q * 512
                                                     + 512],
                                                start=True, stop=True),
                                       reads=[R_bc, R_s32],
                                       writes=[R_bank[4 + q]], sem="s_mm")
                            for q in range(4):
                                p.step("scalar",
                                       lambda e, nb=nb, q=q:
                                       e.activation(
                                           Bbs[nb][:, q * 512:q * 512 + 512],
                                           bank(q), AF.Copy),
                                       reads=[R_bank[q]], writes=[R_Bb[nb]],
                                       sem="s_act")
                                p.step("scalar",
                                       lambda e, nb=nb, q=q:
                                       e.activation(
                                           Cbs[nb][:, q * 512:q * 512 + 512],
                                           bank(4 + q), AF.Copy),
                                       reads=[R_bank[4 + q]],
                                       writes=[R_Cb[nb]], sem="s_act")
                            for m in range(8):
                                sl = (n * 8 + m) % 2
                                ci = n * 8 + m
                                p.step("scalar",
                                       lambda e, m=m, n=n, sl=sl:
                                       e.activation(asl[sl], ds3[m], AF.Exp,
                                                    scale=a_sc[:,
                                                               m * 16 + n:
                                                               m * 16 + n
                                                               + 1]),
                                       reads=[R_ds[m], R_asc],
                                       writes=[R_a3[sl]], sem="s_act")
                                p.step("vector",
                                       lambda e, m=m, sl=sl, nb=nb:
                                       e.tensor_mul(usl[sl], zs[m],
                                                    Bbs[nb]),
                                       reads=[R_zs[m], R_Bb[nb]],
                                       writes=[R_u3[sl]], sem="s_vec")
                                p.step("vector",
                                       lambda e, sl=sl, ci=ci:
                                       e.tensor_tensor_scan(
                                           usl[sl], asl[sl], usl[sl],
                                           car[ci][:, 0:1], ALU.mult,
                                           ALU.add),
                                       reads=[R_a3[sl], R_car[ci]],
                                       writes=[R_u3[sl]], sem="s_vec")
                                p.step("scalar",
                                       lambda e, sl=sl, ci=ci:
                                       e.activation(car[ci][:, 0:1],
                                                    usl[sl][:, 2047:2048],
                                                    AF.Copy),
                                       reads=[R_u3[sl]], writes=[R_car[ci]],
                                       sem="s_act")
                                if m > 0:
                                    psl = (n * 8 + m - 1) % 2
                                    if n == 0:
                                        p.step("vector",
                                               lambda e, m=m, psl=psl:
                                               e.tensor_copy(
                                                   ya[m - 1],
                                                   hst[:, psl * 2048:
                                                       psl * 2048 + 2048]),
                                               reads=[R_hst[psl]],
                                               writes=[R_ya[m - 1]],
                                               sem="s_vec")
                                    else:
                                        p.step("vector",
                                               lambda e, m=m, psl=psl:
                                               e.tensor_tensor(
                                                   ya[m - 1], ya[m - 1],
                                                   hst[:, psl * 2048:
                                                       psl * 2048 + 2048],
                                                   ALU.add),
                                               reads=[R_hst[psl],
                                                      R_ya[m - 1]],
                                               writes=[R_ya[m - 1]],
                                               sem="s_vec")
                                p.step("vector",
                                       lambda e, sl=sl, nb=nb:
                                       e.tensor_mul(
                                           hst[:, sl * 2048:sl * 2048
                                               + 2048],
                                           usl[sl], Cbs[nb]),
                                       reads=[R_u3[sl], R_Cb[nb]],
                                       writes=[R_hst[sl]], sem="s_vec")
                            psl = (n * 8 + 7) % 2
                            if n == 0:
                                p.step("vector",
                                       lambda e, psl=psl:
                                       e.tensor_copy(
                                           ya[7],
                                           hst[:, psl * 2048:
                                               psl * 2048 + 2048]),
                                       reads=[R_hst[psl]], writes=[R_ya[7]],
                                       sem="s_vec")
                            else:
                                p.step("vector",
                                       lambda e, psl=psl:
                                       e.tensor_tensor(
                                           ya[7], ya[7],
                                           hst[:, psl * 2048:
                                               psl * 2048 + 2048], ALU.add),
                                       reads=[R_hst[psl]], writes=[R_ya[7]],
                                       sem="s_vec")
                        for m in range(8):
                            def _ystore(e, sems, tgt, tt=tt, m=m):
                                loc = y_d.ap()[:, m * TC + (tt % 2) * 2048:
                                               m * TC + (tt % 2) * 2048
                                               + 2048]
                                rem = ccy_i.ap()[:,
                                                 m * TC + (tt % 2) * 2048:
                                                 m * TC + (tt % 2) * 2048
                                                 + 2048]
                                with e.If_eq(RG["par"], 0):
                                    e.dma_start(
                                        out=(loc if tt < 2 else rem),
                                        in_=ya[m]).then_inc(sems["s_st"], 16)
                                with e.Else():
                                    e.dma_start(
                                        out=(rem if tt < 2 else loc),
                                        in_=ya[m]).then_inc(sems["s_st"], 16)
                                e.end_ifs()
                            p.step2("gpsimd", _ystore, reads=[R_ya[m]],
                                    writes=[R_ccy[0]], sem="s_st", k=16)
                    p.step("gpsimd",
                           lambda e: e.collective_compute(
                               "AllGather", ALU.bypass, replica_groups=REPL,
                               ins=[ccy_i.ap().opt()],
                               outs=[ccy_o.ap().opt()]),
                           reads=[R_ccy[0]], writes=[R_ccy[1]], sem="s_cc",
                           k=1)

                p.barrier()

                with contextlib.ExitStack() as pc3:
                    EC = pc3.enter_context
                    ssm_s = EC(nc.sbuf_tensor("ssm_s", [128, 8192], BF16))
                    ystg = EC(nc.sbuf_tensor("ystg", [128, 1024], BF16))
                    hstg2 = EC(nc.sbuf_tensor("hstg2", [128, 1024], BF16))
                    gstg2 = EC(nc.sbuf_tensor("gstg2", [128, 1024], BF16))
                    R_ssm = Res(); R_ystg = [Res(), Res()]
                    R_hstg2 = [Res(), Res()]; R_gstg2 = [Res(), Res()]
                    R_otw = [Res(), Res()]
                    otws = [warena[:, 0:1024], warena[:, 1024:2048]]
                    for i in range(8):
                        p.step("sync",
                               lambda e, i=i:
                               e.dma_start(out=x_res[:, ts(i, 4096)],
                                           in_=xres_d.ap()[:, ts(i, 4096)]),
                               reads=[R_xspill], writes=[R_x], sem="s_ld",
                               k=16)
                    for pt in range(NT):
                        for ic in range(16):
                            sl = ic % 2
                            if ic < 8:
                                p.step("sync",
                                       lambda e, ic=ic, pt=pt, sl=sl:
                                       e.dma_start(
                                           out=ystg[:, ts(sl, 512)],
                                           in_=y_d.ap()[:, ic * TC + pt * W:
                                                        ic * TC + pt * W
                                                        + W]),
                                       reads=[R_ccy[0]],
                                       writes=[R_ystg[sl]], sem="s_ld",
                                       k=16)
                            else:
                                def _yld(e, sems, tgt, ic=ic, pt=pt, sl=sl):
                                    r0 = ccy_o.ap()[0:128,
                                                    (ic - 8) * TC + pt * W:
                                                    (ic - 8) * TC + pt * W
                                                    + W]
                                    r1 = ccy_o.ap()[128:256,
                                                    (ic - 8) * TC + pt * W:
                                                    (ic - 8) * TC + pt * W
                                                    + W]
                                    dst = ystg[:, sl * 512:sl * 512 + 512]
                                    with e.If_eq(RG["par"], 0):
                                        e.dma_start(out=dst, in_=r1
                                                    ).then_inc(
                                            sems["s_wl"], 16)
                                    with e.Else():
                                        e.dma_start(out=dst, in_=r0
                                                    ).then_inc(
                                            sems["s_wl"], 16)
                                    e.end_ifs()
                                p.step2("gpsimd", _yld, reads=[R_ccy[1]],
                                        writes=[R_ystg[sl]], sem="s_wl",
                                        k=16)
                            p.step("sync",
                                   lambda e, ic=ic, pt=pt, sl=sl:
                                   e.dma_start(
                                       out=hstg2[:, ts(sl, 512)],
                                       in_=h_d.ap()[:, ic * TC + pt * W:
                                                    ic * TC + pt * W + W]),
                                   reads=[R_hd], writes=[R_hstg2[sl]],
                                   sem="s_ld", k=16)
                            p.step("sync",
                                   lambda e, ic=ic, pt=pt, sl=sl:
                                   e.dma_start(
                                       out=gstg2[:, ts(sl, 512)],
                                       in_=g_d.ap()[:, ic * TC + pt * W:
                                                    ic * TC + pt * W + W]),
                                   reads=[R_gd], writes=[R_gstg2[sl]],
                                   sem="s_ld", k=16)
                            p.step("vector",
                                   lambda e, ic=ic, sl=sl:
                                   e.scalar_tensor_tensor(
                                       ystg[:, ts(sl, 512)],
                                       hstg2[:, ts(sl, 512)],
                                       dvc_s[:, ic:ic + 1],
                                       ystg[:, ts(sl, 512)], ALU.mult,
                                       ALU.add),
                                   reads=[R_hstg2[sl], R_ystg[sl], R3c],
                                   writes=[R_ystg[sl]], sem="s_vec",
                                   drain=True)
                            p.step("vector",
                                   lambda e, ic=ic, sl=sl:
                                   e.tensor_mul(ssm_s[:, ts(ic, 512)],
                                                ystg[:, ts(sl, 512)],
                                                gstg2[:, ts(sl, 512)]),
                                   reads=[R_ystg[sl], R_gstg2[sl]],
                                   writes=[R_ssm], sem="s_vec",
                                   settle=True)
                        for ic in range(16):
                            sl = ic % 2
                            p.step("sync",
                                   lambda e, ic=ic, sl=sl:
                                   e.dma_start(out=otws[sl],
                                               in_=otwc.ap()[ic, :, :]),
                                   reads=[R_wc],
                                   writes=[R_otw[sl]], sem="s_ld", k=16)
                            for mc in range(8):
                                p.step("tensor",
                                       lambda e, ic=ic, mc=mc, sl=sl:
                                       e.matmul(bank(mc),
                                                otws[sl][:, ts(mc, 128)],
                                                ssm_s[:, ts(ic, 512)],
                                                start=(ic == 0),
                                                stop=(ic == 15)),
                                       reads=[R_ssm, R_otw[sl]],
                                       writes=[R_bank[mc]], sem="s_mm")
                        for mc in range(8):
                            p.step("vector",
                                   lambda e, mc=mc, pt=pt:
                                   e.scalar_tensor_tensor(
                                       x_res[:, mc * TC + pt * W:
                                             mc * TC + pt * W + W],
                                       bank(mc), otb_s[:, mc:mc + 1],
                                       x_res[:, mc * TC + pt * W:
                                             mc * TC + pt * W + W],
                                       ALU.add, ALU.add),
                                   reads=[R_bank[mc], R3c, R_x],
                                   writes=[R_x], sem="s_vec")

                p.barrier()
        
        if upto not in ("p1a", "p1", "p2", "p3"):
            p.barrier()
            with contextlib.ExitStack() as p4s:
                E4 = p4s.enter_context
                h4 = E4(nc.sbuf_tensor("h4", [128, 12288], BF16))
                sg4 = E4(nc.sbuf_tensor("sg4", [128, 2048], BF16))
                mr4 = E4(nc.sbuf_tensor("mr4", [1, 1024], F32))
                s4 = E4(nc.sbuf_tensor("s4", [128, 1024], BF16))
                sxf4 = E4(nc.sbuf_tensor("sxf4", [128, 1024], F32))
                R_h4 = Res(); R_sg4 = [Res(), Res()]
                R_mr4 = Res(); R_s4 = [Res(), Res()]
                R_sxf4 = [Res(), Res()]
                R_wg4 = [Res(), Res()]; R_wu4 = [Res(), Res()]
                R_wd4 = [Res(), Res()]; R_xn4 = Res()
                xn4 = warena[:, 0:4096]
                wg4 = [warena[:, 4096:5120], warena[:, 5120:6144]]
                wu4 = [warena[:, 6144:7168], warena[:, 7168:8192]]
                wd4 = [warena[:, 8192:11264], warena[:, 11264:14336]]

                def ln4(pt):
                    for dc in range(DCH):
                        sl = dc % 2
                        p.step("vector",
                               lambda e, dc=dc, pt=pt, sl=sl:
                               e.tensor_copy(s4[:, ts(sl, 512)],
                                             x_res[:, dc * TC + pt * W:
                                                   dc * TC + pt * W + W]),
                               reads=[R_x], writes=[R_s4[sl]], sem="s_vec",
                               settle=True)
                        p.step("vector",
                               lambda e, sl=sl:
                               e.tensor_mul(sg4[:, 0:512],
                                            s4[:, ts(sl, 512)],
                                            s4[:, ts(sl, 512)]),
                               reads=[R_s4[sl]], writes=[R_sg4[0]],
                               sem="s_vec", settle=True)
                        p.step("tensor",
                               lambda e, dc=dc, sl=sl:
                               e.matmul(bank(6)[0:1, :], ones128b,
                                        s4[:, ts(sl, 512)],
                                        start=(dc == 0),
                                        stop=(dc == DCH - 1)),
                               reads=[R_s4[sl], R_c], writes=[R_bank[6]],
                               sem="s_mm")
                        p.step("tensor",
                               lambda e, dc=dc:
                               e.matmul(bank(7)[0:1, :], ones128b,
                                        sg4[:, 0:512],
                                        start=(dc == 0),
                                        stop=(dc == DCH - 1)),
                               reads=[R_sg4[0], R_c], writes=[R_bank[7]],
                               sem="s_mm")
                    p.step("vector",
                           lambda e: e.tensor_scalar(mr4[0:1, 0:512],
                                                     bank(6)[0:1, :],
                                                     1.0 / D, None,
                                                     ALU.mult),
                           reads=[R_bank[6]], writes=[R_mr4], sem="s_vec")
                    p.step("vector",
                           lambda e: e.tensor_scalar(mr4[0:1, 512:1024],
                                                     bank(7)[0:1, :],
                                                     1.0 / D, None,
                                                     ALU.mult),
                           reads=[R_bank[7]], writes=[R_mr4], sem="s_vec")
                    p.step("scalar",
                           lambda e: e.activation(bank(6)[0:1, :],
                                                  mr4[0:1, 0:512],
                                                  AF.Square),
                           reads=[R_mr4], writes=[R_bank[6]], sem="s_act")
                    p.step("vector",
                           lambda e: e.tensor_sub(mr4[0:1, 512:1024],
                                                  mr4[0:1, 512:1024],
                                                  bank(6)[0:1, :]),
                           reads=[R_mr4, R_bank[6]], writes=[R_mr4],
                           sem="s_vec")
                    p.step("scalar",
                           lambda e: e.activation(mr4[0:1, 512:1024],
                                                  mr4[0:1, 512:1024], AF.Ln,
                                                  bias=cval[0:1, 0:1]),
                           reads=[R_mr4, R_c], writes=[R_mr4], sem="s_act")
                    p.step("scalar",
                           lambda e: e.activation(mr4[0:1, 512:1024],
                                                  mr4[0:1, 512:1024], AF.Exp,
                                                  scale=cval[0:1, 2:3]),
                           reads=[R_mr4, R_c], writes=[R_mr4], sem="s_act")
                    p.step("tensor",
                           lambda e: e.matmul(bank(6), ones1f[:, :],
                                              mr4[0:1, 0:512], start=True,
                                              stop=True),
                           reads=[R_mr4, R_c], writes=[R_bank[6]],
                           sem="s_mm")
                    p.step("tensor",
                           lambda e: e.matmul(bank(7), ones1f[:, :],
                                              mr4[0:1, 512:1024], start=True,
                                              stop=True),
                           reads=[R_mr4, R_c], writes=[R_bank[7]],
                           sem="s_mm")
                    for dc in range(DCH):
                        sl = dc % 2
                        p.step("vector",
                               lambda e, dc=dc, pt=pt, sl=sl:
                               e.tensor_sub(sxf4[:, ts(sl, 512)],
                                            x_res[:, dc * TC + pt * W:
                                                  dc * TC + pt * W + W],
                                            bank(6)),
                               reads=[R_x, R_bank[6]], writes=[R_sxf4[sl]],
                               sem="s_vec")
                        p.step("vector",
                               lambda e, dc=dc, sl=sl:
                               e.tensor_mul(xn4[:, ts(dc, 512)],
                                            sxf4[:, ts(sl, 512)], bank(7)),
                               reads=[R_sxf4[sl], R_bank[7]],
                               writes=[R_xn4], sem="s_vec", settle=True)

                for pt in range(NT):
                    ln4(pt)
                    for hc in range(24):
                        sl = hc % 2
                        p.step("sync",
                               lambda e, hc=hc, sl=sl:
                               e.dma_start(out=wg4[sl],
                                           in_=wgtc.ap()[hc, :, :]),
                               reads=[R_wc],
                               writes=[R_wg4[sl]], sem="s_ld", k=16)
                        for kc in range(DCH):
                            p.step("tensor",
                                   lambda e, kc=kc, sl=sl:
                                   e.matmul(bank(sl),
                                            wg4[sl][:, ts(kc, 128)],
                                            xn4[:, ts(kc, 512)],
                                            start=(kc == 0),
                                            stop=(kc == DCH - 1)),
                                   reads=[R_xn4, R_wg4[sl]],
                                   writes=[R_bank[sl]], sem="s_mm")
                        p.step("scalar",
                               lambda e, sl=sl:
                               e.activation(sg4[:, sl * 512:sl * 512 + 512],
                                            bank(sl), AF.Silu,
                                            scale=cval[:, 5:6]),
                               reads=[R_bank[sl], R_c], writes=[R_sg4[sl]],
                               sem="s_act")
                        p.step("sync",
                               lambda e, hc=hc, sl=sl:
                               e.dma_start(out=wu4[sl],
                                           in_=wutc.ap()[hc, :, :]),
                               reads=[R_wc],
                               writes=[R_wu4[sl]], sem="s_ld", k=16)
                        for kc in range(DCH):
                            p.step("tensor",
                                   lambda e, kc=kc, sl=sl:
                                   e.matmul(bank(2 + sl),
                                            wu4[sl][:, ts(kc, 128)],
                                            xn4[:, ts(kc, 512)],
                                            start=(kc == 0),
                                            stop=(kc == DCH - 1)),
                                   reads=[R_xn4, R_wu4[sl]],
                                   writes=[R_bank[2 + sl]], sem="s_mm")
                        p.step("scalar",
                               lambda e, sl=sl:
                               e.activation(sg4[:, 1024 + sl * 512:
                                                1024 + sl * 512 + 512],
                                            bank(2 + sl), AF.Identity,
                                            scale=cval[:, 5:6]),
                               reads=[R_bank[2 + sl], R_c],
                               writes=[R_sg4[sl]], sem="s_act")
                        p.step("vector",
                               lambda e, hc=hc, sl=sl:
                               e.tensor_mul(h4[:, ts(hc, 512)],
                                            sg4[:, sl * 512:sl * 512 + 512],
                                            sg4[:, 1024 + sl * 512:
                                                1024 + sl * 512 + 512]),
                               reads=[R_sg4[sl]], writes=[R_h4],
                               sem="s_vec", settle=True)
                    for mc in range(DCH):
                        sl = mc % 2
                        p.step("sync",
                               lambda e, mc=mc, sl=sl:
                               e.dma_start(out=wd4[sl],
                                           in_=wdtc.ap()[mc, :, :]),
                               reads=[R_wc],
                               writes=[R_wd4[sl]], sem="s_ld", k=16)
                        for kc in range(24):
                            p.step("tensor",
                                   lambda e, kc=kc, sl=sl:
                                   e.matmul(bank(4 + sl),
                                            wd4[sl][:, ts(kc, 128)],
                                            h4[:, ts(kc, 512)],
                                            start=(kc == 0),
                                            stop=(kc == 23)),
                                   reads=[R_h4, R_wd4[sl]],
                                   writes=[R_bank[4 + sl]], sem="s_mm")
                        p.step("vector",
                               lambda e, mc=mc, pt=pt, sl=sl:
                               e.scalar_tensor_tensor(
                                   x_res[:, mc * TC + pt * W:
                                         mc * TC + pt * W + W],
                                   bank(4 + sl), 1.0 / 127.0,
                                   x_res[:, mc * TC + pt * W:
                                         mc * TC + pt * W + W],
                                   ALU.mult, ALU.add),
                               reads=[R_bank[4 + sl], R_x], writes=[R_x],
                               sem="s_vec")

        p.step("sync", lambda e: e.dma_start(out=dbg[0:1, 0:8],
                                             in_=cval[0:1, 0:8]),
               reads=[R_c], sem="s_st", k=16)

        with nc.sbuf_tensor("xout", [128, 2048], F32) as xout:
            R_xout = [Res(), Res()]
            for r in range(TC // 128):
                buf = r % 2
                for dc in range(DCH):
                    slot = dc % 2
                    p.step("tensor",
                           lambda e, r=r, dc=dc, slot=slot:
                           e.transpose(bank(6 + slot, 128),
                                       x_res[:, dc * TC + r * 128:
                                             dc * TC + r * 128 + 128],
                                       id_f[:, :]),
                           reads=[R_x, R_c], writes=[R_bank[6 + slot]],
                           sem="s_mm")
                    p.step("vector",
                           lambda e, dc=dc, buf=buf, slot=slot:
                           e.tensor_copy(xout[:, buf * 1024 + dc * 128:
                                              buf * 1024 + dc * 128 + 128],
                                         bank(6 + slot, 128)),
                           reads=[R_bank[6 + slot]], writes=[R_xout[buf]],
                           sem="s_vec")
                p.step("sync",
                       lambda e, r=r, buf=buf:
                       e.dma_start(out=out[ts(r, 128), :],
                                   in_=xout[:, ts(buf, 1024)]),
                       reads=[R_xout[buf]], sem="s_st", k=16)
            p.raw("sync", lambda e, s: e.wait_ge(s["s_st"], p.cnt["s_st"]))

        for ename in Prog.ENGINES:
            eng_obj = {"tensor": block.tensor, "vector": block.vector,
                       "scalar": block.scalar, "gpsimd": block.gpsimd,
                       "sync": block.sync}[ename]

            def make(ename=ename):
                def _(eng):
                    for fn in p.streams[ename]:
                        fn(eng, sems)
                return _
            eng_obj(make())

    return nc


def make_consts():
    cst_f = np.zeros((128, 256), dtype=np.float32)
    cst_f[:, 0:128] = np.eye(128, dtype=np.float32)
    cst_f[0, 128:256] = 1.0
    cst_b = np.zeros((128, 128 + SELC_W), dtype=np.float32)
    cst_b[:, 0:128] = np.eye(128, dtype=np.float32)
    sel = np.zeros((128, SELC_W), dtype=np.float32)
    pidx = np.arange(128)
    for mc in range(8):
        for r in range(32):
            sel[r, SELF + mc * 128: SELF + (mc + 1) * 128] = \
                (r == 2 * mc + pidx // 64)
            sel[r, SELS + mc * 128: SELS + (mc + 1) * 128] = \
                (r == 16 + 2 * mc + pidx // 64)
        for m in range(16):
            sel[:, HSUM16 + mc * 16 + m] = (m == 2 * mc + pidx // 64)
    for r in range(2):
        sel[r, HSEL2: HSEL2 + 128] = (r == pidx // 64)
    sel[0, ONES1B: ONES1B + 128] = 1.0
    sel[:, ONES128] = 1.0
    cst_b[:, 128:] = sel
    return cst_f, cst_b


def kernel(**inputs):
    f32 = lambda k: np.ascontiguousarray(np.asarray(inputs[k], dtype=np.float32))
    x = f32("L0_attn_x")
    wsm = np.ascontiguousarray(np.concatenate(
        [f32("L0_attn_Wb"), np.zeros((D, 16), np.float32),
         f32("L0_attn_Wfd"), f32("L0_attn_Wsd")], axis=1))
    bsm = np.concatenate([np.zeros(32, np.float32),
                          f32("L0_attn_bfd").ravel(),
                          f32("L0_attn_bsd").ravel()]).reshape(64, 1)
    btf1 = np.tile(f32("L0_attn_btf1").ravel(), 2).reshape(64, 1)
    btf2 = f32("L0_attn_btf2").reshape(1, 1)
    cst_f, cst_b = make_consts()

    wg = np.zeros((D, 8), np.float32)
    wg[:, 0:6] = f32("L0_ffn_Wg")
    bg = np.zeros((8, 1), np.float32)
    bg[0:6, 0] = f32("L0_ffn_bg").ravel()
    w1m = np.ascontiguousarray(
        f32("L0_ffn_W1").reshape(6, 8, 128, 16, 128)
        .transpose(0, 3, 2, 1, 4).reshape(6, 16, 128, 1024))
    w2m = np.ascontiguousarray(
        f32("L0_ffn_W2").reshape(6, 16, 128, 8, 128)
        .transpose(0, 3, 2, 1, 4).reshape(6, 8, 128, 2048))
    b1p = np.ascontiguousarray(
        f32("L0_ffn_b1").reshape(6, 16, 128).transpose(2, 0, 1)
        .reshape(128, 96))
    b2p = np.ascontiguousarray(
        f32("L0_ffn_b2").reshape(6, 8, 128).transpose(2, 0, 1)
        .reshape(128, 48))
    sel6 = np.zeros((128, 768), np.float32)
    for ex in range(6):
        sel6[ex, ex * 128:(ex + 1) * 128] = 1.0

    in_w = f32("L1_attn_in_w"); in_b = f32("L1_attn_in_b")
    conv_w = f32("L1_attn_conv_w"); conv_b = f32("L1_attn_conv_b")
    xproj_w = f32("L1_attn_xproj_w")
    dt_w = f32("L1_attn_dt_w"); dt_b = f32("L1_attn_dt_b")
    A_log = f32("L1_attn_A_log"); Dv = f32("L1_attn_D")
    out_w = f32("L1_attn_out_w"); out_b = f32("L1_attn_out_b")
    nsc = f32("L1_attn_norm_scale")
    p3h = []
    for h in (0, 1):
        Pm = np.r_[h * 1024:(h + 1) * 1024, (1 - h) * 1024:(2 - h) * 1024]
        wperm = np.concatenate([in_w[:, :2048][:, Pm], in_w[:, 2048:][:, Pm]],
                               axis=1)
        inw_p = np.ascontiguousarray(
            wperm.reshape(1024 // 128, 128, 32, 128)
            .transpose(2, 1, 0, 3).reshape(32, 128, 1024))
        inb_c = np.concatenate([in_b[:2048][Pm], in_b[2048:][Pm]])
        inb_p = np.ascontiguousarray(inb_c.reshape(32, 128).T)
        cwP = conv_w[:, Pm]
        cwp_p = np.ascontiguousarray(
            cwP.reshape(4, 16, 128).transpose(2, 0, 1).reshape(128, 64))
        cbp_p = np.ascontiguousarray(conv_b[Pm].reshape(16, 128).T)
        xpw_p = np.ascontiguousarray(xproj_w[Pm, :].reshape(16, 128, 96))
        dtw_p = np.ascontiguousarray(dt_w[:, Pm[:1024]])
        dtb_p = np.ascontiguousarray(dt_b[Pm[:1024]].reshape(8, 128).T)
        alg_p = np.ascontiguousarray(
            A_log[Pm[:1024], :].reshape(8, 128, 16)
            .transpose(1, 0, 2).reshape(128, 128))
        dvc_p = np.ascontiguousarray(Dv[Pm].reshape(16, 128).T)
        otw_p = np.ascontiguousarray(out_w[Pm, :].reshape(16, 128, 1024))
        p3h.append(dict(inw=inw_p, inb=inb_p, cwp=cwp_p, cbp=cbp_p,
                        xpw=xpw_p, dtw=dtw_p, dtb=dtb_p, alg=alg_p,
                        dvc=dvc_p, otw=otw_p))
    otb_p = np.ascontiguousarray(out_b.reshape(8, 128).T)
    nsw_p = np.ascontiguousarray(nsc.reshape(8, 128).T)
    sel32 = np.zeros((32, 4096), np.float32)
    for n in range(32):
        sel32[n, n * 128:(n + 1) * 128] = 1.0

    wgT = np.ascontiguousarray(np.asarray(inputs["L1_ffn_w_gate"]).T)
    wuT = np.ascontiguousarray(np.asarray(inputs["L1_ffn_w_up"]).T)
    wdT = np.ascontiguousarray(np.asarray(inputs["L1_ffn_w_down"]).T)
    wgt_p = np.ascontiguousarray(
        wgT.reshape(8, 128, 24, 128).transpose(2, 1, 0, 3)
        .reshape(24, 128, 1024).astype(np.int32))
    wut_p = np.ascontiguousarray(
        wuT.reshape(8, 128, 24, 128).transpose(2, 1, 0, 3)
        .reshape(24, 128, 1024).astype(np.int32))
    wdt_p = np.ascontiguousarray(
        wdT.reshape(24, 128, 8, 128).transpose(2, 1, 0, 3)
        .reshape(8, 128, 3072).astype(np.int32))

    nc = build_program()
    common = {
        "cst_f": cst_f, "cst_b": cst_b,
        "wq": f32("L0_attn_Wq"), "wk": f32("L0_attn_Wk"),
        "wv": f32("L0_attn_Wv"), "wo": f32("L0_attn_Wo"),
        "wsm": wsm, "wtf1": f32("L0_attn_Wtf1"),
        "wtf2": f32("L0_attn_Wtf2").reshape(32, 1),
        "bsm": bsm, "btf1": btf1, "btf2": btf2,
        "wg": wg, "bg": bg, "w1m": w1m, "w2m": w2m,
        "b1p": b1p, "b2p": b2p, "sel6": sel6,
        "nsw": nsw_p, "otb": otb_p, "sel32": sel32,
        "wgt": wgt_p, "wut": wut_p, "wdt": wdt_p,
    }
    in_maps = []
    for c in range(8):
        b, h = c // 2, c % 2
        m = dict(common)
        m["x"] = np.ascontiguousarray(x[b, h * TC:(h + 1) * TC, :])
        m.update(p3h[h])
        in_maps.append(m)
    res = run_bass_kernel_spmd(nc, in_maps, core_ids=list(range(8)))
    y = np.empty((B, T, D), dtype=np.float32)
    kernel.dbg = [res.results[c].get("dbg") for c in range(8)]
    for c in range(8):
        b, h = c // 2, c % 2
        y[b, h * TC:(h + 1) * TC, :] = res.results[c]["out"]
    return y
